# version 27
# baseline (speedup 1.0000x reference)
import numpy as np
import ml_dtypes
import concourse.bass as bass
import concourse.mybir as mybir
from concourse.bass_utils import run_bass_kernel_spmd

F32 = mybir.dt.float32
BF16 = mybir.dt.bfloat16
I32 = mybir.dt.int32
AF = mybir.ActivationFunctionType
ALU = mybir.AluOpType
AX = mybir.AxisListType
BF = ml_dtypes.bfloat16

SEQ = 4096
D = 1024
NT = SEQ // 128
NQ = SEQ // 512
H = 8
DH = 128
NF = D // 128
SCALE_ATTN = float(DH) ** -0.5
C1 = float(np.sqrt(128.0))
MAGIC = 12582912.0

FUSED = True
SERIAL = False
LIMIT_BQ = 8
LIMIT_H = 8
SKIP_EPI = False
EPI_PART = 4


def _nop(e):
    return e.engine_nop() if hasattr(e, "engine_nop") else e.nop()


class Prog:
    ENGINES = ("sync", "vector", "scalar", "tensor", "gpsimd")

    def __init__(self, nc, serial=False):
        self.nc = nc
        self.steps = []
        self.sems = {}
        self.counts = {}
        self.serial = serial
        self.xq_free = {}
        self.xnT_free = [0, 0]
        self.ws_free = [0, 0]
        self.pb_gate = [None, None]
        self.pf_gate = [None, None]
        self.pO_gate = [None, None]
        self.pf4_gate = None
        self.pf5_gate = None
        self.sqscr_gate = None
        self.prep_gate = [None, None]
        self.rcpbc_gate = [None, None]
        self.rcprow_gate = None
        self.rowrt_gate = [None] * 8
        self.rows_gate = []
        self.f0chain_gate = [None, None]
        self.negMb_gate = [None] * 8
        self.qT_gate = None
        self.feat8_gate = None
        self.screp_gate = None
        self.applyscr_gate = [None, None]
        self.abc_gate = [None, None]
        self.xstore = [None] * NT
        self.rt_i = 0
        self.dc = 0
        self.vc = 0

    def sem(self, name):
        if name not in self.sems:
            self.sems[name] = self.nc.alloc_semaphore(name)
            self.counts[name] = 0
        return name

    def c(self, name):
        return self.counts.get(name, 0)

    def add(self, eng, fn, waits=(), inc=None, dma=False):
        amt = 16 if dma else 1
        incs = inc if isinstance(inc, (list, tuple)) else ([inc] if inc else [])
        out = []
        for s in incs:
            self.sem(s)
            self.counts[s] += amt
            out.append((s, amt))
        ws = [(s, v) for (s, v) in waits if s is not None and v > 0]
        for s, _ in ws:
            self.sem(s)
        self.steps.append((eng, fn, ws, out[:1], dma))
        for extra in out[1:]:
            self.steps.append((eng, _nop, [], [extra], False))
        return self.counts[incs[0]] if incs else 0

    def barrier(self):
        ws = [(s, self.c(s)) for s in ("do0", "do1") if self.c(s) > 0]
        if not ws:
            return
        for eng in self.ENGINES:
            self.add(eng, _nop, waits=ws)

    def emit(self):
        steps = self.steps
        if self.serial:
            self.sem("__SER")
            new = []
            tot = 0
            run = {}
            for (eng, fn, waits, incs, dma) in steps:
                amt = 16 if dma else 1
                if incs:
                    s0, a0 = incs[0]
                    run[s0] = run.get(s0, 0) + a0
                    new.append((eng, fn, waits + [("__SER", tot)], incs, dma))
                    new.append((eng, _nop, [(s0, run[s0])], [("__SER", amt)], False))
                else:
                    new.append((eng, fn, waits + [("__SER", tot)],
                                [("__SER", amt)], dma))
                tot += amt
            steps = new
        with self.nc.Block() as block:
            for eng_name in self.ENGINES:
                def mk(name):
                    def f(eng):
                        hw = {}
                        own = {}
                        for (e, fn, waits, incs, dma) in steps:
                            if e != name:
                                continue
                            pend = []
                            if name != "tensor":
                                for (s, v) in own.items():
                                    if v > hw.get(s, 0):
                                        pend.append((s, v))
                                        hw[s] = v
                            for (s, v) in waits:
                                if v > hw.get(s, 0):
                                    pend.append((s, v))
                                    hw[s] = v
                            for (s, v) in pend[:-1]:
                                eng.wait_ge(self.sems[s], v)
                            inst = fn(eng)
                            if pend:
                                s, v = pend[-1]
                                inst._wait_ge(self.sems[s], v)
                            for (s, a) in incs:
                                inst.then_inc(self.sems[s], a)
                                if not dma:
                                    own[s] = own.get(s, 0) + a
                    return f
                getattr(block, eng_name)(mk(eng_name))



class Mem:
    def __init__(self, nc):
        self.nc = nc
        self.off = (nc.sbuf_base + 31) & ~31
        self._offsets = {}
        A = self._at

        self.ident_bf = A("ident_bf", [128, 128], BF16)
        self.ones_col = A("ones_col", [128, 1], BF16)
        self.onesK1 = A("onesK1", [1, 128], BF16)
        self.ones128 = A("ones128", [128, 128], BF16)
        self.gcol = A("gcol", [128, 2, 8], F32)
        self.lgcol = A("lgcol", [128, 3, 8], F32)
        self.lbcol = A("lbcol", [128, 3, 8], F32)
        self.bcol = A("bcol", [128, 6, 8], F32)
        self.epscol = A("epscol", [128, 1], F32)
        self.eps11 = A("eps11", [1, 1], F32)
        self.cols = A("cols", [128, 16], F32)
        self.cols2 = A("cols2", [128, 16], F32)
        self.cols3 = A("cols3", [128, 16], F32)
        self.cols4 = A("cols4", [128, 8], F32)
        self.sqjunk = A("sqjunk", [128, 1024], BF16)
        self.negM_col = A("negM_col", [128, 8], F32)
        self.negMb = A("negMb", [1, 8], BF16)
        self.m11 = A("m11", [1, 8], F32)
        self.kmax = A("kmax", [128, 64], F32)
        self.rcp_row = A("rcp_row", [1, 512], F32)
        self.rcp_bc = A("rcp_bc", [128, 2, 512], F32)
        self.probs = A("probs", [128, 2, 512], BF16)
        self.sqscr512 = A("sqscr512", [128, 512], BF16)
        self.scale_r = A("scale_r", [128, 8], F32)

        self.xq = A("xq", [128, 2, 1024], F32)
        self.coeffs = A("coeffs", [128, 32, 4], F32)
        self.prep_bf = A("prep_bf", [128, 2, 1024], BF16)
        self.sq1024 = A("sq1024", [128, 1024], BF16, alias="prep_bf")
        self.scrF = A("scrF", [128, 1024], F32)
        self.xnT = A("xnT", [128, 2, 8, 512], BF16)
        self.wbuf1 = A("wbuf1", [128, 8, 1024], BF16)
        self.wbuf2 = A("wbuf2", [128, 8, 1024], BF16)
        self.wstage = A("wstage", [128, 2, 1024], F32)
        self.wstage_i = A("wstage_i", [128, 2, 1024], I32, alias="wstage")
        self.kT = A("kT", [128, 8, 4096], BF16)
        self.hT = A("hT", [128, 8, 4096], BF16, alias="kT")
        self.xq4f0 = A("xq4f0", [128, 4, 1024], F32, alias="kT")
        self.v_sb = A("v_sb", [128, 32, 1024], BF16)
        self.ySB = A("ySB", [128, 3, 8, 512], BF16, alias="v_sb")
        self.xq4f1 = A("xq4f1", [128, 4, 1024], F32, alias="v_sb")
        self.q_rhs = A("q_rhs", [128, 8, 1024], BF16, alias="v_sb")
        self.a_bc = A("a_bc", [128, 512], F32, alias="v_sb", sub=24576)
        self.b_bc = A("b_bc", [128, 512], F32, alias="v_sb", sub=26624)
        self.screp = A("screp", [128, 1024], F32, alias="v_sb", sub=28672)
        self.scrF2 = A("scrF2", [128, 1024], F32, alias="v_sb", sub=32768)
        self.apply_scr = A("apply_scr", [128, 1024], F32, alias="v_sb", sub=36864)
        self.la_bf = A("la_bf", [128, 2, 8, 32], BF16, alias="v_sb", sub=40960)
        self.lb_bf = A("lb_bf", [32, 2, 1024], BF16, alias="v_sb", sub=41984)
        self.l1_bf = A("l1_bf", [32, 2, 512], BF16, alias="v_sb", sub=46080)
        self.scrI = A("scrI", [128, 1024], I32, alias="v_sb", sub=48128)
        self.qbf = A("qbf", [128, 1024], BF16, alias="v_sb", sub=52224)
        self.scrI2 = A("scrI2", [128, 1024], I32, alias="v_sb", sub=54272)
        self.qT = A("qT", [128, 8, 512], BF16, alias="xnT", sub=8192)
        self.wbuf3 = A("wbuf3", [128, 8, 1024], BF16, alias="xnT")
        self.lora_laF = A("lora_laF", [128, 8, 32], F32, alias="v_sb", sub=54272)
        self.lora_lbF = A("lora_lbF", [32, 1024], F32, alias="v_sb", sub=55296)
        self.sqF1 = A("sqF1", [128, 6, 512], BF16, alias="v_sb", sub=48128)
        self.sqX = A("sqX", [128, 2, 512], BF16, alias="xq")
        self.sqA = A("sqA", [128, 8, 512], BF16, alias="xq")
        self.h0chain = A("h0chain", [128, 512], F32, alias="prep_bf", sub=2048)
        self.a_bc2 = A("a_bc2", [128, 512], F32, alias="v_sb", sub=59392)
        self.b_bc2 = A("b_bc2", [128, 512], F32, alias="v_sb", sub=61440)
        self.finalT = A("finalT", [128, 8, 512], BF16, alias="wstage")
        self.feat8 = A("feat8", [128, 8, 512], BF16)
        self.xqa2 = A("xqa2", [128, 2, 1024], F32, alias="feat8")

        assert self.off <= nc.sbuf_top, f"SBUF over budget: {self.off}"
        self.pf = [nc.alloc_psum_tensor(f"pf{i}", [128, 512], F32) for i in range(6)]
        self.pb = [nc.alloc_psum_tensor(f"pb{i}", [128, 8, 128], BF16) for i in range(2)]

    def _at(self, name, shape, dtype, alias=None, sub=0):
        esize = {F32: 4, BF16: 2, I32: 4}[dtype]
        nbytes = int(np.prod(shape[1:])) * esize
        if alias is not None:
            off = self._offsets[alias] + sub
        else:
            off = self.off
            self.off += (nbytes + 31) & ~31
        t = self.nc.alloc_sbuf_tensor_at(name, shape, dtype, offset=off)
        self._offsets[name] = off
        return t


def col(M, i):
    return M.cols[:, i:i + 1]


def col2(M, i):
    return M.cols2[:, i:i + 1]



def emit_consts(p, M, io):
    nc = p.nc
    idb = nc.inline_tensor(np.eye(128, dtype=BF), "identb_c")
    p.add("sync", lambda e: e.dma_start(out=M.ident_bf[:], in_=idb.ap()),
          inc="dc", dma=True)
    p.add("vector", lambda e: e.memset(M.ones_col[:], 1.0), inc="vc")
    p.add("vector", lambda e: e.memset(M.onesK1[:], 1.0), inc="vc")
    p.add("vector", lambda e: e.memset(M.ones128[:], 1.0), inc="vc")
    p.add("vector", lambda e: e.memset(M.epscol[:], 1e-5), inc="vc")
    p.add("vector", lambda e: e.memset(M.eps11[:], 1e-5), inc="vc")
    for key, dst in (("gcols", M.gcol), ("lgcols", M.lgcol), ("lbcols", M.lbcol),
                     ("bcols", M.bcol)):
        if key in io:
            p.add("sync", lambda e, ap=io[key], dst=dst: e.dma_start(out=dst[:], in_=ap),
                  inc="dc", dma=True)
    p.dc = p.c("dc")
    p.vc = p.c("vc")


def emit_prologue_group(p, M, tiles, x_src, mode, bufs, consume):
    cf = M.coeffs
    n = len(tiles)
    avs = []
    xs = []
    for i, (t, (buf, slot, sem)) in enumerate(zip(tiles, bufs)):
        xsl = buf[:, slot, :]
        xs.append(xsl)
        w = [(sem, p.xq_free.get(sem, 0))]
        if p.xstore[t]:
            w.append(p.xstore[t])
        dxv = p.add("sync", lambda e, t=t, xsl=xsl: e.dma_start(
            out=xsl, in_=x_src[t * 128:(t + 1) * 128, :]), waits=w,
            inc=f"d{sem}", dma=True)
        p.add("vector", lambda e, xsl=xsl, i=i: e.reduce_sum(
            M.cols3[:, i:i + 1], xsl, axis=AX.X),
            waits=[(f"d{sem}", dxv)], inc="ve")
        aw = [(f"d{sem}", dxv)]
        av = p.add("scalar", lambda e, xsl=xsl, i=i: e.activation(
            M.sqjunk[:], xsl, AF.Square, accum_out=M.cols3[:, 4 + i:5 + i]),
            waits=aw, inc="ae")
        avs.append(av)
    for i, t in enumerate(tiles):
        p.add("vector", lambda e, t=t, i=i: e.tensor_scalar(
            cf[:, t, 0:1], M.cols3[:, i:i + 1], -1.0 / D, None, ALU.mult),
            inc="ve")
        p.add("vector", lambda e, t=t, i=i: e.tensor_mul(
            M.cols3[:, 8 + 2 * i:9 + 2 * i], cf[:, t, 0:1], cf[:, t, 0:1]),
            inc="ve")
        p.add("vector", lambda e, i=i: e.tensor_scalar(
            M.cols3[:, 8 + 2 * i:9 + 2 * i], M.cols3[:, 4 + i:5 + i], 1.0 / D,
            M.cols3[:, 8 + 2 * i:9 + 2 * i], ALU.mult, ALU.subtract),
            waits=[("ae", avs[i])], inc="ve")
        vv = p.add("vector", lambda e, i=i: e.tensor_scalar(
            M.cols3[:, 9 + 2 * i:10 + 2 * i], M.cols3[:, 8 + 2 * i:9 + 2 * i],
            1e-5, None, ALU.add), inc="ve")
    sq = p.add("scalar", lambda e, n=n: e.activation(
        M.cols4[:, 0:2 * n], M.cols3[:, 8:8 + 2 * n], AF.Sqrt),
        waits=[("ve", vv)], inc="ae")
    qpend = []

    def _quant_fin(t, i, rne, B):
        p.add("vector", lambda e, B=B: e.tensor_copy(M.scrI[:], B[:]),
              waits=[("ge", rne)], inc="ve")
        ps_ = i % 2
        gw2 = [p.prep_gate[ps_]] if p.prep_gate[ps_] else []
        vap = p.add("vector", lambda e, ps_=ps_: e.tensor_copy(
            M.prep_bf[:, ps_, :], M.scrI[:]), waits=gw2, inc="ve")
        consume(t, i, ("ve", vap), i % 2)

    for i, t in enumerate(tiles):
        sv = M.cols4[:, 2 * i:2 * i + 1]
        sve = M.cols4[:, 2 * i + 1:2 * i + 2]
        if mode == "attn":
            p.add("vector", lambda e, sve=sve: e.reciprocal(col(M, 8), sve),
                  waits=[("ae", sq)], inc="ve")
            p.add("vector", lambda e, sv=sv: e.tensor_mul(col(M, 9), col(M, 8), sv),
                  inc="ve")
            p.add("vector", lambda e: e.tensor_scalar(col(M, 9), col(M, 9), 32.0,
                                                      1e-12, ALU.mult, ALU.max),
                  inc="ve")
            p.add("vector", lambda e: e.reciprocal(col(M, 10), col(M, 9)), inc="ve")
            p.add("vector", lambda e: e.tensor_mul(col(M, 10), col(M, 8),
                                                   col(M, 10)), inc="ve")
            p.add("vector", lambda e, t=t: e.tensor_scalar(
                cf[:, t, 1:2], col(M, 10), 32.0, None, ALU.mult), inc="ve")
            p.add("vector", lambda e, t=t: e.tensor_scalar(
                cf[:, t, 2:3], cf[:, t, 1:2], SCALE_ATTN, None, ALU.mult),
                inc="ve")
            ps_ = i % 2
            gw = [p.prep_gate[ps_]] if p.prep_gate[ps_] else []
            vap = p.add("vector", lambda e, t=t, xsl=xs[i], ps_=ps_: e.tensor_scalar(
                M.prep_bf[:, ps_, :], xsl, cf[:, t, 0:1], cf[:, t, 1:2], ALU.add,
                ALU.mult), waits=gw, inc=["ve", bufs[i][2]])
            p.xq_free[bufs[i][2]] = p.c(bufs[i][2])
            consume(t, i, ("ve", vap), i % 2)
        elif mode == "ln":
            p.add("vector", lambda e, t=t, sve=sve: e.reciprocal(cf[:, t, 1:2], sve),
                  waits=[("ae", sq)], inc="ve")
            ps_ = i % 2
            gw = [p.prep_gate[ps_]] if p.prep_gate[ps_] else []
            vap = p.add("vector", lambda e, t=t, xsl=xs[i], ps_=ps_: e.tensor_scalar(
                M.prep_bf[:, ps_, :], xsl, cf[:, t, 0:1], cf[:, t, 1:2], ALU.add,
                ALU.mult), waits=gw, inc=["ve", bufs[i][2]])
            p.xq_free[bufs[i][2]] = p.c(bufs[i][2])
            consume(t, i, ("ve", vap), i % 2)
        else:
            B = (M.apply_scr, M.scrF2)[i % 2]
            p.add("vector", lambda e, t=t, sve=sve: e.reciprocal(cf[:, t, 1:2], sve),
                  waits=[("ae", sq)], inc="ve")
            aw2 = [p.f0chain_gate[i % 2]] if p.f0chain_gate[i % 2] else []
            p.add("vector", lambda e, t=t, xsl=xs[i], B=B: e.tensor_scalar(
                B[:], xsl, cf[:, t, 0:1], cf[:, t, 1:2], ALU.add,
                ALU.mult), waits=aw2, inc=["ve", bufs[i][2]])
            p.xq_free[bufs[i][2]] = p.c(bufs[i][2])
            p.add("vector", lambda e, B=B, i=i: e.tensor_reduce(
                col(M, 8 + i % 2), B[:], AX.X, ALU.max,
                apply_absolute_value=True), inc="ve")
            p.add("vector", lambda e, i=i: e.tensor_scalar(
                col(M, 10 + i % 2), col(M, 8 + i % 2), 1e-12,
                1.0 / 127.0, ALU.max, ALU.mult), inc="ve")
            p.add("vector", lambda e, t=t, i=i: e.reciprocal(
                cf[:, t, 2:3], col(M, 10 + i % 2)), inc="ve")
            p.add("vector", lambda e, t=t, i=i: e.tensor_mul(
                col(M, 12 + i % 2), cf[:, t, 2:3], cf[:, t, 2:3]), inc="ve")
            cv = p.add("vector", lambda e, t=t, i=i: e.tensor_scalar(
                cf[:, t, 3:4], col(M, 12 + i % 2), 1e-5, None, ALU.mult),
                inc="ve")
            p.add("gpsimd", lambda e, t=t, B=B: e.tensor_scalar(
                B[:], B[:], cf[:, t, 2:3], None, ALU.mult),
                waits=[("ve", cv)], inc="ge")
            rne = p.add("gpsimd", lambda e, B=B: e.tensor_scalar(
                B[:], B[:], MAGIC, -MAGIC, ALU.add, ALU.add), inc="ge")
            p.f0chain_gate[i % 2] = ("ge", rne)
            qpend.append((t, i, rne, B))
            if len(qpend) > 1:
                _quant_fin(*qpend.pop(0))
    while qpend:
        _quant_fin(*qpend.pop(0))
    if mode == "quant":
        return None
    return None


def emit_prologue_tile(p, M, t, x_src, mode):
    slot = t % 2
    xsl = M.xq[:, slot, :]
    w = [(f"xf{slot}", p.xq_free.get(f"xf{slot}", 0))]
    if p.xstore[t]:
        w.append(p.xstore[t])
    dxv = p.add("sync", lambda e: e.dma_start(
        out=xsl, in_=x_src[t * 128:(t + 1) * 128, :]), waits=w, inc=f"dx{slot}",
        dma=True)
    gate = p.prep_gate[0]
    gw = [gate] if gate else []
    p.add("vector", lambda e: e.reduce_sum(col(M, 0), xsl, axis=AX.X),
          waits=[(f"dx{slot}", dxv)], inc="ve")
    av = p.add("scalar", lambda e: e.activation(M.sq1024[:], xsl, AF.Square,
                                                accum_out=col(M, 1)),
               waits=[(f"dx{slot}", dxv)] + gw, inc="ae")
    cf = M.coeffs
    p.add("vector", lambda e: e.tensor_scalar(cf[:, t, 0:1], col(M, 0), -1.0 / D,
                                              None, ALU.mult), inc="ve")
    p.add("vector", lambda e: e.tensor_mul(col(M, 2), cf[:, t, 0:1], cf[:, t, 0:1]),
          inc="ve")
    vv = p.add("vector", lambda e: e.tensor_scalar(col(M, 3), col(M, 1), 1.0 / D,
                                                   col(M, 2), ALU.mult, ALU.subtract),
               waits=[("ae", av)], inc="ve")
    if mode == "attn":
        p.add("vector", lambda e: e.tensor_scalar(col(M, 4), col(M, 3), 1e-5, None,
                                                  ALU.add), inc="ve")
        a2 = p.add("scalar", lambda e: e.activation(M.cols2[:, 3:5], M.cols[:, 3:5],
                                                    AF.Sqrt),
                   waits=[("ve", p.c("ve"))], inc="ae")
        p.add("vector", lambda e: e.reciprocal(col(M, 6), col2(M, 4)),
              waits=[("ae", a2)], inc="ve")
        p.add("vector", lambda e: e.tensor_mul(col(M, 7), col(M, 6), col2(M, 3)),
              inc="ve")
        p.add("vector", lambda e: e.tensor_scalar(col(M, 8), col(M, 7), 32.0, 1e-12,
                                                  ALU.mult, ALU.max), inc="ve")
        p.add("vector", lambda e: e.reciprocal(col(M, 9), col(M, 8)), inc="ve")
        p.add("vector", lambda e: e.tensor_mul(col(M, 10), col(M, 6), col(M, 9)),
              inc="ve")
        p.add("vector", lambda e: e.tensor_scalar(cf[:, t, 1:2], col(M, 10), 32.0,
                                                  None, ALU.mult), inc="ve")
        p.add("vector", lambda e: e.tensor_scalar(cf[:, t, 2:3], cf[:, t, 1:2],
                                                  SCALE_ATTN, None, ALU.mult),
              inc="ve")
        vapply = p.add("vector", lambda e: e.tensor_scalar(
            M.prep_bf[:, 0, :], xsl, cf[:, t, 0:1], cf[:, t, 1:2], ALU.add,
            ALU.mult), waits=gw, inc=["ve", f"xf{slot}"])
        p.xq_free[f"xf{slot}"] = p.c(f"xf{slot}")
        return ("ve", vapply)
    if mode == "ln":
        a1 = p.add("scalar", lambda e: e.activation(col2(M, 4), col(M, 3), AF.Sqrt,
                                                    bias=M.epscol[:], scale=1.0),
                   waits=[("ve", vv), ("vc", p.vc)], inc="ae")
        p.add("vector", lambda e: e.reciprocal(cf[:, t, 1:2], col2(M, 4)),
              waits=[("ae", a1)], inc="ve")
        vapply = p.add("vector", lambda e: e.tensor_scalar(
            M.prep_bf[:, 0, :], xsl, cf[:, t, 0:1], cf[:, t, 1:2], ALU.add,
            ALU.mult), waits=gw, inc=["ve", f"xf{slot}"])
        p.xq_free[f"xf{slot}"] = p.c(f"xf{slot}")
        return ("ve", vapply)
    a1 = p.add("scalar", lambda e: e.activation(col2(M, 4), col(M, 3), AF.Sqrt,
                                                bias=M.epscol[:], scale=1.0),
               waits=[("ve", vv)], inc="ae")
    p.add("vector", lambda e: e.reciprocal(cf[:, t, 1:2], col2(M, 4)),
          waits=[("ae", a1)], inc="ve")
    p.add("vector", lambda e: e.tensor_scalar(
        M.scrF[:], xsl, cf[:, t, 0:1], cf[:, t, 1:2], ALU.add, ALU.mult),
        waits=gw, inc=["ve", f"xf{slot}"])
    p.xq_free[f"xf{slot}"] = p.c(f"xf{slot}")
    p.add("vector", lambda e: e.tensor_reduce(col(M, 5), M.scrF[:], AX.X, ALU.max,
                                              apply_absolute_value=True), inc="ve")
    p.add("vector", lambda e: e.tensor_scalar(col(M, 6), col(M, 5), 1e-12,
                                              1.0 / 127.0, ALU.max, ALU.mult),
          inc="ve")
    p.add("vector", lambda e: e.reciprocal(cf[:, t, 2:3], col(M, 6)), inc="ve")
    p.add("vector", lambda e: e.tensor_mul(col(M, 7), cf[:, t, 2:3], cf[:, t, 2:3]),
          inc="ve")
    p.add("vector", lambda e: e.tensor_scalar(cf[:, t, 3:4], col(M, 7), 1e-5,
                                              None, ALU.mult), inc="ve")
    p.add("vector", lambda e: e.tensor_scalar(M.scrF[:], M.scrF[:], cf[:, t, 2:3],
                                              None, ALU.mult), inc="ve")
    p.add("vector", lambda e: e.tensor_scalar(
        M.scrF[:], M.scrF[:], MAGIC, -MAGIC, ALU.add, ALU.add), inc="ve")
    p.add("vector", lambda e: e.tensor_copy(M.scrI[:], M.scrF[:]), inc="ve")
    vapply = p.add("vector", lambda e: e.tensor_copy(M.prep_bf[:, 0, :], M.scrI[:]),
                   inc="ve")
    return ("ve", vapply)


def emit_transpose_prep(p, M, pslot, ready, prep_slot=0):
    w = [ready, ("dc", p.dc)]
    if p.pb_gate[pslot]:
        w.append(p.pb_gate[pslot])
    v = 0
    for j in range(NF):
        v = p.add("tensor", lambda e, j=j: e.transpose(
            M.pb[pslot][:, j, :],
            M.prep_bf[:, prep_slot, j * 128:(j + 1) * 128], M.ident_bf[:]),
            waits=(w if j == 0 else ()), inc=("tp" if j == NF - 1 else None))
    p.prep_gate[prep_slot] = ("tp", v)
    return ("tp", v)


def emit_prep_to_xnT(p, M, t_in_q, xslot, ready_tp):
    pslot = t_in_q % 2
    w = [ready_tp]
    if p.xnT_free[xslot]:
        w.append((f"nf{xslot}", p.xnT_free[xslot]))
    ev = p.add("vector", lambda e: e.tensor_copy(
        M.xnT[:, xslot, :, t_in_q * 128:(t_in_q + 1) * 128], M.pb[pslot][:]),
        waits=w, inc="ve")
    p.pb_gate[pslot] = ("ve", ev)
    return ev


def emit_epilogue_qtile(p, M, io, qt, srcT, src_ready, eps_mode, last_block):
    x_src = io["x_src"]
    x_dst = io["x_dst"]
    for tt in range(4):
        t = qt * 4 + tt
        pslot = tt % 2
        w = [src_ready, ("dc", p.dc)]
        if p.pb_gate[pslot]:
            w.append(p.pb_gate[pslot])
        for oc in range(NF):
            tv = p.add("tensor", lambda e, oc=oc, tt=tt, pslot=pslot: e.transpose(
                M.pb[pslot][:, oc, :], srcT[:, oc, tt * 128:(tt + 1) * 128],
                M.ident_bf[:]),
                waits=(w if oc == 0 else ()), inc=("tp" if oc == NF - 1 else None))
        xslot = t % 2
        xsl = M.xq[:, xslot, :]
        wx = [(f"xf{xslot}", p.xq_free.get(f"xf{xslot}", 0))]
        if p.xstore[t]:
            wx.append(p.xstore[t])
        dxv = p.add("sync", lambda e, t=t, xsl=xsl: e.dma_start(
            out=xsl, in_=x_src[t * 128:(t + 1) * 128, :]), waits=wx,
            inc=f"dx{xslot}", dma=True)
        pbv = M.pb[pslot][:].rearrange("p a b -> p (a b)")
        if EPI_PART < 2:
            addv = p.add("vector", lambda e: _nop(e), waits=[("tp", tv)], inc="ve")
            dov = p.add("sync", lambda e, t=t, xsl=xsl: e.dma_start(
                out=x_dst[t * 128:(t + 1) * 128, :], in_=xsl),
                waits=[("ve", addv), (f"dx{xslot}", dxv)], inc=f"do{xslot}", dma=True)
            p.xstore[t] = (f"do{xslot}", dov)
            p.xq_free[xslot] = p.add("sync", _nop, waits=[(f"do{xslot}", dov)],
                                     inc=f"xf{xslot}")
            p.pb_gate[pslot] = ("ve", addv)
            continue
        rdv = p.add("vector", lambda e, pbv=pbv: e.reduce_sum(col(M, 0), pbv,
                                                              axis=AX.X),
                    waits=[("tp", tv)], inc="ve")
        aw = [("tp", tv), ("ve", rdv)]
        if p.prep_gate[0]:
            aw.append(p.prep_gate[0])
        aq = p.add("scalar", lambda e, pbv=pbv: e.activation(
            M.sq1024[:], pbv, AF.Square, accum_out=col(M, 1)), waits=aw, inc="ae")
        p.prep_gate[0] = ("ae", aq)
        p.add("vector", lambda e: e.tensor_scalar(col(M, 2), col(M, 0), -1.0 / D,
                                                  None, ALU.mult), inc="ve")
        p.add("vector", lambda e: e.tensor_mul(col(M, 3), col(M, 2), col(M, 2)),
              inc="ve")
        vv = p.add("vector", lambda e: e.tensor_scalar(
            col(M, 4), col(M, 1), 1.0 / D, col(M, 3), ALU.mult, ALU.subtract),
            waits=[("ae", aq)], inc="ve")
        if eps_mode == "const":
            p.add("vector", lambda e: e.tensor_scalar(col(M, 4), col(M, 4), 1e-5,
                                                      None, ALU.add), inc="ve")
        else:
            p.add("vector", lambda e, t=t: e.tensor_add(col(M, 4), col(M, 4),
                                                        M.coeffs[:, t, 3:4]),
                  inc="ve")
        a1 = p.add("scalar", lambda e: e.activation(col2(M, 5), col(M, 4), AF.Sqrt),
                   waits=[("ve", p.c("ve"))], inc="ae")
        p.add("vector", lambda e: e.reciprocal(col(M, 6), col2(M, 5)),
              waits=[("ae", a1)], inc="ve")
        if EPI_PART < 3:
            addv = p.add("vector", lambda e: _nop(e), inc="ve")
            p.pb_gate[pslot] = ("ve", addv)
        else:
            p.add("vector", lambda e, pbv=pbv: e.tensor_scalar(
                M.scrF[:], pbv, col(M, 2), col(M, 6), ALU.add, ALU.mult), inc="ve")
            p.pb_gate[pslot] = ("ve", p.c("ve"))
            if EPI_PART < 4:
                addv = p.add("vector", lambda e: _nop(e), inc="ve")
            else:
                addv = p.add("vector", lambda e, xsl=xsl: e.tensor_add(
                    xsl, xsl, M.scrF[:]), waits=[(f"dx{xslot}", dxv)], inc="ve")
        dov = p.add("sync", lambda e, t=t, xsl=xsl: e.dma_start(
            out=x_dst[t * 128:(t + 1) * 128, :], in_=xsl),
            waits=[("ve", addv), (f"dx{xslot}", dxv)], inc=f"do{xslot}", dma=True)
        p.xstore[t] = (f"do{xslot}", dov)
        p.xq_free[f"xf{xslot}"] = p.add("sync", _nop, waits=[(f"do{xslot}", dov)],
                                        inc=f"xf{xslot}")



def emit_attn(p, M, io, layer):
    x_src = io["x_src"]
    wqkvT, woutT, rowrt = io["wqkvT"], io["woutT"], io["rowrt"]
    gc = M.gcol[:, layer, :]

    def prep_w(wb, src_ap, coloff, apply_gamma, gate):
        done = []
        for fi in range(NF):
            slot = fi % 2
            w = [(f"wf{slot}", p.ws_free[slot])]
            if gate:
                w.append(gate)
            dv = p.add("sync", lambda e, fi=fi, slot=slot: e.dma_start(
                out=M.wstage[:, slot, :],
                in_=src_ap[fi * 128:(fi + 1) * 128, coloff:coloff + D]),
                waits=w, inc=f"dw{slot}", dma=True)
            if apply_gamma:
                vv = p.add("vector", lambda e, fi=fi, slot=slot: e.tensor_scalar(
                    wb[:, fi, :], M.wstage[:, slot, :], gc[:, fi:fi + 1], None,
                    ALU.mult), waits=[(f"dw{slot}", dv), ("dc", p.dc)],
                    inc=["ve", f"wf{slot}"])
            else:
                vv = p.add("vector", lambda e, fi=fi, slot=slot: e.tensor_copy(
                    wb[:, fi, :], M.wstage[:, slot, :]),
                    waits=[(f"dw{slot}", dv)], inc=["ve", f"wf{slot}"])
            p.ws_free[slot] = p.c(f"wf{slot}")
            done.append(vv)
        return ("ve", max(done))

    wk_ready = prep_w(M.wbuf1, wqkvT, D, True, None)
    wv_ready = prep_w(M.wbuf2, wqkvT, 2 * D, True, None)
    p.asq_gate = [None] * NF
    ksq_vs = [None] * NF
    last_sqx_mm = 0

    for C in range(NQ):
        xslot = C % 2
        evs = []

        def consume(t, i, rdy, prep_slot, C=C, xslot=xslot, evs=evs):
            tt = t - C * 4
            tp = emit_transpose_prep(p, M, tt % 2, rdy, prep_slot)
            evs.append(emit_prep_to_xnT(p, M, tt, xslot, tp))

        abufs = [(M.xqa2, 0, "xfa0"), (M.xqa2, 1, "xfa1")]
        emit_prologue_group(p, M, [C * 4, C * 4 + 1], x_src, "attn", abufs, consume)
        emit_prologue_group(p, M, [C * 4 + 2, C * 4 + 3], x_src, "attn", abufs,
                            consume)
        xnT_ready = ("ve", max(evs))
        for fo in range(NF):
            pfk = M.pf[fo % 2]
            w0 = [xnT_ready, wk_ready]
            if p.pf_gate[fo % 2]:
                w0.append(p.pf_gate[fo % 2])
            for fi in range(NF):
                mv = p.add("tensor", lambda e, fo=fo, fi=fi, pfk=pfk, xslot=xslot:
                           e.matmul(pfk[:], M.wbuf1[:, fi, fo * 128:(fo + 1) * 128],
                                    M.xnT[:, xslot, fi, :],
                                    start=(fi == 0), stop=(fi == NF - 1)),
                           waits=(w0 if fi == 0 else ()),
                           inc=("mm" if fi == NF - 1 else None))
            kev = p.add("scalar", lambda e, fo=fo, C=C, pfk=pfk: e.activation(
                M.kT[:, fo, C * 512:(C + 1) * 512], pfk[:], AF.Copy),
                waits=[("mm", mv)], inc="ae")
            p.pf_gate[fo % 2] = ("ae", kev)
            sqb = M.sqA[:, fo, :]
            sw = [("ae", kev)]
            if p.asq_gate[fo]:
                sw.append(p.asq_gate[fo])
            ksq_vs[fo] = p.add("scalar", lambda e, fo=fo, C=C, sqb=sqb:
                               e.activation(sqb, M.kT[:, fo, C * 512:(C + 1) * 512],
                                            AF.Square), waits=sw, inc="ae")
        kT_ready_C = kev
        vevs = []
        for tt in range(4):
            for half in range(2):
                pfv = M.pf[2 + half]
                w0 = [xnT_ready, wv_ready]
                if p.pO_gate[half]:
                    w0.append(p.pO_gate[half])
                for fi in range(NF):
                    mv = p.add("tensor", lambda e, tt=tt, half=half, fi=fi, pfv=pfv,
                               xslot=xslot: e.matmul(
                                   pfv[:],
                                   M.xnT[:, xslot, fi, tt * 128:(tt + 1) * 128],
                                   M.wbuf2[:, fi, half * 512:(half + 1) * 512],
                                   start=(fi == 0), stop=(fi == NF - 1)),
                               waits=(w0 if fi == 0 else ()),
                               inc=("mm" if fi == NF - 1 else None))
                vev = p.add("vector", lambda e, tt=tt, half=half, C=C, pfv=pfv:
                            e.tensor_copy(
                                M.v_sb[:, C * 4 + tt, half * 512:(half + 1) * 512],
                                pfv[:]), waits=[("mm", mv)], inc="ve")
                p.pO_gate[half] = ("ve", vev)
                vevs.append(vev)
        p.xnT_free[xslot] = p.add("vector", _nop, inc=f"nf{xslot}")
        for h in range(H):
            sqb = M.sqA[:, h, :]
            pfs = M.pf[4 + h % 2]
            w = [("ae", ksq_vs[h]), ("vc", p.vc)]
            pg = p.pf4_gate if h % 2 == 0 else p.pf5_gate
            if pg:
                w.append(pg)
            mv = p.add("tensor", lambda e, sqb=sqb, pfs=pfs: e.matmul(
                pfs[:], M.ones128[:], sqb, start=True, stop=True,
                skip_group_check=True), waits=w, inc="mm")
            p.asq_gate[h] = ("mm", mv)
            last_sqx_mm = mv
            rv = p.add("vector", lambda e, h=h, C=C, pfs=pfs: e.reduce_max(
                M.kmax[:, h * 8 + C:h * 8 + C + 1], pfs[:], axis=AX.X),
                waits=[("mm", mv)], inc="ve")
            if h % 2 == 0:
                p.pf4_gate = ("ve", rv)
            else:
                p.pf5_gate = ("ve", rv)
    for xs in (0, 1):
        p.xq_free[f"xf{xs}"] = p.add("sync", _nop,
                                     waits=[("mm", last_sqx_mm)], inc=f"xf{xs}")
    phaseA_done = ("ve", p.c("ve"))

    wq_ready = prep_w(M.wbuf1, wqkvT, 0, True, phaseA_done)
    wo_ready = prep_w(M.wbuf2, woutT, 0, False, phaseA_done)

    nq = min(NQ, LIMIT_BQ)

    def q_prep(qt):
        evs = []
        for tt in range(4):
            t = qt * 4 + tt
            xslot2 = t % 2
            xsl = M.xq[:, xslot2, :]
            wx = [(f"xf{xslot2}", p.xq_free.get(f"xf{xslot2}", 0))]
            if p.xstore[t]:
                wx.append(p.xstore[t])
            dxv = p.add("sync", lambda e, t=t, xsl=xsl: e.dma_start(
                out=xsl, in_=x_src[t * 128:(t + 1) * 128, :]), waits=wx,
                inc=f"dx{xslot2}", dma=True)
            gw = [(f"dx{xslot2}", dxv)]
            if p.prep_gate[0]:
                gw.append(p.prep_gate[0])
            if p.xnT_free[0] and tt == 0:
                gw.append(("nf0", p.xnT_free[0]))
            vapply = p.add("vector", lambda e, t=t, xsl=xsl: e.tensor_scalar(
                M.prep_bf[:, 0, :], xsl, M.coeffs[:, t, 0:1], M.coeffs[:, t, 2:3],
                ALU.add, ALU.mult), waits=gw, inc=["ve", f"xf{xslot2}"])
            p.xq_free[f"xf{xslot2}"] = p.c(f"xf{xslot2}")
            tp = emit_transpose_prep(p, M, tt % 2, ("ve", vapply))
            evs.append(emit_prep_to_xnT(p, M, tt, 0, tp))
        return ("ve", max(evs))

    def q_mm(qt, xnT_ready):
        for fo in range(H):
            pfk = M.pf[fo % 2]
            w0 = [xnT_ready, wq_ready]
            if p.pf_gate[fo % 2]:
                w0.append(p.pf_gate[fo % 2])
            for fi in range(NF):
                mv = p.add("tensor", lambda e, fo=fo, fi=fi, pfk=pfk: e.matmul(
                    pfk[:], M.wbuf1[:, fi, fo * 128:(fo + 1) * 128],
                    M.xnT[:, 0, fi, :], start=(fi == 0), stop=(fi == NF - 1)),
                    waits=(w0 if fi == 0 else ()),
                    inc=("mm" if fi == NF - 1 else None))
            qev = p.add("scalar", lambda e, fo=fo, pfk=pfk: e.activation(
                M.qT[:, fo, :], pfk[:], AF.Copy), waits=[("mm", mv)], inc="ae")
            p.pf_gate[fo % 2] = ("ae", qev)
        p.xnT_free[0] = p.add("scalar", _nop, inc="nf0")
        return ("ae", qev)

    def bound_prep(qt, qT_ready):
        evs = []
        for h in range(H):
            sw = [qT_ready]
            if p.feat8_gate:
                sw.append(p.feat8_gate)
            sq = p.add("scalar", lambda e, h=h: e.activation(
                M.feat8[:, h, :], M.qT[:, h, :], AF.Square), waits=sw, inc="ae")
            bank = M.pf[4 + (h % 2)]
            w = [("ae", sq), ("vc", p.vc)]
            g = p.pf4_gate if h % 2 == 0 else p.pf5_gate
            if g:
                w.append(g)
            mv = p.add("tensor", lambda e, h=h, bank=bank: e.matmul(
                bank[:], M.ones128[:], M.feat8[:, h, :], start=True, stop=True,
                skip_group_check=True), waits=w, inc="mm")
            qc = M.cols2[:, 8 + h:9 + h]
            rv = p.add("vector", lambda e, qc=qc, bank=bank: e.reduce_max(
                qc, bank[:], axis=AX.X), waits=[("mm", mv)], inc="ve")
            if h % 2 == 0:
                p.pf4_gate = ("ve", rv)
            else:
                p.pf5_gate = ("ve", rv)
            p.add("vector", lambda e, h=h, qt=qt: e.reduce_max(
                col(M, 13), M.kmax[:, h * 8:h * 8 + qt + 1], axis=AX.X),
                waits=[phaseA_done], inc="ve")
            p.add("vector", lambda e, qc=qc: e.tensor_scalar(
                col(M, 11), qc, C1 * 0.5, None, ALU.mult), inc="ve")
            p.add("vector", lambda e: e.tensor_scalar(
                col(M, 12), col(M, 13), 0.5 / C1, None, ALU.mult), inc="ve")
            nm = p.add("vector", lambda e, h=h: e.scalar_tensor_tensor(
                M.negM_col[:, h:h + 1], col(M, 11), -1.0, col(M, 12),
                ALU.mult, ALU.subtract), inc="ve")
            evs.append(nm)
        p.feat8_gate = ("mm", p.c("mm"))
        return evs

    xnT_ready = q_prep(0)
    qT_ready = q_mm(0, xnT_ready)
    negM_evs = bound_prep(0, qT_ready)

    for qt in range(nq):
        cur_qT_ready = qT_ready
        cur_negM = negM_evs
        for h in range(min(H, LIMIT_H)):
            hp = h % 2
            negM_ev = cur_negM[h]
            nkc = 4 * qt + 4
            pO = M.pf[2 + hp]

            def emit_S(kc, h=h, cur_qT_ready=cur_qT_ready):
                ps = M.pf[kc % 2]
                w0 = [cur_qT_ready, phaseA_done]
                if p.pf_gate[kc % 2]:
                    w0.append(p.pf_gate[kc % 2])
                p.add("tensor", lambda e, h=h, kc=kc, ps=ps: e.matmul(
                    ps[:], M.kT[:, h, kc * 128:(kc + 1) * 128], M.qT[:, h, :],
                    start=True, stop=True, skip_group_check=True),
                    waits=w0, inc="mm")
                return p.c("mm")

            sv_next = emit_S(0)
            for kc in range(nkc):
                ps = M.pf[kc % 2]
                sv = sv_next
                ev = p.add("scalar", lambda e, kc=kc, h=h, ps=ps: e.activation(
                    M.probs[:, kc % 2, :], ps[:], AF.Exp,
                    bias=M.negM_col[:, h:h + 1], scale=1.0),
                    waits=[("mm", sv), ("ve", negM_ev)], inc="ae")
                p.pf_gate[kc % 2] = ("ae", ev)
                pready = ("ae", ev)
                if kc >= 4 * qt:
                    base = qt * 512 - kc * 128
                    gv = p.add("gpsimd", lambda e, kc=kc, base=base: e.affine_select(
                        M.probs[:, kc % 2, :], M.probs[:, kc % 2, :], [[1, 512]],
                        ALU.is_ge, 0.0, base=base, channel_multiplier=-1),
                        waits=[pready], inc="ge")
                    pready = ("ge", gv)
                if kc + 1 < nkc:
                    sv_next = emit_S(kc + 1)
                wO = [pready]
                if kc == 0:
                    if p.pO_gate[hp]:
                        wO.append(p.pO_gate[hp])
                p.add("tensor", lambda e, h=h, kc=kc, pO=pO: e.matmul(
                    pO[:], M.v_sb[:, kc, h * 128:(h + 1) * 128], M.probs[:, kc % 2, :],
                    start=(kc == 0), stop=(kc == nkc - 1), skip_group_check=True),
                    waits=wO, inc=("mm" if kc == nkc - 1 else None))
                pR = M.pf[4 + hp]
                rw0 = []
                if kc == 0:
                    g = p.pf4_gate if hp == 0 else p.pf5_gate
                    if g:
                        rw0.append(g)
                    rw0.append(("vc", p.vc))
                rsv = p.add("tensor", lambda e, kc=kc, nkc=nkc, pR=pR: e.matmul(
                    pR[0:1, :], M.ones_col[:], M.probs[:, kc % 2, :],
                    start=(kc == 0), stop=(kc == nkc - 1), skip_group_check=True),
                    waits=rw0, inc="mm")
                p.probs_gate_g[kc % 2] = ("mm", rsv)
            av_done = ("mm", p.c("mm") - 1)
            rs_done = ("mm", p.c("mm"))
            rw = [rs_done]
            if p.rcprow_gate:
                rw.append(p.rcprow_gate)
            pR = M.pf[4 + hp]
            rc = p.add("vector", lambda e, pR=pR: e.reciprocal(M.rcp_row[:], pR[0:1, :]),
                       waits=rw, inc="ve")
            if hp == 0:
                p.pf4_gate = ("ve", rc)
            else:
                p.pf5_gate = ("ve", rc)
            rslot = p.rt_i % 8
            p.rt_i += 1
            d1w = [("ve", rc)]
            if p.rowrt_gate[rslot]:
                d1w.append(p.rowrt_gate[rslot])
            d1 = p.add("sync", lambda e, rslot=rslot: e.dma_start(
                out=rowrt[rslot:rslot + 1, :], in_=M.rcp_row[:]), waits=d1w,
                inc=f"dro{rslot}", dma=True)
            p.rcprow_gate = (f"dro{rslot}", d1)
            d2w = [(f"dro{rslot}", d1)]
            if p.rcpbc_gate[hp]:
                d2w.append(p.rcpbc_gate[hp])
            d2 = p.add("sync", lambda e, rslot=rslot, hp=hp: e.dma_start(
                out=M.rcp_bc[:, hp, :],
                in_=rowrt[rslot:rslot + 1, :].broadcast_to([128, 512])),
                waits=d2w, inc="dri", dma=True)
            p.rowrt_gate[rslot] = ("dri", d2)
            nv_w = [("dri", d2), av_done]
            if p.feat8_gate:
                nv_w.append(p.feat8_gate)
            nv = p.add("vector", lambda e, h=h, pO=pO, hp=hp: e.tensor_mul(
                M.feat8[:, h, :], pO[:], M.rcp_bc[:, hp, :]), waits=nv_w, inc="ve")
            p.rcpbc_gate[hp] = ("ve", nv)
            p.pO_gate[hp] = ("ve", nv)
        p.feat8_gate = None
        attnout_ready = ("ve", p.c("ve"))

        if qt + 1 < nq:
            xnT_ready = q_prep(qt + 1)

        for oc in range(NF):
            pfk = M.pf[oc % 2]
            w0 = [attnout_ready, wo_ready]
            if p.pf_gate[oc % 2]:
                w0.append(p.pf_gate[oc % 2])
            for fi in range(NF):
                mv = p.add("tensor", lambda e, oc=oc, fi=fi, pfk=pfk: e.matmul(
                    pfk[:], M.wbuf2[:, fi, oc * 128:(oc + 1) * 128],
                    M.feat8[:, fi, :], start=(fi == 0), stop=(fi == NF - 1)),
                    waits=(w0 if fi == 0 else ()),
                    inc=("mm" if fi == NF - 1 else None))
            fv = p.add("scalar", lambda e, oc=oc, pfk=pfk: e.activation(
                M.finalT[:, oc, :], pfk[:], AF.Copy), waits=[("mm", mv)], inc="ae")
            p.pf_gate[oc % 2] = ("ae", fv)
        p.feat8_gate = ("mm", p.c("mm"))
        final_ready = ("ae", fv)

        if qt + 1 < nq:
            qT_ready = q_mm(qt + 1, xnT_ready)
            negM_evs = bound_prep(qt + 1, qT_ready)

        if not SKIP_EPI:
            emit_epilogue_qtile(p, M, io, qt, M.finalT, final_ready, "const",
                                io.get("last", False))



def emit_f0(p, M, io):
    rhsT = io["rhsT"]
    for i in range(NF):
        slot = i % 2
        dv = p.add("sync", lambda e, i=i, slot=slot: e.dma_start(
            out=M.wstage[:, slot, :], in_=rhsT[i * 128:(i + 1) * 128, :]),
            waits=[(f"wf{slot}", p.ws_free[slot])], inc=f"dw{slot}", dma=True)
        p.add("vector", lambda e, slot=slot: e.tensor_reduce(
            col2(M, 8), M.wstage[:, slot, :], AX.X, ALU.max,
            apply_absolute_value=True), waits=[(f"dw{slot}", dv)], inc="ve")
        p.add("vector", lambda e, i=i: e.tensor_scalar(
            M.scale_r[:, i:i + 1], col2(M, 8), 1e-12, 1.0 / 127.0, ALU.max,
            ALU.mult), inc="ve")
        p.add("vector", lambda e, i=i: e.reciprocal(col2(M, 9), M.scale_r[:, i:i + 1]),
              inc="ve")
        p.add("vector", lambda e, slot=slot: e.tensor_scalar(
            M.scrF2[:], M.wstage[:, slot, :], col2(M, 9), None, ALU.mult),
            inc=["ve", f"wf{slot}"])
        p.ws_free[slot] = p.c(f"wf{slot}")
        p.add("vector", lambda e: e.tensor_scalar(
            M.scrF2[:], M.scrF2[:], MAGIC, -MAGIC, ALU.add, ALU.add), inc="ve")
        p.add("vector", lambda e: e.tensor_copy(M.scrI2[:], M.scrF2[:]), inc="ve")
        qv = p.add("vector", lambda e: e.tensor_copy(M.qbf[:], M.scrI2[:]), inc="ve")
        pslot = i % 2
        wt = [("ve", qv), ("dc", p.dc)]
        if p.pb_gate[pslot]:
            wt.append(p.pb_gate[pslot])
        for j in range(NF):
            tv = p.add("tensor", lambda e, j=j, pslot=pslot: e.transpose(
                M.pb[pslot][:, j, :], M.qbf[:, j * 128:(j + 1) * 128], M.ident_bf[:]),
                waits=(wt if j == 0 else ()), inc=("tp" if j == NF - 1 else None))
        ev = p.add("vector", lambda e, i=i, pslot=pslot: e.tensor_copy(
            M.q_rhs[:, :, i * 128:(i + 1) * 128], M.pb[pslot][:]),
            waits=[("tp", tv)], inc="ve")
        p.pb_gate[pslot] = ("ve", ev)
    qrhs_ready = ("ve", p.c("ve"))

    def f0_prep(qt):
        evs = []
        xslot = qt % 2

        def consume(t, i, rdy, prep_slot, qt=qt, xslot=xslot, evs=evs):
            tt = t - qt * 4
            tp = emit_transpose_prep(p, M, tt % 2, rdy, prep_slot)
            evs.append(emit_prep_to_xnT(p, M, tt, xslot, tp))

        emit_prologue_group(p, M, [qt * 4 + k for k in range(4)], io["x_src"],
                            "quant",
                            [(M.xq4f0, k, f"xf0{k}") for k in range(4)], consume)
        return ("ve", max(evs))

    xnT_ready = f0_prep(0)
    for qt in range(NQ):
        xslot = qt % 2
        for oc in range(NF):
            pfk = M.pf[oc % 2]
            w0 = [xnT_ready, qrhs_ready]
            if p.pf_gate[oc % 2]:
                w0.append(p.pf_gate[oc % 2])
            for j in range(NF):
                mv = p.add("tensor", lambda e, oc=oc, j=j, pfk=pfk, xslot=xslot:
                           e.matmul(pfk[:], M.q_rhs[:, j, oc * 128:(oc + 1) * 128],
                                    M.xnT[:, xslot, j, :],
                                    start=(j == 0), stop=(j == NF - 1)),
                           waits=(w0 if j == 0 else ()),
                           inc=("mm" if j == NF - 1 else None))
            fw = [("mm", mv), ("dc", p.dc)]
            if p.qT_gate:
                fw.append(p.qT_gate)
            fv = p.add("scalar", lambda e, oc=oc, pfk=pfk: e.activation(
                M.finalT[:, oc, :], pfk[:], AF.Copy,
                scale=M.scale_r[:, oc:oc + 1]), waits=fw, inc="ae")
            p.pf_gate[oc % 2] = ("ae", fv)
        p.qT_gate = None
        final_ready = ("ae", fv)
        p.xnT_free[xslot] = p.add("scalar", _nop, inc=f"nf{xslot}")
        if qt + 1 < NQ:
            xnT_ready = f0_prep(qt + 1)
        emit_epilogue_qtile(p, M, io, qt, M.finalT, final_ready, "quant",
                            io.get("last", False))
        p.qT_gate = ("tp", p.c("tp"))



def emit_f1(p, M, io):
    qwT, scT, laT, lbT = io["qwT"], io["scT"], io["laT"], io["lbT"]
    rowrt = io["rowrt"]

    def h0_consume(t, i, rdy, prep_slot):
        pslot = t % 2
        tp = emit_transpose_prep(p, M, pslot, rdy, prep_slot)
        ev = p.add("vector", lambda e, t=t, pslot=pslot: e.tensor_copy(
            M.hT[:, :, t * 128:(t + 1) * 128], M.pb[pslot][:]), waits=[tp], inc="ve")
        p.pb_gate[pslot] = ("ve", ev)

    h_qt_ready = [None] * NQ

    p.screp_gate = None
    p.lora_scr_gate = None
    p.scrF2_gate = None
    W3 = (M.wbuf1, M.wbuf2, M.wbuf3)
    w_big = {}
    w_lora = {}
    lora_lastuse = [None, None]

    def prep_big(i, gate, js):
        wb = W3[i % 3]
        vv = 0
        for j in js:
            slot = j % 2
            w = [(f"wf{slot}", p.ws_free[slot])]
            if gate and j == js[0]:
                w.append(gate)
            dv = p.add("sync", lambda e, i=i, j=j, slot=slot: e.dma_start(
                out=M.wstage_i[:, slot, :], in_=qwT[i, j * 128:(j + 1) * 128, :]),
                waits=w, inc=f"dw{slot}", dma=True)
            sg = [p.screp_gate] if p.screp_gate else []
            dsv = 0
            for g in range(8):
                dsv = p.add("sync", lambda e, i=i, j=j, g=g: e.dma_start(
                    out=M.screp[16 * g:16 * (g + 1), :],
                    in_=scT[i, 8 * j + g:8 * j + g + 1, :].broadcast_to([16, D])),
                    waits=(sg if g == 0 else ()), inc="dsc", dma=True)
            tw = [(f"dw{slot}", dv)]
            if gate and j == js[0]:
                tw.append(gate)
            p.add("gpsimd", lambda e, slot=slot: e.tensor_scalar(
                M.scrF2[:], M.wstage_i[:, slot, :], 1.0 / 7.5, -1.0, ALU.mult,
                ALU.add), waits=tw, inc=["ge", f"wf{slot}"])
            p.ws_free[slot] = p.c(f"wf{slot}")
            vv = p.add("gpsimd", lambda e, j=j, wb=wb: e.tensor_mul(
                wb[:, j, :], M.scrF2[:], M.screp[:]),
                waits=[("dsc", dsv)], inc="ge")
            p.screp_gate = ("ge", vv)
        return ("ge", vv)

    def prep_lora(i, gate):
        li = i % 2
        lw = [gate] if gate else []
        if p.lora_scr_gate:
            lw.append(p.lora_scr_gate)
        dv = p.add("sync", lambda e, i=i: e.dma_start(
            out=M.lora_laF[:], in_=laT[i].rearrange("(a p) r -> p a r", p=128)),
            waits=lw, inc="dla", dma=True)
        lv = p.add("vector", lambda e, li=li: e.tensor_copy(
            M.la_bf[:, li, :, :], M.lora_laF[:]), waits=[("dla", dv)], inc="ve")
        dv2 = p.add("sync", lambda e, i=i: e.dma_start(
            out=M.lora_lbF[:], in_=lbT[i]),
            waits=([gate] if gate else []), inc="dlb", dma=True)
        lv2 = p.add("vector", lambda e, li=li: e.tensor_copy(
            M.lb_bf[:, li, :], M.lora_lbF[:]), waits=[("dlb", dv2)], inc="ve")
        p.lora_scr_gate = ("ve", lv2)
        return ("ve", lv2)

    for g in range(NT // 4):
        if g == 0:
            w_lora[0] = prep_lora(0, None)
            w_big[0] = prep_big(0, None, list(range(NF)))
        elif g == 2:
            w_big[1] = prep_big(1, None, list(range(NF)))
            w_lora[1] = prep_lora(1, None)
        emit_prologue_group(p, M, [g * 4 + k for k in range(4)], io["x_src"],
                            "ln", [(M.xq4f1, k, f"xf1{k}") for k in range(4)],
                            h0_consume)
    h_qt_ready = [("ve", p.c("ve"))] * NQ

    p.ysb_gate = [None, None, None]

    def y_lora(mi, src3, src_ready, qsl, extra=()):
        li = mi % 2
        lw = [src_ready, w_big[mi], w_lora[mi]] + list(extra)
        if p.pO_gate[li]:
            lw.append(p.pO_gate[li])
        if lora_lastuse[li]:
            lw.append(lora_lastuse[li])
        for j in range(NF):
            mv = p.add("tensor", lambda e, j=j, qsl=qsl, li=li: e.matmul(
                M.pf[2 + li][0:32, :], M.la_bf[:, li, j, :], src3[:, j, qsl],
                start=(j == 0), stop=(j == NF - 1), skip_group_check=True),
                waits=(lw if j == 0 else ()),
                inc=("mm" if j == NF - 1 else None))
        l1v = p.add("vector", lambda e, li=li: e.tensor_copy(
            M.l1_bf[:, li, :], M.pf[2 + li][0:32, :]), waits=[("mm", mv)],
            inc="ve")
        p.pO_gate[li] = ("ve", l1v)
        return l1v

    def y_group(mi, src3, src_ready, qsl, slot, l1v=None):
        li = mi % 2
        wb = W3[mi % 3]
        if l1v is None:
            l1v = y_lora(mi, src3, src_ready, qsl)
        evs = []
        for oc in range(NF):
            pfk = M.pf[oc % 2]
            w0 = [src_ready, w_big[mi], w_lora[mi], ("ve", l1v)]
            if p.pf_gate[oc % 2]:
                w0.append(p.pf_gate[oc % 2])
            for j in range(NF):
                p.add("tensor", lambda e, oc=oc, j=j, pfk=pfk, qsl=qsl, wb=wb:
                      e.matmul(pfk[:], wb[:, j, oc * 128:(oc + 1) * 128],
                               src3[:, j, qsl], start=(j == 0), stop=False,
                               skip_group_check=True),
                      waits=(w0 if j == 0 else ()))
            mv = p.add("tensor", lambda e, oc=oc, pfk=pfk, li=li: e.matmul(
                pfk[:], M.lb_bf[:, li, oc * 128:(oc + 1) * 128],
                M.l1_bf[:, li, :],
                start=False, stop=True, skip_group_check=True), inc="mm")
            ew = [("mm", mv), ("dc", p.dc)]
            if oc == 0 and p.ysb_gate[slot]:
                ew.append(p.ysb_gate[slot])
            yev = p.add("scalar", lambda e, oc=oc, pfk=pfk, mi=mi, slot=slot:
                        e.activation(M.ySB[:, slot, oc, :], pfk[:], AF.Identity,
                                     bias=M.bcol[:, mi, oc:oc + 1]),
                        waits=ew, inc="ae")
            p.pf_gate[oc % 2] = ("ae", yev)
            evs.append(yev)
        return evs, p.c("mm")

    sq_gate = [None] * NF

    def _sqb(oc, s):
        if oc < 6:
            return M.sqF1[:, oc, :]
        if s < 2:
            return M.sqX[:, oc - 6, :]
        return M.sqscr512[:]

    def emit_sq(k, s):
        slot = k % 3
        cur_evs = y1q[k][0]
        hi = NF if s < 2 else 7
        vs = {}
        for oc in range(hi):
            qw = [("ae", cur_evs[oc])]
            if oc == 6 and s == 2:
                if p.sqscr_gate:
                    qw.append(p.sqscr_gate)
            elif sq_gate[oc]:
                qw.append(sq_gate[oc])
            vs[oc] = p.add("scalar", lambda e, oc=oc, slot=slot, s=s:
                           e.activation(_sqb(oc, s), M.ySB[:, slot, oc, :],
                                        AF.Square), waits=qw, inc="ae")
        return vs

    def emit_stats_mm(k, sqvs, s):
        slot = k % 3
        cur_evs = y1q[k][0]
        qmv = 0
        for oc in range(NF):
            sw = [("ae", cur_evs[oc]), ("vc", p.vc)]
            if oc == 0 and p.pf4_gate:
                sw.append(p.pf4_gate)
            p.add("tensor", lambda e, oc=oc, slot=slot: e.matmul(
                M.pf[4][:], M.ones128[:], M.ySB[:, slot, oc, :],
                start=(oc == 0), stop=(oc == NF - 1), skip_group_check=True),
                waits=sw, inc="mm")
            if oc in sqvs:
                sqv = sqvs[oc]
            else:
                sqv = p.add("scalar", lambda e, slot=slot: e.activation(
                    M.sqscr512[:], M.ySB[:, slot, 7, :], AF.Square),
                    waits=[("ae", cur_evs[7]), ("mm", qmv)], inc="ae")
            qw2 = [("ae", sqv), ("vc", p.vc)]
            if oc == 0 and p.pf5_gate:
                qw2.append(p.pf5_gate)
            qmv = p.add("tensor", lambda e, oc=oc, slot=slot, s=s: e.matmul(
                M.pf[5][:], M.ones128[:], _sqb(oc, s),
                start=(oc == 0), stop=(oc == NF - 1), skip_group_check=True),
                waits=qw2, inc="mm")
            if oc >= 6 and s == 2:
                p.sqscr_gate = ("mm", qmv)
            else:
                sq_gate[oc] = ("mm", qmv)
        return qmv

    chain_bv = {}

    def _ab(k):
        return (M.a_bc, M.b_bc) if k % 2 == 0 else (M.a_bc2, M.b_bc2)

    def chain_emit(k, qmv):
        ab, bb = _ab(k)
        h0 = M.h0chain[:]
        rw = [("mm", qmv)]
        if p.abc_gate[k % 2]:
            rw.append(p.abc_gate[k % 2])
        p.add("vector", lambda e, bb=bb: e.tensor_scalar(
            bb[:], M.pf[4][:], -1.0 / D, None, ALU.mult), waits=rw, inc="ve")
        p.pf4_gate = ("ve", p.c("ve"))
        p.add("vector", lambda e, ab=ab: e.tensor_scalar(
            ab[:], M.pf[5][:], 1.0 / D, None, ALU.mult), inc="ve")
        p.pf5_gate = ("ve", p.c("ve"))
        p.add("vector", lambda e, h0=h0, bb=bb: e.tensor_mul(h0, bb[:], bb[:]),
              inc="ve")
        p.add("vector", lambda e, h0=h0, ab=ab: e.tensor_sub(ab[:], ab[:], h0),
              inc="ve")
        vv = p.add("vector", lambda e, ab=ab: e.tensor_scalar(
            ab[:], ab[:], 1e-5, None, ALU.add), inc="ve")
        a1 = p.add("scalar", lambda e, h0=h0, ab=ab: e.activation(
            h0, ab[:], AF.Sqrt), waits=[("ve", vv)], inc="ae")
        p.add("vector", lambda e, h0=h0, ab=ab: e.reciprocal(ab[:], h0),
              waits=[("ae", a1)], inc="ve")
        chain_bv[k] = p.add("vector", lambda e, ab=ab, bb=bb: e.tensor_mul(
            bb[:], bb[:], ab[:]), inc="ve")

    NG = 3 * NQ

    def g_args(g):
        s, qt = g // NQ, g % NQ
        return 2 * s, 2 * s + 1, qt, slice(qt * 512, (qt + 1) * 512), g % 3, s

    def y1_emit(g, l1v=None):
        a, _, qt, qsl, slot, s = g_args(g)
        return y_group(a, M.hT, h_qt_ready[qt], qsl, slot, l1v=l1v)

    prev_y2_last = None
    y1q = {}
    y1q[0] = y1_emit(0)
    y1q[1] = y1_emit(1)
    l1v3 = y_lora(0, M.hT, h_qt_ready[2], slice(1024, 1536),
                  extra=[("mm", y1q[1][1])])
    y1q[2] = y1_emit(2, l1v=l1v3)
    qmv = emit_stats_mm(0, emit_sq(0, 0), 0)
    chain_emit(0, qmv)
    sqx_last = None
    for g in range(NG):
        a, b, qt, qsl, slot, s = g_args(g)
        cur_evs = y1q[g][0]
        ab, bb = _ab(g)
        if qt == 0 and s > 0:
            w_lora[b] = prep_lora(b, prev_y2_last)
        if g == 2 * NQ:
            for xs in (0, 1):
                p.xq_free[f"xf{xs}"] = p.add(
                    "sync", _nop, waits=[sqx_last], inc=f"xf{xs}")
        sqvs_next = emit_sq(g + 1, (g + 1) // NQ) if g + 1 < NG else None
        gelu_vs = []
        for oc in range(NF):
            aslot = oc % 2
            asl = M.apply_scr[:, aslot * 512:(aslot + 1) * 512]
            ub = M.sqjunk[:, aslot * 512:(aslot + 1) * 512]
            pw = [("ve", chain_bv[g]), ("ae", cur_evs[oc])]
            if p.applyscr_gate[aslot]:
                pw.append(p.applyscr_gate[aslot])
            p.add("vector", lambda e, oc=oc, asl=asl, slot=slot, ab=ab:
                  e.tensor_mul(asl, M.ySB[:, slot, oc, :], ab[:]),
                  waits=pw, inc="ve")
            v1 = p.add("vector", lambda e, asl=asl, ub=ub, bb=bb:
                       e.tensor_add(ub, asl, bb[:]), inc="ve")
            gv = p.add("scalar", lambda e, oc=oc, ub=ub, s=s: e.activation(
                M.feat8[:, oc, :], ub, AF.Gelu,
                bias=M.lbcol[:, s, oc:oc + 1], scale=M.lgcol[:, s, oc:oc + 1]),
                waits=[("ve", v1), ("dc", p.dc)], inc="ae")
            p.applyscr_gate[aslot] = ("ae", gv)
            gelu_vs.append(gv)
        p.abc_gate[g % 2] = ("ae", max(gelu_vs))
        p.ysb_gate[slot] = ("ae", max(gelu_vs))
        g_ready = ("ae", max(gelu_vs))
        l1v_next = None
        if g + 3 < NG:
            a3, _, qt3, qsl3, _, _ = g_args(g + 3)
            l1v_next = y_lora(a3, M.hT, h_qt_ready[qt3], qsl3,
                              extra=[("mm", y1q[g + 2][1])])
        y2_evs, y2_last = y_group(b, M.feat8, g_ready, slice(0, 512), slot)
        hv = 0
        for oc in range(NF):
            hv = p.add("vector", lambda e, oc=oc, qsl=qsl, slot=slot:
                       e.tensor_add(M.hT[:, oc, qsl], M.hT[:, oc, qsl],
                                    M.ySB[:, slot, oc, :]),
                       waits=[("ae", y2_evs[oc])], inc="ve")
        h_qt_ready[qt] = ("ve", hv)
        p.ysb_gate[slot] = ("ve", hv)
        if g + 1 < NG:
            qmv = emit_stats_mm(g + 1, sqvs_next, (g + 1) // NQ)
            chain_emit(g + 1, qmv)
            if g + 1 == 2 * NQ:
                sqx_last = ("mm", qmv)
        if s == 2:
            emit_epilogue_qtile(p, M, io, qt,
                                M.hT[:, :, qt * 512:(qt + 1) * 512],
                                h_qt_ready[qt], "const",
                                io.get("last", False))
        if g + 3 < NG:
            y1q[g + 3] = y1_emit(g + 3, l1v=l1v_next)
        if s < 2:
            nx0, nx1 = 2 * s + 2, 2 * s + 3
            g0 = None if s == 0 else prev_y2_last
            if qt == 0:
                prep_big(nx0, g0, [0, 1, 2])
            elif qt == 1:
                prep_big(nx0, None, [3, 4, 5])
            elif qt == 2:
                w_big[nx0] = prep_big(nx0, None, [6, 7])
            elif qt == 4:
                y1_all = max(y1q[gg][1] for gg in range(NQ * s, NQ * s + NQ))
                lora_lastuse[0] = ("mm", y1_all)
                prep_big(nx1, ("mm", y1_all), [0, 1, 2])
                w_lora[nx0] = prep_lora(nx0, ("mm", y1_all))
            elif qt == 5:
                prep_big(nx1, None, [3, 4, 5])
            elif qt == 6:
                w_big[nx1] = prep_big(nx1, None, [6, 7])
        if qt == NQ - 1:
            y1_last = max(y1q[gg][1] for gg in range(NQ * s, NQ * s + NQ))
            prev_y2_last = ("mm", y2_last)
            lora_lastuse = [("mm", y1_all if s < 2 else y1_last),
                            ("mm", y2_last)]



def _new_prog(serial=False):
    nc = bass.Bass("TRN2", target_bir_lowering=False, dynamic_dma_scratch_size=512)
    p = Prog(nc, serial=serial)
    p.probs_gate_g = [None, None]
    M = Mem(nc)
    return nc, p, M


def _finish(p):
    p.add("sync", _nop, waits=[(s, p.c(s)) for s in ("do0", "do1")])
    p.emit()


def build_attn_program(layer, serial=False):
    nc, p, M = _new_prog(serial)
    x = nc.dram_tensor("x", [SEQ, D], F32, kind="ExternalInput").ap()
    out = nc.dram_tensor("out", [SEQ, D], F32, kind="ExternalOutput").ap()
    wqkvT = nc.dram_tensor("wqkvT", [D, 3 * D], F32, kind="ExternalInput").ap()
    woutT = nc.dram_tensor("woutT", [D, D], F32, kind="ExternalInput").ap()
    gcols = nc.dram_tensor("gcols", [128, 2, 8], F32, kind="ExternalInput").ap()
    rowrt = nc.dram_tensor("rowrt", [8, 512], F32, kind="Internal").ap()
    io = {"x_src": x, "x_dst": out, "wqkvT": wqkvT, "woutT": woutT,
          "gcols": gcols, "rowrt": rowrt}
    emit_consts(p, M, io)
    emit_attn(p, M, io, layer)
    _finish(p)
    return nc


def build_f0_program(serial=False):
    nc, p, M = _new_prog(serial)
    x = nc.dram_tensor("x", [SEQ, D], F32, kind="ExternalInput").ap()
    out = nc.dram_tensor("out", [SEQ, D], F32, kind="ExternalOutput").ap()
    rhsT = nc.dram_tensor("rhsT", [D, D], F32, kind="ExternalInput").ap()
    io = {"x_src": x, "x_dst": out, "rhsT": rhsT}
    emit_consts(p, M, io)
    emit_f0(p, M, io)
    _finish(p)
    return nc


def build_f1_program(serial=False):
    nc, p, M = _new_prog(serial)
    x = nc.dram_tensor("x", [SEQ, D], F32, kind="ExternalInput").ap()
    out = nc.dram_tensor("out", [SEQ, D], F32, kind="ExternalOutput").ap()
    qwT = nc.dram_tensor("qwT", [6, D, D], I32, kind="ExternalInput").ap()
    scT = nc.dram_tensor("scT", [6, 64, D], F32, kind="ExternalInput").ap()
    laT = nc.dram_tensor("laT", [6, D, 32], F32, kind="ExternalInput").ap()
    lbT = nc.dram_tensor("lbT", [6, 32, D], F32, kind="ExternalInput").ap()
    lgcols = nc.dram_tensor("lgcols", [128, 3, 8], F32, kind="ExternalInput").ap()
    lbcols = nc.dram_tensor("lbcols", [128, 3, 8], F32, kind="ExternalInput").ap()
    bcols = nc.dram_tensor("bcols", [128, 6, 8], F32, kind="ExternalInput").ap()
    rowrt = nc.dram_tensor("rowrt", [8, 512], F32, kind="Internal").ap()
    io = {"x_src": x, "x_dst": out, "qwT": qwT, "scT": scT, "laT": laT,
          "lbT": lbT, "lgcols": lgcols, "lbcols": lbcols, "bcols": bcols,
          "rowrt": rowrt}
    emit_consts(p, M, io)
    emit_f1(p, M, io)
    _finish(p)
    return nc


def build_fused_program(serial=False):
    nc, p, M = _new_prog(serial)
    x = nc.dram_tensor("x", [SEQ, D], F32, kind="ExternalInput").ap()
    out = nc.dram_tensor("out", [SEQ, D], F32, kind="ExternalOutput").ap()
    xbuf = nc.dram_tensor("xbuf", [SEQ, D], F32, kind="Internal").ap()
    wqkvT0 = nc.dram_tensor("wqkvT0", [D, 3 * D], F32, kind="ExternalInput").ap()
    woutT0 = nc.dram_tensor("woutT0", [D, D], F32, kind="ExternalInput").ap()
    wqkvT1 = nc.dram_tensor("wqkvT1", [D, 3 * D], F32, kind="ExternalInput").ap()
    woutT1 = nc.dram_tensor("woutT1", [D, D], F32, kind="ExternalInput").ap()
    gcols = nc.dram_tensor("gcols", [128, 2, 8], F32, kind="ExternalInput").ap()
    rhsT = nc.dram_tensor("rhsT", [D, D], F32, kind="ExternalInput").ap()
    qwT = nc.dram_tensor("qwT", [6, D, D], I32, kind="ExternalInput").ap()
    scT = nc.dram_tensor("scT", [6, 64, D], F32, kind="ExternalInput").ap()
    laT = nc.dram_tensor("laT", [6, D, 32], F32, kind="ExternalInput").ap()
    lbT = nc.dram_tensor("lbT", [6, 32, D], F32, kind="ExternalInput").ap()
    lgcols = nc.dram_tensor("lgcols", [128, 3, 8], F32, kind="ExternalInput").ap()
    lbcols = nc.dram_tensor("lbcols", [128, 3, 8], F32, kind="ExternalInput").ap()
    bcols = nc.dram_tensor("bcols", [128, 6, 8], F32, kind="ExternalInput").ap()
    rowrt = nc.dram_tensor("rowrt", [8, 512], F32, kind="Internal").ap()
    base = {"gcols": gcols, "lgcols": lgcols, "lbcols": lbcols, "bcols": bcols,
            "rowrt": rowrt}
    emit_consts(p, M, base)
    io = dict(base, x_src=x, x_dst=xbuf, wqkvT=wqkvT0, woutT=woutT0)
    emit_attn(p, M, io, 0)
    p.barrier()
    io = dict(base, x_src=xbuf, x_dst=xbuf, rhsT=rhsT)
    emit_f0(p, M, io)
    p.barrier()
    io = dict(base, x_src=xbuf, x_dst=xbuf, wqkvT=wqkvT1, woutT=woutT1)
    emit_attn(p, M, io, 1)
    p.barrier()
    io = dict(base, x_src=xbuf, x_dst=out, qwT=qwT, scT=scT, laT=laT, lbT=lbT,
              last=True)
    emit_f1(p, M, io)
    _finish(p)
    return nc



def _layout_weights(inputs):
    ct = np.ascontiguousarray
    d = {}
    d["wqkvT0"] = ct(inputs["L0_attn_w_qkv"].T)
    d["woutT0"] = ct(inputs["L0_attn_w_out"].T)
    d["wqkvT1"] = ct(inputs["L1_attn_w_qkv"].T)
    d["woutT1"] = ct(inputs["L1_attn_w_out"].T)
    g0 = ct(inputs["L0_attn_gamma"].reshape(8, 128).T)
    g1 = ct(inputs["L1_attn_gamma"].reshape(8, 128).T)
    d["gcols"] = ct(np.stack([g0, g1], axis=1))
    d["rhsT"] = ct(inputs["L0_ffn_rhs"].T)
    d["qwT"] = ct(inputs["L1_ffn_qweights"].transpose(0, 2, 1))
    d["scT"] = ct(inputs["L1_ffn_scales"].transpose(0, 2, 1))
    d["laT"] = ct(inputs["L1_ffn_lora_a"].transpose(0, 2, 1))
    d["lbT"] = ct(inputs["L1_ffn_lora_b"].transpose(0, 2, 1))
    d["lgcols"] = ct(inputs["L1_ffn_ln_gamma"].reshape(3, 8, 128).transpose(2, 0, 1))
    d["lbcols"] = ct(inputs["L1_ffn_ln_beta"].reshape(3, 8, 128).transpose(2, 0, 1))
    d["bcols"] = ct(inputs["L1_ffn_biases"].reshape(6, 8, 128).transpose(2, 0, 1))
    return d


def kernel(**inputs):
    ins = {k: np.asarray(v) for k, v in inputs.items()}
    w = _layout_weights(ins)
    xs = [np.ascontiguousarray(ins["L0_attn_x"][b]) for b in range(8)]
    cores = list(range(8))
    if FUSED:
        nc = build_fused_program(serial=SERIAL)
        in_maps = [dict(w, x=xs[b]) for b in range(8)]
        res = run_bass_kernel_spmd(nc, in_maps, core_ids=cores)
        return np.stack([res.results[b]["out"] for b in range(8)], axis=0)
    cur = xs
    nc = build_attn_program(0, serial=SERIAL)
    res = run_bass_kernel_spmd(nc, [
        {"x": cur[b], "wqkvT": w["wqkvT0"], "woutT": w["woutT0"],
         "gcols": w["gcols"]} for b in range(8)], core_ids=cores)
    cur = [res.results[b]["out"] for b in range(8)]
    nc = build_f0_program(serial=SERIAL)
    res = run_bass_kernel_spmd(nc, [
        {"x": cur[b], "rhsT": w["rhsT"]} for b in range(8)], core_ids=cores)
    cur = [res.results[b]["out"] for b in range(8)]
    nc = build_attn_program(1, serial=SERIAL)
    res = run_bass_kernel_spmd(nc, [
        {"x": cur[b], "wqkvT": w["wqkvT1"], "woutT": w["woutT1"],
         "gcols": w["gcols"]} for b in range(8)], core_ids=cores)
    cur = [res.results[b]["out"] for b in range(8)]
    nc = build_f1_program(serial=SERIAL)
    res = run_bass_kernel_spmd(nc, [
        {"x": cur[b], "qwT": w["qwT"], "scT": w["scT"], "laT": w["laT"],
         "lbT": w["lbT"], "lgcols": w["lgcols"], "lbcols": w["lbcols"],
         "bcols": w["bcols"]} for b in range(8)], core_ids=cores)
    return np.stack([res.results[b]["out"] for b in range(8)], axis=0)


# version 28
# speedup vs baseline: 1.1067x; 1.1067x over previous
import numpy as np
import ml_dtypes
import concourse.bass as bass
import concourse.mybir as mybir
from concourse.bass_utils import run_bass_kernel_spmd

F32 = mybir.dt.float32
BF16 = mybir.dt.bfloat16
I32 = mybir.dt.int32
AF = mybir.ActivationFunctionType
ALU = mybir.AluOpType
AX = mybir.AxisListType
BF = ml_dtypes.bfloat16

SEQ = 4096
D = 1024
NT = SEQ // 128
NQ = SEQ // 512
H = 8
DH = 128
NF = D // 128
SCALE_ATTN = float(DH) ** -0.5
C1 = float(np.sqrt(128.0))
MAGIC = 12582912.0

FUSED = True
SERIAL = False
LIMIT_BQ = 8
LIMIT_H = 8
SKIP_EPI = False
EPI_PART = 4


def _nop(e):
    return e.engine_nop() if hasattr(e, "engine_nop") else e.nop()


class Prog:
    ENGINES = ("sync", "vector", "scalar", "tensor", "gpsimd")

    def __init__(self, nc, serial=False):
        self.nc = nc
        self.steps = []
        self.sems = {}
        self.counts = {}
        self.serial = serial
        self.xq_free = {}
        self.xnT_free = [0, 0]
        self.ws_free = [0, 0]
        self.pb_gate = [None, None]
        self.pf_gate = [None, None]
        self.pO_gate = [None, None]
        self.pf4_gate = None
        self.pf5_gate = None
        self.sqscr_gate = None
        self.prep_gate = [None, None]
        self.rcpbc_gate = [None, None]
        self.rcprow_gate = None
        self.rowrt_gate = [None] * 8
        self.rows_gate = []
        self.f0chain_gate = None
        self.negMb_gate = [None] * 8
        self.qT_gate = None
        self.feat8_gate = None
        self.screp_gate = None
        self.applyscr_gate = [None, None]
        self.abc_gate = [None, None]
        self.xstore = [None] * NT
        self.rt_i = 0
        self.dc = 0
        self.vc = 0

    def sem(self, name):
        if name not in self.sems:
            self.sems[name] = self.nc.alloc_semaphore(name)
            self.counts[name] = 0
        return name

    def c(self, name):
        return self.counts.get(name, 0)

    def add(self, eng, fn, waits=(), inc=None, dma=False):
        amt = 16 if dma else 1
        incs = inc if isinstance(inc, (list, tuple)) else ([inc] if inc else [])
        out = []
        for s in incs:
            self.sem(s)
            self.counts[s] += amt
            out.append((s, amt))
        ws = [(s, v) for (s, v) in waits if s is not None and v > 0]
        for s, _ in ws:
            self.sem(s)
        self.steps.append((eng, fn, ws, out[:1], dma))
        for extra in out[1:]:
            self.steps.append((eng, _nop, [], [extra], False))
        return self.counts[incs[0]] if incs else 0

    def barrier(self):
        ws = [(s, self.c(s)) for s in ("do0", "do1") if self.c(s) > 0]
        if not ws:
            return
        for eng in self.ENGINES:
            self.add(eng, _nop, waits=ws)

    def emit(self):
        steps = self.steps
        if self.serial:
            self.sem("__SER")
            new = []
            tot = 0
            run = {}
            for (eng, fn, waits, incs, dma) in steps:
                amt = 16 if dma else 1
                if incs:
                    s0, a0 = incs[0]
                    run[s0] = run.get(s0, 0) + a0
                    new.append((eng, fn, waits + [("__SER", tot)], incs, dma))
                    new.append((eng, _nop, [(s0, run[s0])], [("__SER", amt)], False))
                else:
                    new.append((eng, fn, waits + [("__SER", tot)],
                                [("__SER", amt)], dma))
                tot += amt
            steps = new
        with self.nc.Block() as block:
            for eng_name in self.ENGINES:
                def mk(name):
                    def f(eng):
                        hw = {}
                        own = {}
                        for (e, fn, waits, incs, dma) in steps:
                            if e != name:
                                continue
                            pend = []
                            if name != "tensor":
                                for (s, v) in own.items():
                                    if v > hw.get(s, 0):
                                        pend.append((s, v))
                                        hw[s] = v
                            for (s, v) in waits:
                                if v > hw.get(s, 0):
                                    pend.append((s, v))
                                    hw[s] = v
                            for (s, v) in pend[:-1]:
                                eng.wait_ge(self.sems[s], v)
                            inst = fn(eng)
                            if pend:
                                s, v = pend[-1]
                                inst._wait_ge(self.sems[s], v)
                            for (s, a) in incs:
                                inst.then_inc(self.sems[s], a)
                                if not dma:
                                    own[s] = own.get(s, 0) + a
                    return f
                getattr(block, eng_name)(mk(eng_name))



class Mem:
    def __init__(self, nc):
        self.nc = nc
        self.off = (nc.sbuf_base + 31) & ~31
        self._offsets = {}
        A = self._at

        self.ident_bf = A("ident_bf", [128, 128], BF16)
        self.ones_col = A("ones_col", [128, 1], BF16)
        self.onesK1 = A("onesK1", [1, 128], BF16)
        self.ones128 = A("ones128", [128, 128], BF16)
        self.gcol = A("gcol", [128, 2, 8], F32)
        self.lgcol = A("lgcol", [128, 3, 8], F32)
        self.lbcol = A("lbcol", [128, 3, 8], F32)
        self.bcol = A("bcol", [128, 6, 8], F32)
        self.epscol = A("epscol", [128, 1], F32)
        self.eps11 = A("eps11", [1, 1], F32)
        self.cols = A("cols", [128, 16], F32)
        self.cols2 = A("cols2", [128, 16], F32)
        self.cols3 = A("cols3", [128, 16], F32)
        self.cols4 = A("cols4", [128, 8], F32)
        self.sqjunk = A("sqjunk", [128, 1024], BF16)
        self.negM_col = A("negM_col", [128, 8], F32)
        self.negMb = A("negMb", [1, 8], BF16)
        self.m11 = A("m11", [1, 8], F32)
        self.kmax = A("kmax", [128, 64], F32)
        self.rcp_row = A("rcp_row", [1, 512], F32)
        self.rcp_bc = A("rcp_bc", [128, 2, 512], F32)
        self.probs = A("probs", [128, 2, 512], BF16)
        self.sqscr512 = A("sqscr512", [128, 512], BF16)
        self.scale_r = A("scale_r", [128, 8], F32)

        self.xq = A("xq", [128, 2, 1024], F32)
        self.coeffs = A("coeffs", [128, 32, 4], F32)
        self.prep_bf = A("prep_bf", [128, 2, 1024], BF16)
        self.sq1024 = A("sq1024", [128, 1024], BF16, alias="prep_bf")
        self.scrF = A("scrF", [128, 1024], F32)
        self.xnT = A("xnT", [128, 2, 8, 512], BF16)
        self.wbuf1 = A("wbuf1", [128, 8, 1024], BF16)
        self.wbuf2 = A("wbuf2", [128, 8, 1024], BF16)
        self.wstage = A("wstage", [128, 2, 1024], F32)
        self.wstage_i = A("wstage_i", [128, 2, 1024], I32, alias="wstage")
        self.kT = A("kT", [128, 8, 4096], BF16)
        self.hT = A("hT", [128, 8, 4096], BF16, alias="kT")
        self.xq4f0 = A("xq4f0", [128, 4, 1024], F32, alias="kT")
        self.v_sb = A("v_sb", [128, 32, 1024], BF16)
        self.ySB = A("ySB", [128, 3, 8, 512], BF16, alias="v_sb")
        self.xq4f1 = A("xq4f1", [128, 4, 1024], F32, alias="v_sb")
        self.q_rhs = A("q_rhs", [128, 8, 1024], BF16, alias="v_sb")
        self.a_bc = A("a_bc", [128, 512], F32, alias="v_sb", sub=24576)
        self.b_bc = A("b_bc", [128, 512], F32, alias="v_sb", sub=26624)
        self.screp = A("screp", [128, 1024], F32, alias="v_sb", sub=28672)
        self.scrF2 = A("scrF2", [128, 1024], F32, alias="v_sb", sub=32768)
        self.apply_scr = A("apply_scr", [128, 1024], F32, alias="v_sb", sub=36864)
        self.la_bf = A("la_bf", [128, 2, 8, 32], BF16, alias="v_sb", sub=40960)
        self.lb_bf = A("lb_bf", [32, 2, 1024], BF16, alias="v_sb", sub=41984)
        self.l1_bf = A("l1_bf", [32, 2, 512], BF16, alias="v_sb", sub=46080)
        self.scrI = A("scrI", [128, 1024], I32, alias="v_sb", sub=48128)
        self.qbf = A("qbf", [128, 1024], BF16, alias="v_sb", sub=52224)
        self.scrI2 = A("scrI2", [128, 1024], I32, alias="v_sb", sub=54272)
        self.qT = A("qT", [128, 8, 512], BF16, alias="xnT", sub=8192)
        self.wbuf3 = A("wbuf3", [128, 8, 1024], BF16, alias="xnT")
        self.lora_laF = A("lora_laF", [128, 8, 32], F32, alias="v_sb", sub=54272)
        self.lora_lbF = A("lora_lbF", [32, 1024], F32, alias="v_sb", sub=55296)
        self.sqF1 = A("sqF1", [128, 6, 512], BF16, alias="v_sb", sub=48128)
        self.sqX = A("sqX", [128, 2, 512], BF16, alias="xq")
        self.sqA = A("sqA", [128, 8, 512], BF16, alias="xq")
        self.h0chain = A("h0chain", [128, 512], F32, alias="prep_bf", sub=2048)
        self.a_bc2 = A("a_bc2", [128, 512], F32, alias="v_sb", sub=59392)
        self.b_bc2 = A("b_bc2", [128, 512], F32, alias="v_sb", sub=61440)
        self.finalT = A("finalT", [128, 8, 512], BF16, alias="wstage")
        self.feat8 = A("feat8", [128, 8, 512], BF16)
        self.xqa2 = A("xqa2", [128, 2, 1024], F32, alias="feat8")

        assert self.off <= nc.sbuf_top, f"SBUF over budget: {self.off}"
        self.pf = [nc.alloc_psum_tensor(f"pf{i}", [128, 512], F32) for i in range(6)]
        self.pb = [nc.alloc_psum_tensor(f"pb{i}", [128, 8, 128], BF16) for i in range(2)]

    def _at(self, name, shape, dtype, alias=None, sub=0):
        esize = {F32: 4, BF16: 2, I32: 4}[dtype]
        nbytes = int(np.prod(shape[1:])) * esize
        if alias is not None:
            off = self._offsets[alias] + sub
        else:
            off = self.off
            self.off += (nbytes + 31) & ~31
        t = self.nc.alloc_sbuf_tensor_at(name, shape, dtype, offset=off)
        self._offsets[name] = off
        return t


def col(M, i):
    return M.cols[:, i:i + 1]


def col2(M, i):
    return M.cols2[:, i:i + 1]



def emit_consts(p, M, io):
    nc = p.nc
    idb = nc.inline_tensor(np.eye(128, dtype=BF), "identb_c")
    p.add("sync", lambda e: e.dma_start(out=M.ident_bf[:], in_=idb.ap()),
          inc="dc", dma=True)
    p.add("vector", lambda e: e.memset(M.ones_col[:], 1.0), inc="vc")
    p.add("vector", lambda e: e.memset(M.onesK1[:], 1.0), inc="vc")
    p.add("vector", lambda e: e.memset(M.ones128[:], 1.0), inc="vc")
    p.add("vector", lambda e: e.memset(M.epscol[:], 1e-5), inc="vc")
    p.add("vector", lambda e: e.memset(M.eps11[:], 1e-5), inc="vc")
    for key, dst in (("gcols", M.gcol), ("lgcols", M.lgcol), ("lbcols", M.lbcol),
                     ("bcols", M.bcol)):
        if key in io:
            p.add("sync", lambda e, ap=io[key], dst=dst: e.dma_start(out=dst[:], in_=ap),
                  inc="dc", dma=True)
    p.dc = p.c("dc")
    p.vc = p.c("vc")


def emit_prologue_group(p, M, tiles, x_src, mode, bufs, consume):
    cf = M.coeffs
    n = len(tiles)
    avs = []
    xs = []
    for i, (t, (buf, slot, sem)) in enumerate(zip(tiles, bufs)):
        xsl = buf[:, slot, :]
        xs.append(xsl)
        w = [(sem, p.xq_free.get(sem, 0))]
        if p.xstore[t]:
            w.append(p.xstore[t])
        dxv = p.add("sync", lambda e, t=t, xsl=xsl: e.dma_start(
            out=xsl, in_=x_src[t * 128:(t + 1) * 128, :]), waits=w,
            inc=f"d{sem}", dma=True)
        p.add("vector", lambda e, xsl=xsl, i=i: e.reduce_sum(
            M.cols3[:, i:i + 1], xsl, axis=AX.X),
            waits=[(f"d{sem}", dxv)], inc="ve")
        aw = [(f"d{sem}", dxv)]
        av = p.add("scalar", lambda e, xsl=xsl, i=i: e.activation(
            M.sqjunk[:], xsl, AF.Square, accum_out=M.cols3[:, 4 + i:5 + i]),
            waits=aw, inc="ae")
        avs.append(av)
    for i, t in enumerate(tiles):
        p.add("vector", lambda e, t=t, i=i: e.tensor_scalar(
            cf[:, t, 0:1], M.cols3[:, i:i + 1], -1.0 / D, None, ALU.mult),
            inc="ve")
        p.add("vector", lambda e, t=t, i=i: e.tensor_mul(
            M.cols3[:, 8 + 2 * i:9 + 2 * i], cf[:, t, 0:1], cf[:, t, 0:1]),
            inc="ve")
        p.add("vector", lambda e, i=i: e.tensor_scalar(
            M.cols3[:, 8 + 2 * i:9 + 2 * i], M.cols3[:, 4 + i:5 + i], 1.0 / D,
            M.cols3[:, 8 + 2 * i:9 + 2 * i], ALU.mult, ALU.subtract),
            waits=[("ae", avs[i])], inc="ve")
        vv = p.add("vector", lambda e, i=i: e.tensor_scalar(
            M.cols3[:, 9 + 2 * i:10 + 2 * i], M.cols3[:, 8 + 2 * i:9 + 2 * i],
            1e-5, None, ALU.add), inc="ve")
    sq = p.add("scalar", lambda e, n=n: e.activation(
        M.cols4[:, 0:2 * n], M.cols3[:, 8:8 + 2 * n], AF.Sqrt),
        waits=[("ve", vv)], inc="ae")
    for i, t in enumerate(tiles):
        sv = M.cols4[:, 2 * i:2 * i + 1]
        sve = M.cols4[:, 2 * i + 1:2 * i + 2]
        if mode == "attn":
            p.add("vector", lambda e, sve=sve: e.reciprocal(col(M, 8), sve),
                  waits=[("ae", sq)], inc="ve")
            p.add("vector", lambda e, sv=sv: e.tensor_mul(col(M, 9), col(M, 8), sv),
                  inc="ve")
            p.add("vector", lambda e: e.tensor_scalar(col(M, 9), col(M, 9), 32.0,
                                                      1e-12, ALU.mult, ALU.max),
                  inc="ve")
            p.add("vector", lambda e: e.reciprocal(col(M, 10), col(M, 9)), inc="ve")
            p.add("vector", lambda e: e.tensor_mul(col(M, 10), col(M, 8),
                                                   col(M, 10)), inc="ve")
            p.add("vector", lambda e, t=t: e.tensor_scalar(
                cf[:, t, 1:2], col(M, 10), 32.0, None, ALU.mult), inc="ve")
            p.add("vector", lambda e, t=t: e.tensor_scalar(
                cf[:, t, 2:3], cf[:, t, 1:2], SCALE_ATTN, None, ALU.mult),
                inc="ve")
            ps_ = i % 2
            gw = [p.prep_gate[ps_]] if p.prep_gate[ps_] else []
            vap = p.add("vector", lambda e, t=t, xsl=xs[i], ps_=ps_: e.tensor_scalar(
                M.prep_bf[:, ps_, :], xsl, cf[:, t, 0:1], cf[:, t, 1:2], ALU.add,
                ALU.mult), waits=gw, inc=["ve", bufs[i][2]])
            p.xq_free[bufs[i][2]] = p.c(bufs[i][2])
        elif mode == "ln":
            p.add("vector", lambda e, t=t, sve=sve: e.reciprocal(cf[:, t, 1:2], sve),
                  waits=[("ae", sq)], inc="ve")
            ps_ = i % 2
            gw = [p.prep_gate[ps_]] if p.prep_gate[ps_] else []
            vap = p.add("vector", lambda e, t=t, xsl=xs[i], ps_=ps_: e.tensor_scalar(
                M.prep_bf[:, ps_, :], xsl, cf[:, t, 0:1], cf[:, t, 1:2], ALU.add,
                ALU.mult), waits=gw, inc=["ve", bufs[i][2]])
            p.xq_free[bufs[i][2]] = p.c(bufs[i][2])
        else:
            p.add("vector", lambda e, t=t, sve=sve: e.reciprocal(cf[:, t, 1:2], sve),
                  waits=[("ae", sq)], inc="ve")
            aw2 = [p.f0chain_gate] if p.f0chain_gate else []
            p.add("vector", lambda e, t=t, xsl=xs[i]: e.tensor_scalar(
                M.apply_scr[:], xsl, cf[:, t, 0:1], cf[:, t, 1:2], ALU.add,
                ALU.mult), waits=aw2, inc=["ve", bufs[i][2]])
            p.xq_free[bufs[i][2]] = p.c(bufs[i][2])
            p.add("vector", lambda e: e.tensor_reduce(
                col(M, 8), M.apply_scr[:], AX.X, ALU.max,
                apply_absolute_value=True), inc="ve")
            p.add("vector", lambda e: e.tensor_scalar(col(M, 9), col(M, 8), 1e-12,
                                                      1.0 / 127.0, ALU.max,
                                                      ALU.mult), inc="ve")
            p.add("vector", lambda e, t=t: e.reciprocal(cf[:, t, 2:3], col(M, 9)),
                  inc="ve")
            p.add("vector", lambda e, t=t: e.tensor_mul(col(M, 10), cf[:, t, 2:3],
                                                        cf[:, t, 2:3]), inc="ve")
            p.add("vector", lambda e, t=t: e.tensor_scalar(
                cf[:, t, 3:4], col(M, 10), 1e-5, None, ALU.mult), inc="ve")
            zv = p.add("vector", lambda e, t=t: e.tensor_scalar(
                M.apply_scr[:], M.apply_scr[:], cf[:, t, 2:3], None, ALU.mult),
                inc="ve")
            p.add("vector", lambda e: e.tensor_scalar(
                M.apply_scr[:], M.apply_scr[:], MAGIC, -MAGIC, ALU.add, ALU.add),
                inc="ve")
            p.add("vector", lambda e: e.tensor_copy(M.scrI[:], M.apply_scr[:]),
                  inc="ve")
            ps_ = i % 2
            gw2 = [p.prep_gate[ps_]] if p.prep_gate[ps_] else []
            vap = p.add("vector", lambda e, ps_=ps_: e.tensor_copy(
                M.prep_bf[:, ps_, :], M.scrI[:]), waits=gw2, inc="ve")
        consume(t, i, ("ve", vap), i % 2)
    return None


def emit_prologue_tile(p, M, t, x_src, mode):
    slot = t % 2
    xsl = M.xq[:, slot, :]
    w = [(f"xf{slot}", p.xq_free.get(f"xf{slot}", 0))]
    if p.xstore[t]:
        w.append(p.xstore[t])
    dxv = p.add("sync", lambda e: e.dma_start(
        out=xsl, in_=x_src[t * 128:(t + 1) * 128, :]), waits=w, inc=f"dx{slot}",
        dma=True)
    gate = p.prep_gate[0]
    gw = [gate] if gate else []
    p.add("vector", lambda e: e.reduce_sum(col(M, 0), xsl, axis=AX.X),
          waits=[(f"dx{slot}", dxv)], inc="ve")
    av = p.add("scalar", lambda e: e.activation(M.sq1024[:], xsl, AF.Square,
                                                accum_out=col(M, 1)),
               waits=[(f"dx{slot}", dxv)] + gw, inc="ae")
    cf = M.coeffs
    p.add("vector", lambda e: e.tensor_scalar(cf[:, t, 0:1], col(M, 0), -1.0 / D,
                                              None, ALU.mult), inc="ve")
    p.add("vector", lambda e: e.tensor_mul(col(M, 2), cf[:, t, 0:1], cf[:, t, 0:1]),
          inc="ve")
    vv = p.add("vector", lambda e: e.tensor_scalar(col(M, 3), col(M, 1), 1.0 / D,
                                                   col(M, 2), ALU.mult, ALU.subtract),
               waits=[("ae", av)], inc="ve")
    if mode == "attn":
        p.add("vector", lambda e: e.tensor_scalar(col(M, 4), col(M, 3), 1e-5, None,
                                                  ALU.add), inc="ve")
        a2 = p.add("scalar", lambda e: e.activation(M.cols2[:, 3:5], M.cols[:, 3:5],
                                                    AF.Sqrt),
                   waits=[("ve", p.c("ve"))], inc="ae")
        p.add("vector", lambda e: e.reciprocal(col(M, 6), col2(M, 4)),
              waits=[("ae", a2)], inc="ve")
        p.add("vector", lambda e: e.tensor_mul(col(M, 7), col(M, 6), col2(M, 3)),
              inc="ve")
        p.add("vector", lambda e: e.tensor_scalar(col(M, 8), col(M, 7), 32.0, 1e-12,
                                                  ALU.mult, ALU.max), inc="ve")
        p.add("vector", lambda e: e.reciprocal(col(M, 9), col(M, 8)), inc="ve")
        p.add("vector", lambda e: e.tensor_mul(col(M, 10), col(M, 6), col(M, 9)),
              inc="ve")
        p.add("vector", lambda e: e.tensor_scalar(cf[:, t, 1:2], col(M, 10), 32.0,
                                                  None, ALU.mult), inc="ve")
        p.add("vector", lambda e: e.tensor_scalar(cf[:, t, 2:3], cf[:, t, 1:2],
                                                  SCALE_ATTN, None, ALU.mult),
              inc="ve")
        vapply = p.add("vector", lambda e: e.tensor_scalar(
            M.prep_bf[:, 0, :], xsl, cf[:, t, 0:1], cf[:, t, 1:2], ALU.add,
            ALU.mult), waits=gw, inc=["ve", f"xf{slot}"])
        p.xq_free[f"xf{slot}"] = p.c(f"xf{slot}")
        return ("ve", vapply)
    if mode == "ln":
        a1 = p.add("scalar", lambda e: e.activation(col2(M, 4), col(M, 3), AF.Sqrt,
                                                    bias=M.epscol[:], scale=1.0),
                   waits=[("ve", vv), ("vc", p.vc)], inc="ae")
        p.add("vector", lambda e: e.reciprocal(cf[:, t, 1:2], col2(M, 4)),
              waits=[("ae", a1)], inc="ve")
        vapply = p.add("vector", lambda e: e.tensor_scalar(
            M.prep_bf[:, 0, :], xsl, cf[:, t, 0:1], cf[:, t, 1:2], ALU.add,
            ALU.mult), waits=gw, inc=["ve", f"xf{slot}"])
        p.xq_free[f"xf{slot}"] = p.c(f"xf{slot}")
        return ("ve", vapply)
    a1 = p.add("scalar", lambda e: e.activation(col2(M, 4), col(M, 3), AF.Sqrt,
                                                bias=M.epscol[:], scale=1.0),
               waits=[("ve", vv)], inc="ae")
    p.add("vector", lambda e: e.reciprocal(cf[:, t, 1:2], col2(M, 4)),
          waits=[("ae", a1)], inc="ve")
    p.add("vector", lambda e: e.tensor_scalar(
        M.scrF[:], xsl, cf[:, t, 0:1], cf[:, t, 1:2], ALU.add, ALU.mult),
        waits=gw, inc=["ve", f"xf{slot}"])
    p.xq_free[f"xf{slot}"] = p.c(f"xf{slot}")
    p.add("vector", lambda e: e.tensor_reduce(col(M, 5), M.scrF[:], AX.X, ALU.max,
                                              apply_absolute_value=True), inc="ve")
    p.add("vector", lambda e: e.tensor_scalar(col(M, 6), col(M, 5), 1e-12,
                                              1.0 / 127.0, ALU.max, ALU.mult),
          inc="ve")
    p.add("vector", lambda e: e.reciprocal(cf[:, t, 2:3], col(M, 6)), inc="ve")
    p.add("vector", lambda e: e.tensor_mul(col(M, 7), cf[:, t, 2:3], cf[:, t, 2:3]),
          inc="ve")
    p.add("vector", lambda e: e.tensor_scalar(cf[:, t, 3:4], col(M, 7), 1e-5,
                                              None, ALU.mult), inc="ve")
    p.add("vector", lambda e: e.tensor_scalar(M.scrF[:], M.scrF[:], cf[:, t, 2:3],
                                              None, ALU.mult), inc="ve")
    p.add("vector", lambda e: e.tensor_scalar(
        M.scrF[:], M.scrF[:], MAGIC, -MAGIC, ALU.add, ALU.add), inc="ve")
    p.add("vector", lambda e: e.tensor_copy(M.scrI[:], M.scrF[:]), inc="ve")
    vapply = p.add("vector", lambda e: e.tensor_copy(M.prep_bf[:, 0, :], M.scrI[:]),
                   inc="ve")
    return ("ve", vapply)


def emit_transpose_prep(p, M, pslot, ready, prep_slot=0):
    w = [ready, ("dc", p.dc)]
    if p.pb_gate[pslot]:
        w.append(p.pb_gate[pslot])
    v = 0
    for j in range(NF):
        v = p.add("tensor", lambda e, j=j: e.transpose(
            M.pb[pslot][:, j, :],
            M.prep_bf[:, prep_slot, j * 128:(j + 1) * 128], M.ident_bf[:]),
            waits=(w if j == 0 else ()), inc=("tp" if j == NF - 1 else None))
    p.prep_gate[prep_slot] = ("tp", v)
    return ("tp", v)


def emit_prep_to_xnT(p, M, t_in_q, xslot, ready_tp):
    pslot = t_in_q % 2
    w = [ready_tp]
    if p.xnT_free[xslot]:
        w.append((f"nf{xslot}", p.xnT_free[xslot]))
    ev = p.add("vector", lambda e: e.tensor_copy(
        M.xnT[:, xslot, :, t_in_q * 128:(t_in_q + 1) * 128], M.pb[pslot][:]),
        waits=w, inc="ve")
    p.pb_gate[pslot] = ("ve", ev)
    return ev


def emit_epilogue_qtile(p, M, io, qt, srcT, src_ready, eps_mode, last_block):
    x_src = io["x_src"]
    x_dst = io["x_dst"]
    for tt in range(4):
        t = qt * 4 + tt
        pslot = tt % 2
        w = [src_ready, ("dc", p.dc)]
        if p.pb_gate[pslot]:
            w.append(p.pb_gate[pslot])
        for oc in range(NF):
            tv = p.add("tensor", lambda e, oc=oc, tt=tt, pslot=pslot: e.transpose(
                M.pb[pslot][:, oc, :], srcT[:, oc, tt * 128:(tt + 1) * 128],
                M.ident_bf[:]),
                waits=(w if oc == 0 else ()), inc=("tp" if oc == NF - 1 else None))
        xslot = t % 2
        xsl = M.xq[:, xslot, :]
        wx = [(f"xf{xslot}", p.xq_free.get(f"xf{xslot}", 0))]
        if p.xstore[t]:
            wx.append(p.xstore[t])
        dxv = p.add("sync", lambda e, t=t, xsl=xsl: e.dma_start(
            out=xsl, in_=x_src[t * 128:(t + 1) * 128, :]), waits=wx,
            inc=f"dx{xslot}", dma=True)
        pbv = M.pb[pslot][:].rearrange("p a b -> p (a b)")
        if EPI_PART < 2:
            addv = p.add("vector", lambda e: _nop(e), waits=[("tp", tv)], inc="ve")
            dov = p.add("sync", lambda e, t=t, xsl=xsl: e.dma_start(
                out=x_dst[t * 128:(t + 1) * 128, :], in_=xsl),
                waits=[("ve", addv), (f"dx{xslot}", dxv)], inc=f"do{xslot}", dma=True)
            p.xstore[t] = (f"do{xslot}", dov)
            p.xq_free[xslot] = p.add("sync", _nop, waits=[(f"do{xslot}", dov)],
                                     inc=f"xf{xslot}")
            p.pb_gate[pslot] = ("ve", addv)
            continue
        rdv = p.add("vector", lambda e, pbv=pbv: e.reduce_sum(col(M, 0), pbv,
                                                              axis=AX.X),
                    waits=[("tp", tv)], inc="ve")
        aw = [("tp", tv), ("ve", rdv)]
        if p.prep_gate[0]:
            aw.append(p.prep_gate[0])
        aq = p.add("scalar", lambda e, pbv=pbv: e.activation(
            M.sq1024[:], pbv, AF.Square, accum_out=col(M, 1)), waits=aw, inc="ae")
        p.prep_gate[0] = ("ae", aq)
        p.add("vector", lambda e: e.tensor_scalar(col(M, 2), col(M, 0), -1.0 / D,
                                                  None, ALU.mult), inc="ve")
        p.add("vector", lambda e: e.tensor_mul(col(M, 3), col(M, 2), col(M, 2)),
              inc="ve")
        vv = p.add("vector", lambda e: e.tensor_scalar(
            col(M, 4), col(M, 1), 1.0 / D, col(M, 3), ALU.mult, ALU.subtract),
            waits=[("ae", aq)], inc="ve")
        if eps_mode == "const":
            p.add("vector", lambda e: e.tensor_scalar(col(M, 4), col(M, 4), 1e-5,
                                                      None, ALU.add), inc="ve")
        else:
            p.add("vector", lambda e, t=t: e.tensor_add(col(M, 4), col(M, 4),
                                                        M.coeffs[:, t, 3:4]),
                  inc="ve")
        a1 = p.add("scalar", lambda e: e.activation(col2(M, 5), col(M, 4), AF.Sqrt),
                   waits=[("ve", p.c("ve"))], inc="ae")
        p.add("vector", lambda e: e.reciprocal(col(M, 6), col2(M, 5)),
              waits=[("ae", a1)], inc="ve")
        if EPI_PART < 3:
            addv = p.add("vector", lambda e: _nop(e), inc="ve")
            p.pb_gate[pslot] = ("ve", addv)
        else:
            p.add("vector", lambda e, pbv=pbv: e.tensor_scalar(
                M.scrF[:], pbv, col(M, 2), col(M, 6), ALU.add, ALU.mult), inc="ve")
            p.pb_gate[pslot] = ("ve", p.c("ve"))
            if EPI_PART < 4:
                addv = p.add("vector", lambda e: _nop(e), inc="ve")
            else:
                addv = p.add("vector", lambda e, xsl=xsl: e.tensor_add(
                    xsl, xsl, M.scrF[:]), waits=[(f"dx{xslot}", dxv)], inc="ve")
        dov = p.add("sync", lambda e, t=t, xsl=xsl: e.dma_start(
            out=x_dst[t * 128:(t + 1) * 128, :], in_=xsl),
            waits=[("ve", addv), (f"dx{xslot}", dxv)], inc=f"do{xslot}", dma=True)
        p.xstore[t] = (f"do{xslot}", dov)
        p.xq_free[f"xf{xslot}"] = p.add("sync", _nop, waits=[(f"do{xslot}", dov)],
                                        inc=f"xf{xslot}")



def emit_attn(p, M, io, layer):
    x_src = io["x_src"]
    wqkvT, woutT, rowrt = io["wqkvT"], io["woutT"], io["rowrt"]
    gc = M.gcol[:, layer, :]

    def prep_w(wb, src_ap, coloff, apply_gamma, gate):
        done = []
        for fi in range(NF):
            slot = fi % 2
            w = [(f"wf{slot}", p.ws_free[slot])]
            if gate:
                w.append(gate)
            dv = p.add("sync", lambda e, fi=fi, slot=slot: e.dma_start(
                out=M.wstage[:, slot, :],
                in_=src_ap[fi * 128:(fi + 1) * 128, coloff:coloff + D]),
                waits=w, inc=f"dw{slot}", dma=True)
            if apply_gamma:
                vv = p.add("vector", lambda e, fi=fi, slot=slot: e.tensor_scalar(
                    wb[:, fi, :], M.wstage[:, slot, :], gc[:, fi:fi + 1], None,
                    ALU.mult), waits=[(f"dw{slot}", dv), ("dc", p.dc)],
                    inc=["ve", f"wf{slot}"])
            else:
                vv = p.add("vector", lambda e, fi=fi, slot=slot: e.tensor_copy(
                    wb[:, fi, :], M.wstage[:, slot, :]),
                    waits=[(f"dw{slot}", dv)], inc=["ve", f"wf{slot}"])
            p.ws_free[slot] = p.c(f"wf{slot}")
            done.append(vv)
        return ("ve", max(done))

    wk_ready = prep_w(M.wbuf1, wqkvT, D, True, None)
    wv_ready = prep_w(M.wbuf2, wqkvT, 2 * D, True, None)
    p.asq_gate = [None] * NF
    ksq_vs = [None] * NF
    last_sqx_mm = 0

    def prologue_C(C):
        xslot = C % 2
        evs = []

        def consume(t, i, rdy, prep_slot, C=C, xslot=xslot, evs=evs):
            tt = t - C * 4
            tp = emit_transpose_prep(p, M, tt % 2, rdy, prep_slot)
            evs.append(emit_prep_to_xnT(p, M, tt, xslot, tp))

        abufs = [(M.xqa2, 0, "xfa0"), (M.xqa2, 1, "xfa1")]
        emit_prologue_group(p, M, [C * 4, C * 4 + 1], x_src, "attn", abufs,
                            consume)
        emit_prologue_group(p, M, [C * 4 + 2, C * 4 + 3], x_src, "attn", abufs,
                            consume)
        return ("ve", max(evs))

    xnT_ready = prologue_C(0)
    for C in range(NQ):
        xslot = C % 2
        for fo in range(NF):
            pfk = M.pf[fo % 2]
            w0 = [xnT_ready, wk_ready]
            if p.pf_gate[fo % 2]:
                w0.append(p.pf_gate[fo % 2])
            for fi in range(NF):
                mv = p.add("tensor", lambda e, fo=fo, fi=fi, pfk=pfk, xslot=xslot:
                           e.matmul(pfk[:], M.wbuf1[:, fi, fo * 128:(fo + 1) * 128],
                                    M.xnT[:, xslot, fi, :],
                                    start=(fi == 0), stop=(fi == NF - 1)),
                           waits=(w0 if fi == 0 else ()),
                           inc=("mm" if fi == NF - 1 else None))
            kev = p.add("scalar", lambda e, fo=fo, C=C, pfk=pfk: e.activation(
                M.kT[:, fo, C * 512:(C + 1) * 512], pfk[:], AF.Copy),
                waits=[("mm", mv)], inc="ae")
            p.pf_gate[fo % 2] = ("ae", kev)
            sqb = M.sqA[:, fo, :]
            sw = [("ae", kev)]
            if p.asq_gate[fo]:
                sw.append(p.asq_gate[fo])
            ksq_vs[fo] = p.add("scalar", lambda e, fo=fo, C=C, sqb=sqb:
                               e.activation(sqb, M.kT[:, fo, C * 512:(C + 1) * 512],
                                            AF.Square), waits=sw, inc="ae")
        kT_ready_C = kev
        vevs = []
        for tt in range(4):
            for half in range(2):
                pfv = M.pf[2 + half]
                w0 = [xnT_ready, wv_ready]
                if p.pO_gate[half]:
                    w0.append(p.pO_gate[half])
                for fi in range(NF):
                    mv = p.add("tensor", lambda e, tt=tt, half=half, fi=fi, pfv=pfv,
                               xslot=xslot: e.matmul(
                                   pfv[:],
                                   M.xnT[:, xslot, fi, tt * 128:(tt + 1) * 128],
                                   M.wbuf2[:, fi, half * 512:(half + 1) * 512],
                                   start=(fi == 0), stop=(fi == NF - 1)),
                               waits=(w0 if fi == 0 else ()),
                               inc=("mm" if fi == NF - 1 else None))
                vev = p.add("vector", lambda e, tt=tt, half=half, C=C, pfv=pfv:
                            e.tensor_copy(
                                M.v_sb[:, C * 4 + tt, half * 512:(half + 1) * 512],
                                pfv[:]), waits=[("mm", mv)], inc="ve")
                p.pO_gate[half] = ("ve", vev)
                vevs.append(vev)
        p.xnT_free[xslot] = p.add("vector", _nop, inc=f"nf{xslot}")
        if C + 1 < NQ:
            xnT_ready = prologue_C(C + 1)
        for h in range(H):
            sqb = M.sqA[:, h, :]
            pfs = M.pf[4 + h % 2]
            w = [("ae", ksq_vs[h]), ("vc", p.vc)]
            pg = p.pf4_gate if h % 2 == 0 else p.pf5_gate
            if pg:
                w.append(pg)
            mv = p.add("tensor", lambda e, sqb=sqb, pfs=pfs: e.matmul(
                pfs[:], M.ones128[:], sqb, start=True, stop=True,
                skip_group_check=True), waits=w, inc="mm")
            p.asq_gate[h] = ("mm", mv)
            last_sqx_mm = mv
            rv = p.add("vector", lambda e, h=h, C=C, pfs=pfs: e.reduce_max(
                M.kmax[:, h * 8 + C:h * 8 + C + 1], pfs[:], axis=AX.X),
                waits=[("mm", mv)], inc="ve")
            if h % 2 == 0:
                p.pf4_gate = ("ve", rv)
            else:
                p.pf5_gate = ("ve", rv)
    for xs in (0, 1):
        p.xq_free[f"xf{xs}"] = p.add("sync", _nop,
                                     waits=[("mm", last_sqx_mm)], inc=f"xf{xs}")
    phaseA_done = ("ve", p.c("ve"))

    wq_ready = prep_w(M.wbuf1, wqkvT, 0, True, phaseA_done)
    wo_ready = prep_w(M.wbuf2, woutT, 0, False, phaseA_done)

    nq = min(NQ, LIMIT_BQ)

    def q_prep(qt):
        evs = []
        for tt in range(4):
            t = qt * 4 + tt
            xslot2 = t % 2
            xsl = M.xq[:, xslot2, :]
            wx = [(f"xf{xslot2}", p.xq_free.get(f"xf{xslot2}", 0))]
            if p.xstore[t]:
                wx.append(p.xstore[t])
            dxv = p.add("sync", lambda e, t=t, xsl=xsl: e.dma_start(
                out=xsl, in_=x_src[t * 128:(t + 1) * 128, :]), waits=wx,
                inc=f"dx{xslot2}", dma=True)
            gw = [(f"dx{xslot2}", dxv)]
            if p.prep_gate[0]:
                gw.append(p.prep_gate[0])
            if p.xnT_free[0] and tt == 0:
                gw.append(("nf0", p.xnT_free[0]))
            vapply = p.add("vector", lambda e, t=t, xsl=xsl: e.tensor_scalar(
                M.prep_bf[:, 0, :], xsl, M.coeffs[:, t, 0:1], M.coeffs[:, t, 2:3],
                ALU.add, ALU.mult), waits=gw, inc=["ve", f"xf{xslot2}"])
            p.xq_free[f"xf{xslot2}"] = p.c(f"xf{xslot2}")
            tp = emit_transpose_prep(p, M, tt % 2, ("ve", vapply))
            evs.append(emit_prep_to_xnT(p, M, tt, 0, tp))
        return ("ve", max(evs))

    def q_mm(qt, xnT_ready):
        for fo in range(H):
            pfk = M.pf[fo % 2]
            w0 = [xnT_ready, wq_ready]
            if p.pf_gate[fo % 2]:
                w0.append(p.pf_gate[fo % 2])
            for fi in range(NF):
                mv = p.add("tensor", lambda e, fo=fo, fi=fi, pfk=pfk: e.matmul(
                    pfk[:], M.wbuf1[:, fi, fo * 128:(fo + 1) * 128],
                    M.xnT[:, 0, fi, :], start=(fi == 0), stop=(fi == NF - 1)),
                    waits=(w0 if fi == 0 else ()),
                    inc=("mm" if fi == NF - 1 else None))
            qev = p.add("scalar", lambda e, fo=fo, pfk=pfk: e.activation(
                M.qT[:, fo, :], pfk[:], AF.Copy), waits=[("mm", mv)], inc="ae")
            p.pf_gate[fo % 2] = ("ae", qev)
        p.xnT_free[0] = p.add("scalar", _nop, inc="nf0")
        return ("ae", qev)

    def bound_prep(qt, qT_ready):
        evs = []
        for h in range(H):
            sw = [qT_ready]
            if p.feat8_gate:
                sw.append(p.feat8_gate)
            sq = p.add("scalar", lambda e, h=h: e.activation(
                M.feat8[:, h, :], M.qT[:, h, :], AF.Square), waits=sw, inc="ae")
            bank = M.pf[4 + (h % 2)]
            w = [("ae", sq), ("vc", p.vc)]
            g = p.pf4_gate if h % 2 == 0 else p.pf5_gate
            if g:
                w.append(g)
            mv = p.add("tensor", lambda e, h=h, bank=bank: e.matmul(
                bank[:], M.ones128[:], M.feat8[:, h, :], start=True, stop=True,
                skip_group_check=True), waits=w, inc="mm")
            qc = M.cols2[:, 8 + h:9 + h]
            rv = p.add("vector", lambda e, qc=qc, bank=bank: e.reduce_max(
                qc, bank[:], axis=AX.X), waits=[("mm", mv)], inc="ve")
            if h % 2 == 0:
                p.pf4_gate = ("ve", rv)
            else:
                p.pf5_gate = ("ve", rv)
            p.add("vector", lambda e, h=h, qt=qt: e.reduce_max(
                col(M, 13), M.kmax[:, h * 8:h * 8 + qt + 1], axis=AX.X),
                waits=[phaseA_done], inc="ve")
            p.add("vector", lambda e, qc=qc: e.tensor_scalar(
                col(M, 11), qc, C1 * 0.5, None, ALU.mult), inc="ve")
            p.add("vector", lambda e: e.tensor_scalar(
                col(M, 12), col(M, 13), 0.5 / C1, None, ALU.mult), inc="ve")
            nm = p.add("vector", lambda e, h=h: e.scalar_tensor_tensor(
                M.negM_col[:, h:h + 1], col(M, 11), -1.0, col(M, 12),
                ALU.mult, ALU.subtract), inc="ve")
            evs.append(nm)
        p.feat8_gate = ("mm", p.c("mm"))
        return evs

    xnT_ready = q_prep(0)
    qT_ready = q_mm(0, xnT_ready)
    negM_evs = bound_prep(0, qT_ready)

    for qt in range(nq):
        cur_qT_ready = qT_ready
        cur_negM = negM_evs
        for h in range(min(H, LIMIT_H)):
            hp = h % 2
            negM_ev = cur_negM[h]
            nkc = 4 * qt + 4
            pO = M.pf[2 + hp]

            def emit_S(kc, h=h, cur_qT_ready=cur_qT_ready):
                ps = M.pf[kc % 2]
                w0 = [cur_qT_ready, phaseA_done]
                if p.pf_gate[kc % 2]:
                    w0.append(p.pf_gate[kc % 2])
                p.add("tensor", lambda e, h=h, kc=kc, ps=ps: e.matmul(
                    ps[:], M.kT[:, h, kc * 128:(kc + 1) * 128], M.qT[:, h, :],
                    start=True, stop=True, skip_group_check=True),
                    waits=w0, inc="mm")
                return p.c("mm")

            sv_next = emit_S(0)
            for kc in range(nkc):
                ps = M.pf[kc % 2]
                sv = sv_next
                ev = p.add("scalar", lambda e, kc=kc, h=h, ps=ps: e.activation(
                    M.probs[:, kc % 2, :], ps[:], AF.Exp,
                    bias=M.negM_col[:, h:h + 1], scale=1.0),
                    waits=[("mm", sv), ("ve", negM_ev)], inc="ae")
                p.pf_gate[kc % 2] = ("ae", ev)
                pready = ("ae", ev)
                if kc >= 4 * qt:
                    base = qt * 512 - kc * 128
                    gv = p.add("gpsimd", lambda e, kc=kc, base=base: e.affine_select(
                        M.probs[:, kc % 2, :], M.probs[:, kc % 2, :], [[1, 512]],
                        ALU.is_ge, 0.0, base=base, channel_multiplier=-1),
                        waits=[pready], inc="ge")
                    pready = ("ge", gv)
                if kc + 1 < nkc:
                    sv_next = emit_S(kc + 1)
                wO = [pready]
                if kc == 0:
                    if p.pO_gate[hp]:
                        wO.append(p.pO_gate[hp])
                p.add("tensor", lambda e, h=h, kc=kc, pO=pO: e.matmul(
                    pO[:], M.v_sb[:, kc, h * 128:(h + 1) * 128], M.probs[:, kc % 2, :],
                    start=(kc == 0), stop=(kc == nkc - 1), skip_group_check=True),
                    waits=wO, inc=("mm" if kc == nkc - 1 else None))
                pR = M.pf[4 + hp]
                rw0 = []
                if kc == 0:
                    g = p.pf4_gate if hp == 0 else p.pf5_gate
                    if g:
                        rw0.append(g)
                    rw0.append(("vc", p.vc))
                rsv = p.add("tensor", lambda e, kc=kc, nkc=nkc, pR=pR: e.matmul(
                    pR[0:1, :], M.ones_col[:], M.probs[:, kc % 2, :],
                    start=(kc == 0), stop=(kc == nkc - 1), skip_group_check=True),
                    waits=rw0, inc="mm")
                p.probs_gate_g[kc % 2] = ("mm", rsv)
            av_done = ("mm", p.c("mm") - 1)
            rs_done = ("mm", p.c("mm"))
            rw = [rs_done]
            if p.rcprow_gate:
                rw.append(p.rcprow_gate)
            pR = M.pf[4 + hp]
            rc = p.add("vector", lambda e, pR=pR: e.reciprocal(M.rcp_row[:], pR[0:1, :]),
                       waits=rw, inc="ve")
            if hp == 0:
                p.pf4_gate = ("ve", rc)
            else:
                p.pf5_gate = ("ve", rc)
            rslot = p.rt_i % 8
            p.rt_i += 1
            d1w = [("ve", rc)]
            if p.rowrt_gate[rslot]:
                d1w.append(p.rowrt_gate[rslot])
            d1 = p.add("sync", lambda e, rslot=rslot: e.dma_start(
                out=rowrt[rslot:rslot + 1, :], in_=M.rcp_row[:]), waits=d1w,
                inc=f"dro{rslot}", dma=True)
            p.rcprow_gate = (f"dro{rslot}", d1)
            d2w = [(f"dro{rslot}", d1)]
            if p.rcpbc_gate[hp]:
                d2w.append(p.rcpbc_gate[hp])
            d2 = p.add("sync", lambda e, rslot=rslot, hp=hp: e.dma_start(
                out=M.rcp_bc[:, hp, :],
                in_=rowrt[rslot:rslot + 1, :].broadcast_to([128, 512])),
                waits=d2w, inc="dri", dma=True)
            p.rowrt_gate[rslot] = ("dri", d2)
            nv_w = [("dri", d2), av_done]
            if p.feat8_gate:
                nv_w.append(p.feat8_gate)
            nv = p.add("vector", lambda e, h=h, pO=pO, hp=hp: e.tensor_mul(
                M.feat8[:, h, :], pO[:], M.rcp_bc[:, hp, :]), waits=nv_w, inc="ve")
            p.rcpbc_gate[hp] = ("ve", nv)
            p.pO_gate[hp] = ("ve", nv)
        p.feat8_gate = None
        attnout_ready = ("ve", p.c("ve"))

        if qt + 1 < nq:
            xnT_ready = q_prep(qt + 1)

        for oc in range(NF):
            pfk = M.pf[oc % 2]
            w0 = [attnout_ready, wo_ready]
            if p.pf_gate[oc % 2]:
                w0.append(p.pf_gate[oc % 2])
            for fi in range(NF):
                mv = p.add("tensor", lambda e, oc=oc, fi=fi, pfk=pfk: e.matmul(
                    pfk[:], M.wbuf2[:, fi, oc * 128:(oc + 1) * 128],
                    M.feat8[:, fi, :], start=(fi == 0), stop=(fi == NF - 1)),
                    waits=(w0 if fi == 0 else ()),
                    inc=("mm" if fi == NF - 1 else None))
            fv = p.add("scalar", lambda e, oc=oc, pfk=pfk: e.activation(
                M.finalT[:, oc, :], pfk[:], AF.Copy), waits=[("mm", mv)], inc="ae")
            p.pf_gate[oc % 2] = ("ae", fv)
        p.feat8_gate = ("mm", p.c("mm"))
        final_ready = ("ae", fv)

        if qt + 1 < nq:
            qT_ready = q_mm(qt + 1, xnT_ready)
            negM_evs = bound_prep(qt + 1, qT_ready)

        if not SKIP_EPI:
            emit_epilogue_qtile(p, M, io, qt, M.finalT, final_ready, "const",
                                io.get("last", False))



def emit_f0(p, M, io):
    rhsT = io["rhsT"]
    for i in range(NF):
        slot = i % 2
        dv = p.add("sync", lambda e, i=i, slot=slot: e.dma_start(
            out=M.wstage[:, slot, :], in_=rhsT[i * 128:(i + 1) * 128, :]),
            waits=[(f"wf{slot}", p.ws_free[slot])], inc=f"dw{slot}", dma=True)
        p.add("vector", lambda e, slot=slot: e.tensor_reduce(
            col2(M, 8), M.wstage[:, slot, :], AX.X, ALU.max,
            apply_absolute_value=True), waits=[(f"dw{slot}", dv)], inc="ve")
        p.add("vector", lambda e, i=i: e.tensor_scalar(
            M.scale_r[:, i:i + 1], col2(M, 8), 1e-12, 1.0 / 127.0, ALU.max,
            ALU.mult), inc="ve")
        p.add("vector", lambda e, i=i: e.reciprocal(col2(M, 9), M.scale_r[:, i:i + 1]),
              inc="ve")
        p.add("vector", lambda e, slot=slot: e.tensor_scalar(
            M.scrF2[:], M.wstage[:, slot, :], col2(M, 9), None, ALU.mult),
            inc=["ve", f"wf{slot}"])
        p.ws_free[slot] = p.c(f"wf{slot}")
        p.add("vector", lambda e: e.tensor_scalar(
            M.scrF2[:], M.scrF2[:], MAGIC, -MAGIC, ALU.add, ALU.add), inc="ve")
        p.add("vector", lambda e: e.tensor_copy(M.scrI2[:], M.scrF2[:]), inc="ve")
        qv = p.add("vector", lambda e: e.tensor_copy(M.qbf[:], M.scrI2[:]), inc="ve")
        pslot = i % 2
        wt = [("ve", qv), ("dc", p.dc)]
        if p.pb_gate[pslot]:
            wt.append(p.pb_gate[pslot])
        for j in range(NF):
            tv = p.add("tensor", lambda e, j=j, pslot=pslot: e.transpose(
                M.pb[pslot][:, j, :], M.qbf[:, j * 128:(j + 1) * 128], M.ident_bf[:]),
                waits=(wt if j == 0 else ()), inc=("tp" if j == NF - 1 else None))
        ev = p.add("vector", lambda e, i=i, pslot=pslot: e.tensor_copy(
            M.q_rhs[:, :, i * 128:(i + 1) * 128], M.pb[pslot][:]),
            waits=[("tp", tv)], inc="ve")
        p.pb_gate[pslot] = ("ve", ev)
    qrhs_ready = ("ve", p.c("ve"))

    def f0_prep(qt):
        evs = []
        xslot = qt % 2

        def consume(t, i, rdy, prep_slot, qt=qt, xslot=xslot, evs=evs):
            tt = t - qt * 4
            tp = emit_transpose_prep(p, M, tt % 2, rdy, prep_slot)
            evs.append(emit_prep_to_xnT(p, M, tt, xslot, tp))

        emit_prologue_group(p, M, [qt * 4 + k for k in range(4)], io["x_src"],
                            "quant",
                            [(M.xq4f0, k, f"xf0{k}") for k in range(4)], consume)
        return ("ve", max(evs))

    xnT_ready = f0_prep(0)
    for qt in range(NQ):
        xslot = qt % 2
        for oc in range(NF):
            pfk = M.pf[oc % 2]
            w0 = [xnT_ready, qrhs_ready]
            if p.pf_gate[oc % 2]:
                w0.append(p.pf_gate[oc % 2])
            for j in range(NF):
                mv = p.add("tensor", lambda e, oc=oc, j=j, pfk=pfk, xslot=xslot:
                           e.matmul(pfk[:], M.q_rhs[:, j, oc * 128:(oc + 1) * 128],
                                    M.xnT[:, xslot, j, :],
                                    start=(j == 0), stop=(j == NF - 1)),
                           waits=(w0 if j == 0 else ()),
                           inc=("mm" if j == NF - 1 else None))
            fw = [("mm", mv), ("dc", p.dc)]
            if p.qT_gate:
                fw.append(p.qT_gate)
            fv = p.add("scalar", lambda e, oc=oc, pfk=pfk: e.activation(
                M.finalT[:, oc, :], pfk[:], AF.Copy,
                scale=M.scale_r[:, oc:oc + 1]), waits=fw, inc="ae")
            p.pf_gate[oc % 2] = ("ae", fv)
        p.qT_gate = None
        final_ready = ("ae", fv)
        p.xnT_free[xslot] = p.add("scalar", _nop, inc=f"nf{xslot}")
        if qt + 1 < NQ:
            xnT_ready = f0_prep(qt + 1)
        emit_epilogue_qtile(p, M, io, qt, M.finalT, final_ready, "quant",
                            io.get("last", False))
        p.qT_gate = ("tp", p.c("tp"))



def emit_f1(p, M, io):
    qwT, scT, laT, lbT = io["qwT"], io["scT"], io["laT"], io["lbT"]
    rowrt = io["rowrt"]

    def h0_consume(t, i, rdy, prep_slot):
        pslot = t % 2
        tp = emit_transpose_prep(p, M, pslot, rdy, prep_slot)
        ev = p.add("vector", lambda e, t=t, pslot=pslot: e.tensor_copy(
            M.hT[:, :, t * 128:(t + 1) * 128], M.pb[pslot][:]), waits=[tp], inc="ve")
        p.pb_gate[pslot] = ("ve", ev)

    h_qt_ready = [None] * NQ

    p.screp_gate = None
    p.lora_scr_gate = None
    p.scrF2_gate = None
    W3 = (M.wbuf1, M.wbuf2, M.wbuf3)
    w_big = {}
    w_lora = {}
    lora_lastuse = [None, None]

    def prep_big(i, gate, js):
        wb = W3[i % 3]
        vv = 0
        for j in js:
            slot = j % 2
            w = [(f"wf{slot}", p.ws_free[slot])]
            if gate and j == js[0]:
                w.append(gate)
            dv = p.add("sync", lambda e, i=i, j=j, slot=slot: e.dma_start(
                out=M.wstage_i[:, slot, :], in_=qwT[i, j * 128:(j + 1) * 128, :]),
                waits=w, inc=f"dw{slot}", dma=True)
            sg = [p.screp_gate] if p.screp_gate else []
            dsv = 0
            for g in range(8):
                dsv = p.add("sync", lambda e, i=i, j=j, g=g: e.dma_start(
                    out=M.screp[16 * g:16 * (g + 1), :],
                    in_=scT[i, 8 * j + g:8 * j + g + 1, :].broadcast_to([16, D])),
                    waits=(sg if g == 0 else ()), inc="dsc", dma=True)
            tw = [(f"dw{slot}", dv)]
            if gate and j == js[0]:
                tw.append(gate)
            p.add("gpsimd", lambda e, slot=slot: e.tensor_scalar(
                M.scrF2[:], M.wstage_i[:, slot, :], 1.0 / 7.5, -1.0, ALU.mult,
                ALU.add), waits=tw, inc=["ge", f"wf{slot}"])
            p.ws_free[slot] = p.c(f"wf{slot}")
            vv = p.add("gpsimd", lambda e, j=j, wb=wb: e.tensor_mul(
                wb[:, j, :], M.scrF2[:], M.screp[:]),
                waits=[("dsc", dsv)], inc="ge")
            p.screp_gate = ("ge", vv)
        return ("ge", vv)

    def prep_lora(i, gate):
        li = i % 2
        lw = [gate] if gate else []
        if p.lora_scr_gate:
            lw.append(p.lora_scr_gate)
        dv = p.add("sync", lambda e, i=i: e.dma_start(
            out=M.lora_laF[:], in_=laT[i].rearrange("(a p) r -> p a r", p=128)),
            waits=lw, inc="dla", dma=True)
        lv = p.add("vector", lambda e, li=li: e.tensor_copy(
            M.la_bf[:, li, :, :], M.lora_laF[:]), waits=[("dla", dv)], inc="ve")
        dv2 = p.add("sync", lambda e, i=i: e.dma_start(
            out=M.lora_lbF[:], in_=lbT[i]),
            waits=([gate] if gate else []), inc="dlb", dma=True)
        lv2 = p.add("vector", lambda e, li=li: e.tensor_copy(
            M.lb_bf[:, li, :], M.lora_lbF[:]), waits=[("dlb", dv2)], inc="ve")
        p.lora_scr_gate = ("ve", lv2)
        return ("ve", lv2)

    for g in range(NT // 4):
        if g == 0:
            w_lora[0] = prep_lora(0, None)
            w_big[0] = prep_big(0, None, list(range(NF)))
        elif g == 2:
            w_big[1] = prep_big(1, None, list(range(NF)))
            w_lora[1] = prep_lora(1, None)
        emit_prologue_group(p, M, [g * 4 + k for k in range(4)], io["x_src"],
                            "ln", [(M.xq4f1, k, f"xf1{k}") for k in range(4)],
                            h0_consume)
    h_qt_ready = [("ve", p.c("ve"))] * NQ

    p.ysb_gate = [None, None, None]

    def y_lora(mi, src3, src_ready, qsl, extra=()):
        li = mi % 2
        lw = [src_ready, w_big[mi], w_lora[mi]] + list(extra)
        if p.pO_gate[li]:
            lw.append(p.pO_gate[li])
        if lora_lastuse[li]:
            lw.append(lora_lastuse[li])
        for j in range(NF):
            mv = p.add("tensor", lambda e, j=j, qsl=qsl, li=li: e.matmul(
                M.pf[2 + li][0:32, :], M.la_bf[:, li, j, :], src3[:, j, qsl],
                start=(j == 0), stop=(j == NF - 1), skip_group_check=True),
                waits=(lw if j == 0 else ()),
                inc=("mm" if j == NF - 1 else None))
        l1v = p.add("vector", lambda e, li=li: e.tensor_copy(
            M.l1_bf[:, li, :], M.pf[2 + li][0:32, :]), waits=[("mm", mv)],
            inc="ve")
        p.pO_gate[li] = ("ve", l1v)
        return l1v

    def y_group(mi, src3, src_ready, qsl, slot, l1v=None):
        li = mi % 2
        wb = W3[mi % 3]
        if l1v is None:
            l1v = y_lora(mi, src3, src_ready, qsl)
        evs = []
        for oc in range(NF):
            pfk = M.pf[oc % 2]
            w0 = [src_ready, w_big[mi], w_lora[mi], ("ve", l1v)]
            if p.pf_gate[oc % 2]:
                w0.append(p.pf_gate[oc % 2])
            for j in range(NF):
                p.add("tensor", lambda e, oc=oc, j=j, pfk=pfk, qsl=qsl, wb=wb:
                      e.matmul(pfk[:], wb[:, j, oc * 128:(oc + 1) * 128],
                               src3[:, j, qsl], start=(j == 0), stop=False,
                               skip_group_check=True),
                      waits=(w0 if j == 0 else ()))
            mv = p.add("tensor", lambda e, oc=oc, pfk=pfk, li=li: e.matmul(
                pfk[:], M.lb_bf[:, li, oc * 128:(oc + 1) * 128],
                M.l1_bf[:, li, :],
                start=False, stop=True, skip_group_check=True), inc="mm")
            ew = [("mm", mv), ("dc", p.dc)]
            if oc == 0 and p.ysb_gate[slot]:
                ew.append(p.ysb_gate[slot])
            yev = p.add("scalar", lambda e, oc=oc, pfk=pfk, mi=mi, slot=slot:
                        e.activation(M.ySB[:, slot, oc, :], pfk[:], AF.Identity,
                                     bias=M.bcol[:, mi, oc:oc + 1]),
                        waits=ew, inc="ae")
            p.pf_gate[oc % 2] = ("ae", yev)
            evs.append(yev)
        return evs, p.c("mm")

    sq_gate = [None] * NF

    def _sqb(oc, s):
        if oc < 6:
            return M.sqF1[:, oc, :]
        if s < 2:
            return M.sqX[:, oc - 6, :]
        return M.sqscr512[:]

    def emit_sq(k, s):
        slot = k % 3
        cur_evs = y1q[k][0]
        hi = NF if s < 2 else 7
        vs = {}
        for oc in range(hi):
            qw = [("ae", cur_evs[oc])]
            if oc == 6 and s == 2:
                if p.sqscr_gate:
                    qw.append(p.sqscr_gate)
            elif sq_gate[oc]:
                qw.append(sq_gate[oc])
            vs[oc] = p.add("scalar", lambda e, oc=oc, slot=slot, s=s:
                           e.activation(_sqb(oc, s), M.ySB[:, slot, oc, :],
                                        AF.Square), waits=qw, inc="ae")
        return vs

    def emit_stats_mm(k, sqvs, s):
        slot = k % 3
        cur_evs = y1q[k][0]
        qmv = 0
        for oc in range(NF):
            sw = [("ae", cur_evs[oc]), ("vc", p.vc)]
            if oc == 0 and p.pf4_gate:
                sw.append(p.pf4_gate)
            p.add("tensor", lambda e, oc=oc, slot=slot: e.matmul(
                M.pf[4][:], M.ones128[:], M.ySB[:, slot, oc, :],
                start=(oc == 0), stop=(oc == NF - 1), skip_group_check=True),
                waits=sw, inc="mm")
            if oc in sqvs:
                sqv = sqvs[oc]
            else:
                sqv = p.add("scalar", lambda e, slot=slot: e.activation(
                    M.sqscr512[:], M.ySB[:, slot, 7, :], AF.Square),
                    waits=[("ae", cur_evs[7]), ("mm", qmv)], inc="ae")
            qw2 = [("ae", sqv), ("vc", p.vc)]
            if oc == 0 and p.pf5_gate:
                qw2.append(p.pf5_gate)
            qmv = p.add("tensor", lambda e, oc=oc, slot=slot, s=s: e.matmul(
                M.pf[5][:], M.ones128[:], _sqb(oc, s),
                start=(oc == 0), stop=(oc == NF - 1), skip_group_check=True),
                waits=qw2, inc="mm")
            if oc >= 6 and s == 2:
                p.sqscr_gate = ("mm", qmv)
            else:
                sq_gate[oc] = ("mm", qmv)
        return qmv

    chain_bv = {}

    def _ab(k):
        return (M.a_bc, M.b_bc) if k % 2 == 0 else (M.a_bc2, M.b_bc2)

    def chain_emit(k, qmv):
        ab, bb = _ab(k)
        h0 = M.h0chain[:]
        rw = [("mm", qmv)]
        if p.abc_gate[k % 2]:
            rw.append(p.abc_gate[k % 2])
        p.add("vector", lambda e, bb=bb: e.tensor_scalar(
            bb[:], M.pf[4][:], -1.0 / D, None, ALU.mult), waits=rw, inc="ve")
        p.pf4_gate = ("ve", p.c("ve"))
        p.add("vector", lambda e, ab=ab: e.tensor_scalar(
            ab[:], M.pf[5][:], 1.0 / D, None, ALU.mult), inc="ve")
        p.pf5_gate = ("ve", p.c("ve"))
        p.add("vector", lambda e, h0=h0, bb=bb: e.tensor_mul(h0, bb[:], bb[:]),
              inc="ve")
        p.add("vector", lambda e, h0=h0, ab=ab: e.tensor_sub(ab[:], ab[:], h0),
              inc="ve")
        vv = p.add("vector", lambda e, ab=ab: e.tensor_scalar(
            ab[:], ab[:], 1e-5, None, ALU.add), inc="ve")
        a1 = p.add("scalar", lambda e, h0=h0, ab=ab: e.activation(
            h0, ab[:], AF.Sqrt), waits=[("ve", vv)], inc="ae")
        p.add("vector", lambda e, h0=h0, ab=ab: e.reciprocal(ab[:], h0),
              waits=[("ae", a1)], inc="ve")
        chain_bv[k] = p.add("vector", lambda e, ab=ab, bb=bb: e.tensor_mul(
            bb[:], bb[:], ab[:]), inc="ve")

    NG = 3 * NQ

    def g_args(g):
        s, qt = g // NQ, g % NQ
        return 2 * s, 2 * s + 1, qt, slice(qt * 512, (qt + 1) * 512), g % 3, s

    def y1_emit(g, l1v=None):
        a, _, qt, qsl, slot, s = g_args(g)
        return y_group(a, M.hT, h_qt_ready[qt], qsl, slot, l1v=l1v)

    prev_y2_last = None
    y1q = {}
    y1q[0] = y1_emit(0)
    y1q[1] = y1_emit(1)
    l1v3 = y_lora(0, M.hT, h_qt_ready[2], slice(1024, 1536),
                  extra=[("mm", y1q[1][1])])
    y1q[2] = y1_emit(2, l1v=l1v3)
    qmv = emit_stats_mm(0, emit_sq(0, 0), 0)
    chain_emit(0, qmv)
    sqx_last = None
    for g in range(NG):
        a, b, qt, qsl, slot, s = g_args(g)
        cur_evs = y1q[g][0]
        ab, bb = _ab(g)
        if qt == 0 and s > 0:
            w_lora[b] = prep_lora(b, prev_y2_last)
        if g == 2 * NQ:
            for xs in (0, 1):
                p.xq_free[f"xf{xs}"] = p.add(
                    "sync", _nop, waits=[sqx_last], inc=f"xf{xs}")
        sqvs_next = emit_sq(g + 1, (g + 1) // NQ) if g + 1 < NG else None
        gelu_vs = []
        for oc in range(NF):
            aslot = oc % 2
            asl = M.apply_scr[:, aslot * 512:(aslot + 1) * 512]
            ub = M.sqjunk[:, aslot * 512:(aslot + 1) * 512]
            pw = [("ve", chain_bv[g]), ("ae", cur_evs[oc])]
            if p.applyscr_gate[aslot]:
                pw.append(p.applyscr_gate[aslot])
            p.add("vector", lambda e, oc=oc, asl=asl, slot=slot, ab=ab:
                  e.tensor_mul(asl, M.ySB[:, slot, oc, :], ab[:]),
                  waits=pw, inc="ve")
            v1 = p.add("vector", lambda e, asl=asl, ub=ub, bb=bb:
                       e.tensor_add(ub, asl, bb[:]), inc="ve")
            gv = p.add("scalar", lambda e, oc=oc, ub=ub, s=s: e.activation(
                M.feat8[:, oc, :], ub, AF.Gelu,
                bias=M.lbcol[:, s, oc:oc + 1], scale=M.lgcol[:, s, oc:oc + 1]),
                waits=[("ve", v1), ("dc", p.dc)], inc="ae")
            p.applyscr_gate[aslot] = ("ae", gv)
            gelu_vs.append(gv)
        p.abc_gate[g % 2] = ("ae", max(gelu_vs))
        p.ysb_gate[slot] = ("ae", max(gelu_vs))
        g_ready = ("ae", max(gelu_vs))
        l1v_next = None
        if g + 3 < NG:
            a3, _, qt3, qsl3, _, _ = g_args(g + 3)
            l1v_next = y_lora(a3, M.hT, h_qt_ready[qt3], qsl3,
                              extra=[("mm", y1q[g + 2][1])])
        y2_evs, y2_last = y_group(b, M.feat8, g_ready, slice(0, 512), slot)
        hv = 0
        for oc in range(NF):
            hv = p.add("vector", lambda e, oc=oc, qsl=qsl, slot=slot:
                       e.tensor_add(M.hT[:, oc, qsl], M.hT[:, oc, qsl],
                                    M.ySB[:, slot, oc, :]),
                       waits=[("ae", y2_evs[oc])], inc="ve")
        h_qt_ready[qt] = ("ve", hv)
        p.ysb_gate[slot] = ("ve", hv)
        if g + 1 < NG:
            qmv = emit_stats_mm(g + 1, sqvs_next, (g + 1) // NQ)
            chain_emit(g + 1, qmv)
            if g + 1 == 2 * NQ:
                sqx_last = ("mm", qmv)
        if s == 2:
            emit_epilogue_qtile(p, M, io, qt,
                                M.hT[:, :, qt * 512:(qt + 1) * 512],
                                h_qt_ready[qt], "const",
                                io.get("last", False))
        if g + 3 < NG:
            y1q[g + 3] = y1_emit(g + 3, l1v=l1v_next)
        if s < 2:
            nx0, nx1 = 2 * s + 2, 2 * s + 3
            g0 = None if s == 0 else prev_y2_last
            if qt == 0:
                prep_big(nx0, g0, [0, 1, 2])
            elif qt == 1:
                prep_big(nx0, None, [3, 4, 5])
            elif qt == 2:
                w_big[nx0] = prep_big(nx0, None, [6, 7])
            elif qt == 4:
                y1_all = max(y1q[gg][1] for gg in range(NQ * s, NQ * s + NQ))
                lora_lastuse[0] = ("mm", y1_all)
                prep_big(nx1, ("mm", y1_all), [0, 1, 2])
                w_lora[nx0] = prep_lora(nx0, ("mm", y1_all))
            elif qt == 5:
                prep_big(nx1, None, [3, 4, 5])
            elif qt == 6:
                w_big[nx1] = prep_big(nx1, None, [6, 7])
        if qt == NQ - 1:
            y1_last = max(y1q[gg][1] for gg in range(NQ * s, NQ * s + NQ))
            prev_y2_last = ("mm", y2_last)
            lora_lastuse = [("mm", y1_all if s < 2 else y1_last),
                            ("mm", y2_last)]



def _new_prog(serial=False):
    nc = bass.Bass("TRN2", target_bir_lowering=False, dynamic_dma_scratch_size=512)
    p = Prog(nc, serial=serial)
    p.probs_gate_g = [None, None]
    M = Mem(nc)
    return nc, p, M


def _finish(p):
    p.add("sync", _nop, waits=[(s, p.c(s)) for s in ("do0", "do1")])
    p.emit()


def build_attn_program(layer, serial=False):
    nc, p, M = _new_prog(serial)
    x = nc.dram_tensor("x", [SEQ, D], F32, kind="ExternalInput").ap()
    out = nc.dram_tensor("out", [SEQ, D], F32, kind="ExternalOutput").ap()
    wqkvT = nc.dram_tensor("wqkvT", [D, 3 * D], F32, kind="ExternalInput").ap()
    woutT = nc.dram_tensor("woutT", [D, D], F32, kind="ExternalInput").ap()
    gcols = nc.dram_tensor("gcols", [128, 2, 8], F32, kind="ExternalInput").ap()
    rowrt = nc.dram_tensor("rowrt", [8, 512], F32, kind="Internal").ap()
    io = {"x_src": x, "x_dst": out, "wqkvT": wqkvT, "woutT": woutT,
          "gcols": gcols, "rowrt": rowrt}
    emit_consts(p, M, io)
    emit_attn(p, M, io, layer)
    _finish(p)
    return nc


def build_f0_program(serial=False):
    nc, p, M = _new_prog(serial)
    x = nc.dram_tensor("x", [SEQ, D], F32, kind="ExternalInput").ap()
    out = nc.dram_tensor("out", [SEQ, D], F32, kind="ExternalOutput").ap()
    rhsT = nc.dram_tensor("rhsT", [D, D], F32, kind="ExternalInput").ap()
    io = {"x_src": x, "x_dst": out, "rhsT": rhsT}
    emit_consts(p, M, io)
    emit_f0(p, M, io)
    _finish(p)
    return nc


def build_f1_program(serial=False):
    nc, p, M = _new_prog(serial)
    x = nc.dram_tensor("x", [SEQ, D], F32, kind="ExternalInput").ap()
    out = nc.dram_tensor("out", [SEQ, D], F32, kind="ExternalOutput").ap()
    qwT = nc.dram_tensor("qwT", [6, D, D], I32, kind="ExternalInput").ap()
    scT = nc.dram_tensor("scT", [6, 64, D], F32, kind="ExternalInput").ap()
    laT = nc.dram_tensor("laT", [6, D, 32], F32, kind="ExternalInput").ap()
    lbT = nc.dram_tensor("lbT", [6, 32, D], F32, kind="ExternalInput").ap()
    lgcols = nc.dram_tensor("lgcols", [128, 3, 8], F32, kind="ExternalInput").ap()
    lbcols = nc.dram_tensor("lbcols", [128, 3, 8], F32, kind="ExternalInput").ap()
    bcols = nc.dram_tensor("bcols", [128, 6, 8], F32, kind="ExternalInput").ap()
    rowrt = nc.dram_tensor("rowrt", [8, 512], F32, kind="Internal").ap()
    io = {"x_src": x, "x_dst": out, "qwT": qwT, "scT": scT, "laT": laT,
          "lbT": lbT, "lgcols": lgcols, "lbcols": lbcols, "bcols": bcols,
          "rowrt": rowrt}
    emit_consts(p, M, io)
    emit_f1(p, M, io)
    _finish(p)
    return nc


def build_fused_program(serial=False):
    nc, p, M = _new_prog(serial)
    x = nc.dram_tensor("x", [SEQ, D], F32, kind="ExternalInput").ap()
    out = nc.dram_tensor("out", [SEQ, D], F32, kind="ExternalOutput").ap()
    xbuf = nc.dram_tensor("xbuf", [SEQ, D], F32, kind="Internal").ap()
    wqkvT0 = nc.dram_tensor("wqkvT0", [D, 3 * D], F32, kind="ExternalInput").ap()
    woutT0 = nc.dram_tensor("woutT0", [D, D], F32, kind="ExternalInput").ap()
    wqkvT1 = nc.dram_tensor("wqkvT1", [D, 3 * D], F32, kind="ExternalInput").ap()
    woutT1 = nc.dram_tensor("woutT1", [D, D], F32, kind="ExternalInput").ap()
    gcols = nc.dram_tensor("gcols", [128, 2, 8], F32, kind="ExternalInput").ap()
    rhsT = nc.dram_tensor("rhsT", [D, D], F32, kind="ExternalInput").ap()
    qwT = nc.dram_tensor("qwT", [6, D, D], I32, kind="ExternalInput").ap()
    scT = nc.dram_tensor("scT", [6, 64, D], F32, kind="ExternalInput").ap()
    laT = nc.dram_tensor("laT", [6, D, 32], F32, kind="ExternalInput").ap()
    lbT = nc.dram_tensor("lbT", [6, 32, D], F32, kind="ExternalInput").ap()
    lgcols = nc.dram_tensor("lgcols", [128, 3, 8], F32, kind="ExternalInput").ap()
    lbcols = nc.dram_tensor("lbcols", [128, 3, 8], F32, kind="ExternalInput").ap()
    bcols = nc.dram_tensor("bcols", [128, 6, 8], F32, kind="ExternalInput").ap()
    rowrt = nc.dram_tensor("rowrt", [8, 512], F32, kind="Internal").ap()
    base = {"gcols": gcols, "lgcols": lgcols, "lbcols": lbcols, "bcols": bcols,
            "rowrt": rowrt}
    emit_consts(p, M, base)
    io = dict(base, x_src=x, x_dst=xbuf, wqkvT=wqkvT0, woutT=woutT0)
    emit_attn(p, M, io, 0)
    p.barrier()
    io = dict(base, x_src=xbuf, x_dst=xbuf, rhsT=rhsT)
    emit_f0(p, M, io)
    p.barrier()
    io = dict(base, x_src=xbuf, x_dst=xbuf, wqkvT=wqkvT1, woutT=woutT1)
    emit_attn(p, M, io, 1)
    p.barrier()
    io = dict(base, x_src=xbuf, x_dst=out, qwT=qwT, scT=scT, laT=laT, lbT=lbT,
              last=True)
    emit_f1(p, M, io)
    _finish(p)
    return nc



def _layout_weights(inputs):
    ct = np.ascontiguousarray
    d = {}
    d["wqkvT0"] = ct(inputs["L0_attn_w_qkv"].T)
    d["woutT0"] = ct(inputs["L0_attn_w_out"].T)
    d["wqkvT1"] = ct(inputs["L1_attn_w_qkv"].T)
    d["woutT1"] = ct(inputs["L1_attn_w_out"].T)
    g0 = ct(inputs["L0_attn_gamma"].reshape(8, 128).T)
    g1 = ct(inputs["L1_attn_gamma"].reshape(8, 128).T)
    d["gcols"] = ct(np.stack([g0, g1], axis=1))
    d["rhsT"] = ct(inputs["L0_ffn_rhs"].T)
    d["qwT"] = ct(inputs["L1_ffn_qweights"].transpose(0, 2, 1))
    d["scT"] = ct(inputs["L1_ffn_scales"].transpose(0, 2, 1))
    d["laT"] = ct(inputs["L1_ffn_lora_a"].transpose(0, 2, 1))
    d["lbT"] = ct(inputs["L1_ffn_lora_b"].transpose(0, 2, 1))
    d["lgcols"] = ct(inputs["L1_ffn_ln_gamma"].reshape(3, 8, 128).transpose(2, 0, 1))
    d["lbcols"] = ct(inputs["L1_ffn_ln_beta"].reshape(3, 8, 128).transpose(2, 0, 1))
    d["bcols"] = ct(inputs["L1_ffn_biases"].reshape(6, 8, 128).transpose(2, 0, 1))
    return d


def kernel(**inputs):
    ins = {k: np.asarray(v) for k, v in inputs.items()}
    w = _layout_weights(ins)
    xs = [np.ascontiguousarray(ins["L0_attn_x"][b]) for b in range(8)]
    cores = list(range(8))
    if FUSED:
        nc = build_fused_program(serial=SERIAL)
        in_maps = [dict(w, x=xs[b]) for b in range(8)]
        res = run_bass_kernel_spmd(nc, in_maps, core_ids=cores)
        return np.stack([res.results[b]["out"] for b in range(8)], axis=0)
    cur = xs
    nc = build_attn_program(0, serial=SERIAL)
    res = run_bass_kernel_spmd(nc, [
        {"x": cur[b], "wqkvT": w["wqkvT0"], "woutT": w["woutT0"],
         "gcols": w["gcols"]} for b in range(8)], core_ids=cores)
    cur = [res.results[b]["out"] for b in range(8)]
    nc = build_f0_program(serial=SERIAL)
    res = run_bass_kernel_spmd(nc, [
        {"x": cur[b], "rhsT": w["rhsT"]} for b in range(8)], core_ids=cores)
    cur = [res.results[b]["out"] for b in range(8)]
    nc = build_attn_program(1, serial=SERIAL)
    res = run_bass_kernel_spmd(nc, [
        {"x": cur[b], "wqkvT": w["wqkvT1"], "woutT": w["woutT1"],
         "gcols": w["gcols"]} for b in range(8)], core_ids=cores)
    cur = [res.results[b]["out"] for b in range(8)]
    nc = build_f1_program(serial=SERIAL)
    res = run_bass_kernel_spmd(nc, [
        {"x": cur[b], "qwT": w["qwT"], "scT": w["scT"], "laT": w["laT"],
         "lbT": w["lbT"], "lgcols": w["lgcols"], "lbcols": w["lbcols"],
         "bcols": w["bcols"]} for b in range(8)], core_ids=cores)
    return np.stack([res.results[b]["out"] for b in range(8)], axis=0)


# version 30 (final)
# speedup vs baseline: 1.1106x; 1.0035x over previous
import numpy as np
import ml_dtypes
import concourse.bass as bass
import concourse.mybir as mybir
from concourse.bass_utils import run_bass_kernel_spmd

F32 = mybir.dt.float32
BF16 = mybir.dt.bfloat16
I32 = mybir.dt.int32
AF = mybir.ActivationFunctionType
ALU = mybir.AluOpType
AX = mybir.AxisListType
BF = ml_dtypes.bfloat16

SEQ = 4096
D = 1024
NT = SEQ // 128
NQ = SEQ // 512
H = 8
DH = 128
NF = D // 128
SCALE_ATTN = float(DH) ** -0.5
C1 = float(np.sqrt(128.0))
MAGIC = 12582912.0

FUSED = True
SERIAL = False
LIMIT_BQ = 8
LIMIT_H = 8
SKIP_EPI = False
EPI_PART = 4


def _nop(e):
    return e.engine_nop() if hasattr(e, "engine_nop") else e.nop()


class Prog:
    ENGINES = ("sync", "vector", "scalar", "tensor", "gpsimd")

    def __init__(self, nc, serial=False):
        self.nc = nc
        self.steps = []
        self.sems = {}
        self.counts = {}
        self.serial = serial
        self.xq_free = {}
        self.xnT_free = [0, 0]
        self.ws_free = [0, 0]
        self.pb_gate = [None, None]
        self.pf_gate = [None, None]
        self.pO_gate = [None, None]
        self.pf4_gate = None
        self.pf5_gate = None
        self.sqscr_gate = None
        self.prep_gate = [None, None]
        self.rcpbc_gate = [None, None]
        self.rcprow_gate = None
        self.rowrt_gate = [None] * 8
        self.rows_gate = []
        self.f0chain_gate = None
        self.escr_gate = [None, None]
        self.negMb_gate = [None] * 8
        self.qT_gate = None
        self.feat8_gate = None
        self.screp_gate = None
        self.applyscr_gate = [None, None]
        self.abc_gate = [None, None]
        self.xstore = [None] * NT
        self.rt_i = 0
        self.dc = 0
        self.vc = 0

    def sem(self, name):
        if name not in self.sems:
            self.sems[name] = self.nc.alloc_semaphore(name)
            self.counts[name] = 0
        return name

    def c(self, name):
        return self.counts.get(name, 0)

    def add(self, eng, fn, waits=(), inc=None, dma=False):
        amt = 16 if dma else 1
        incs = inc if isinstance(inc, (list, tuple)) else ([inc] if inc else [])
        out = []
        for s in incs:
            self.sem(s)
            self.counts[s] += amt
            out.append((s, amt))
        ws = [(s, v) for (s, v) in waits if s is not None and v > 0]
        for s, _ in ws:
            self.sem(s)
        self.steps.append((eng, fn, ws, out[:1], dma))
        for extra in out[1:]:
            self.steps.append((eng, _nop, [], [extra], False))
        return self.counts[incs[0]] if incs else 0

    def barrier(self):
        ws = [(s, self.c(s)) for s in ("do0", "do1") if self.c(s) > 0]
        if not ws:
            return
        for eng in self.ENGINES:
            self.add(eng, _nop, waits=ws)

    def emit(self):
        steps = self.steps
        if self.serial:
            self.sem("__SER")
            new = []
            tot = 0
            run = {}
            for (eng, fn, waits, incs, dma) in steps:
                amt = 16 if dma else 1
                if incs:
                    s0, a0 = incs[0]
                    run[s0] = run.get(s0, 0) + a0
                    new.append((eng, fn, waits + [("__SER", tot)], incs, dma))
                    new.append((eng, _nop, [(s0, run[s0])], [("__SER", amt)], False))
                else:
                    new.append((eng, fn, waits + [("__SER", tot)],
                                [("__SER", amt)], dma))
                tot += amt
            steps = new
        with self.nc.Block() as block:
            for eng_name in self.ENGINES:
                def mk(name):
                    def f(eng):
                        hw = {}
                        own = {}
                        for (e, fn, waits, incs, dma) in steps:
                            if e != name:
                                continue
                            pend = []
                            if name != "tensor":
                                for (s, v) in own.items():
                                    if v > hw.get(s, 0):
                                        pend.append((s, v))
                                        hw[s] = v
                            for (s, v) in waits:
                                if v > hw.get(s, 0):
                                    pend.append((s, v))
                                    hw[s] = v
                            for (s, v) in pend[:-1]:
                                eng.wait_ge(self.sems[s], v)
                            inst = fn(eng)
                            if pend:
                                s, v = pend[-1]
                                inst._wait_ge(self.sems[s], v)
                            for (s, a) in incs:
                                inst.then_inc(self.sems[s], a)
                                if not dma:
                                    own[s] = own.get(s, 0) + a
                    return f
                getattr(block, eng_name)(mk(eng_name))



class Mem:
    def __init__(self, nc):
        self.nc = nc
        self.off = (nc.sbuf_base + 31) & ~31
        self._offsets = {}
        A = self._at

        self.ident_bf = A("ident_bf", [128, 128], BF16)
        self.ones_col = A("ones_col", [128, 1], BF16)
        self.onesK1 = A("onesK1", [1, 128], BF16)
        self.ones128 = A("ones128", [128, 128], BF16)
        self.gcol = A("gcol", [128, 2, 8], F32)
        self.lgcol = A("lgcol", [128, 3, 8], F32)
        self.lbcol = A("lbcol", [128, 3, 8], F32)
        self.bcol = A("bcol", [128, 6, 8], F32)
        self.epscol = A("epscol", [128, 1], F32)
        self.eps11 = A("eps11", [1, 1], F32)
        self.cols = A("cols", [128, 16], F32)
        self.cols2 = A("cols2", [128, 16], F32)
        self.cols3 = A("cols3", [128, 16], F32)
        self.cols4 = A("cols4", [128, 8], F32)
        self.sqjunk = A("sqjunk", [128, 1024], BF16)
        self.negM_col = A("negM_col", [128, 8], F32)
        self.negMb = A("negMb", [1, 8], BF16)
        self.m11 = A("m11", [1, 8], F32)
        self.kmax = A("kmax", [128, 64], F32)
        self.rcp_row = A("rcp_row", [1, 512], F32)
        self.rcp_bc = A("rcp_bc", [128, 2, 512], F32)
        self.probs = A("probs", [128, 2, 512], BF16)
        self.sqscr512 = A("sqscr512", [128, 512], BF16)
        self.scale_r = A("scale_r", [128, 8], F32)

        self.xq = A("xq", [128, 2, 1024], F32)
        self.coeffs = A("coeffs", [128, 32, 4], F32)
        self.prep_bf = A("prep_bf", [128, 2, 1024], BF16)
        self.sq1024 = A("sq1024", [128, 1024], BF16, alias="prep_bf")
        self.scrF = A("scrF", [128, 1024], F32)
        self.xnT = A("xnT", [128, 2, 8, 512], BF16)
        self.wbuf1 = A("wbuf1", [128, 8, 1024], BF16)
        self.wbuf2 = A("wbuf2", [128, 8, 1024], BF16)
        self.wstage = A("wstage", [128, 2, 1024], F32)
        self.wstage_i = A("wstage_i", [128, 2, 1024], I32, alias="wstage")
        self.kT = A("kT", [128, 8, 4096], BF16)
        self.hT = A("hT", [128, 8, 4096], BF16, alias="kT")
        self.xq4f0 = A("xq4f0", [128, 4, 1024], F32, alias="kT")
        self.v_sb = A("v_sb", [128, 32, 1024], BF16)
        self.ySB = A("ySB", [128, 3, 8, 512], BF16, alias="v_sb")
        self.xq4f1 = A("xq4f1", [128, 4, 1024], F32, alias="v_sb")
        self.q_rhs = A("q_rhs", [128, 8, 1024], BF16, alias="v_sb")
        self.a_bc = A("a_bc", [128, 512], F32, alias="v_sb", sub=24576)
        self.b_bc = A("b_bc", [128, 512], F32, alias="v_sb", sub=26624)
        self.screp = A("screp", [128, 1024], F32, alias="v_sb", sub=28672)
        self.scrF2 = A("scrF2", [128, 1024], F32, alias="v_sb", sub=32768)
        self.apply_scr = A("apply_scr", [128, 1024], F32, alias="v_sb", sub=36864)
        self.la_bf = A("la_bf", [128, 2, 8, 32], BF16, alias="v_sb", sub=40960)
        self.lb_bf = A("lb_bf", [32, 2, 1024], BF16, alias="v_sb", sub=41984)
        self.l1_bf = A("l1_bf", [32, 2, 512], BF16, alias="v_sb", sub=46080)
        self.scrI = A("scrI", [128, 1024], I32, alias="v_sb", sub=48128)
        self.qbf = A("qbf", [128, 1024], BF16, alias="v_sb", sub=52224)
        self.scrI2 = A("scrI2", [128, 1024], I32, alias="v_sb", sub=54272)
        self.qT = A("qT", [128, 8, 512], BF16, alias="xnT", sub=8192)
        self.wbuf3 = A("wbuf3", [128, 8, 1024], BF16, alias="xnT")
        self.lora_laF = A("lora_laF", [128, 8, 32], F32, alias="v_sb", sub=54272)
        self.lora_lbF = A("lora_lbF", [32, 1024], F32, alias="v_sb", sub=55296)
        self.sqF1 = A("sqF1", [128, 6, 512], BF16, alias="v_sb", sub=48128)
        self.sqX = A("sqX", [128, 2, 512], BF16, alias="xq")
        self.sqA = A("sqA", [128, 8, 512], BF16, alias="xq")
        self.h0chain = A("h0chain", [128, 512], F32, alias="prep_bf", sub=2048)
        self.a_bc2 = A("a_bc2", [128, 512], F32, alias="v_sb", sub=59392)
        self.b_bc2 = A("b_bc2", [128, 512], F32, alias="v_sb", sub=61440)
        self.finalT = A("finalT", [128, 8, 512], BF16, alias="wstage")
        self.feat8 = A("feat8", [128, 8, 512], BF16)
        self.xqa2 = A("xqa2", [128, 2, 1024], F32, alias="feat8")

        assert self.off <= nc.sbuf_top, f"SBUF over budget: {self.off}"
        self.pf = [nc.alloc_psum_tensor(f"pf{i}", [128, 512], F32) for i in range(6)]
        self.pb = [nc.alloc_psum_tensor(f"pb{i}", [128, 8, 128], BF16) for i in range(2)]

    def _at(self, name, shape, dtype, alias=None, sub=0):
        esize = {F32: 4, BF16: 2, I32: 4}[dtype]
        nbytes = int(np.prod(shape[1:])) * esize
        if alias is not None:
            off = self._offsets[alias] + sub
        else:
            off = self.off
            self.off += (nbytes + 31) & ~31
        t = self.nc.alloc_sbuf_tensor_at(name, shape, dtype, offset=off)
        self._offsets[name] = off
        return t


def col(M, i):
    return M.cols[:, i:i + 1]


def col2(M, i):
    return M.cols2[:, i:i + 1]



def emit_consts(p, M, io):
    nc = p.nc
    idb = nc.inline_tensor(np.eye(128, dtype=BF), "identb_c")
    p.add("sync", lambda e: e.dma_start(out=M.ident_bf[:], in_=idb.ap()),
          inc="dc", dma=True)
    p.add("vector", lambda e: e.memset(M.ones_col[:], 1.0), inc="vc")
    p.add("vector", lambda e: e.memset(M.onesK1[:], 1.0), inc="vc")
    p.add("vector", lambda e: e.memset(M.ones128[:], 1.0), inc="vc")
    p.add("vector", lambda e: e.memset(M.epscol[:], 1e-5), inc="vc")
    p.add("vector", lambda e: e.memset(M.eps11[:], 1e-5), inc="vc")
    for key, dst in (("gcols", M.gcol), ("lgcols", M.lgcol), ("lbcols", M.lbcol),
                     ("bcols", M.bcol)):
        if key in io:
            p.add("sync", lambda e, ap=io[key], dst=dst: e.dma_start(out=dst[:], in_=ap),
                  inc="dc", dma=True)
    p.dc = p.c("dc")
    p.vc = p.c("vc")


def emit_prologue_group(p, M, tiles, x_src, mode, bufs, consume):
    cf = M.coeffs
    n = len(tiles)
    avs = []
    xs = []
    for i, (t, (buf, slot, sem)) in enumerate(zip(tiles, bufs)):
        xsl = buf[:, slot, :]
        xs.append(xsl)
        w = [(sem, p.xq_free.get(sem, 0))]
        if p.xstore[t]:
            w.append(p.xstore[t])
        dxv = p.add("sync", lambda e, t=t, xsl=xsl: e.dma_start(
            out=xsl, in_=x_src[t * 128:(t + 1) * 128, :]), waits=w,
            inc=f"d{sem}", dma=True)
        p.add("vector", lambda e, xsl=xsl, i=i: e.reduce_sum(
            M.cols3[:, i:i + 1], xsl, axis=AX.X),
            waits=[(f"d{sem}", dxv)], inc="ve")
        aw = [(f"d{sem}", dxv)]
        av = p.add("scalar", lambda e, xsl=xsl, i=i: e.activation(
            M.sqjunk[:], xsl, AF.Square, accum_out=M.cols3[:, 4 + i:5 + i]),
            waits=aw, inc="ae")
        avs.append(av)
    for i, t in enumerate(tiles):
        p.add("vector", lambda e, t=t, i=i: e.tensor_scalar(
            cf[:, t, 0:1], M.cols3[:, i:i + 1], -1.0 / D, None, ALU.mult),
            inc="ve")
        p.add("vector", lambda e, t=t, i=i: e.tensor_mul(
            M.cols3[:, 8 + 2 * i:9 + 2 * i], cf[:, t, 0:1], cf[:, t, 0:1]),
            inc="ve")
        p.add("vector", lambda e, i=i: e.tensor_scalar(
            M.cols3[:, 8 + 2 * i:9 + 2 * i], M.cols3[:, 4 + i:5 + i], 1.0 / D,
            M.cols3[:, 8 + 2 * i:9 + 2 * i], ALU.mult, ALU.subtract),
            waits=[("ae", avs[i])], inc="ve")
        vv = p.add("vector", lambda e, i=i: e.tensor_scalar(
            M.cols3[:, 9 + 2 * i:10 + 2 * i], M.cols3[:, 8 + 2 * i:9 + 2 * i],
            1e-5, None, ALU.add), inc="ve")
    sq = p.add("scalar", lambda e, n=n: e.activation(
        M.cols4[:, 0:2 * n], M.cols3[:, 8:8 + 2 * n], AF.Sqrt),
        waits=[("ve", vv)], inc="ae")
    for i, t in enumerate(tiles):
        sv = M.cols4[:, 2 * i:2 * i + 1]
        sve = M.cols4[:, 2 * i + 1:2 * i + 2]
        if mode == "attn":
            p.add("vector", lambda e, sve=sve: e.reciprocal(col(M, 8), sve),
                  waits=[("ae", sq)], inc="ve")
            p.add("vector", lambda e, sv=sv: e.tensor_mul(col(M, 9), col(M, 8), sv),
                  inc="ve")
            p.add("vector", lambda e: e.tensor_scalar(col(M, 9), col(M, 9), 32.0,
                                                      1e-12, ALU.mult, ALU.max),
                  inc="ve")
            p.add("vector", lambda e: e.reciprocal(col(M, 10), col(M, 9)), inc="ve")
            p.add("vector", lambda e: e.tensor_mul(col(M, 10), col(M, 8),
                                                   col(M, 10)), inc="ve")
            p.add("vector", lambda e, t=t: e.tensor_scalar(
                cf[:, t, 1:2], col(M, 10), 32.0, None, ALU.mult), inc="ve")
            p.add("vector", lambda e, t=t: e.tensor_scalar(
                cf[:, t, 2:3], cf[:, t, 1:2], SCALE_ATTN, None, ALU.mult),
                inc="ve")
            ps_ = i % 2
            gw = [p.prep_gate[ps_]] if p.prep_gate[ps_] else []
            vap = p.add("vector", lambda e, t=t, xsl=xs[i], ps_=ps_: e.tensor_scalar(
                M.prep_bf[:, ps_, :], xsl, cf[:, t, 0:1], cf[:, t, 1:2], ALU.add,
                ALU.mult), waits=gw, inc=["ve", bufs[i][2]])
            p.xq_free[bufs[i][2]] = p.c(bufs[i][2])
        elif mode == "ln":
            p.add("vector", lambda e, t=t, sve=sve: e.reciprocal(cf[:, t, 1:2], sve),
                  waits=[("ae", sq)], inc="ve")
            ps_ = i % 2
            gw = [p.prep_gate[ps_]] if p.prep_gate[ps_] else []
            vap = p.add("vector", lambda e, t=t, xsl=xs[i], ps_=ps_: e.tensor_scalar(
                M.prep_bf[:, ps_, :], xsl, cf[:, t, 0:1], cf[:, t, 1:2], ALU.add,
                ALU.mult), waits=gw, inc=["ve", bufs[i][2]])
            p.xq_free[bufs[i][2]] = p.c(bufs[i][2])
        else:
            p.add("vector", lambda e, t=t, sve=sve: e.reciprocal(cf[:, t, 1:2], sve),
                  waits=[("ae", sq)], inc="ve")
            aw2 = [p.f0chain_gate] if p.f0chain_gate else []
            p.add("vector", lambda e, t=t, xsl=xs[i]: e.tensor_scalar(
                M.apply_scr[:], xsl, cf[:, t, 0:1], cf[:, t, 1:2], ALU.add,
                ALU.mult), waits=aw2, inc=["ve", bufs[i][2]])
            p.xq_free[bufs[i][2]] = p.c(bufs[i][2])
            p.add("vector", lambda e: e.tensor_reduce(
                col(M, 8), M.apply_scr[:], AX.X, ALU.max,
                apply_absolute_value=True), inc="ve")
            p.add("vector", lambda e: e.tensor_scalar(col(M, 9), col(M, 8), 1e-12,
                                                      1.0 / 127.0, ALU.max,
                                                      ALU.mult), inc="ve")
            p.add("vector", lambda e, t=t: e.reciprocal(cf[:, t, 2:3], col(M, 9)),
                  inc="ve")
            p.add("vector", lambda e, t=t: e.tensor_mul(col(M, 10), cf[:, t, 2:3],
                                                        cf[:, t, 2:3]), inc="ve")
            p.add("vector", lambda e, t=t: e.tensor_scalar(
                cf[:, t, 3:4], col(M, 10), 1e-5, None, ALU.mult), inc="ve")
            zv = p.add("vector", lambda e, t=t: e.tensor_scalar(
                M.apply_scr[:], M.apply_scr[:], cf[:, t, 2:3], None, ALU.mult),
                inc="ve")
            p.add("vector", lambda e: e.tensor_scalar(
                M.apply_scr[:], M.apply_scr[:], MAGIC, -MAGIC, ALU.add, ALU.add),
                inc="ve")
            p.add("vector", lambda e: e.tensor_copy(M.scrI[:], M.apply_scr[:]),
                  inc="ve")
            ps_ = i % 2
            gw2 = [p.prep_gate[ps_]] if p.prep_gate[ps_] else []
            vap = p.add("vector", lambda e, ps_=ps_: e.tensor_copy(
                M.prep_bf[:, ps_, :], M.scrI[:]), waits=gw2, inc="ve")
        consume(t, i, ("ve", vap), i % 2)
    return None


def emit_prologue_tile(p, M, t, x_src, mode):
    slot = t % 2
    xsl = M.xq[:, slot, :]
    w = [(f"xf{slot}", p.xq_free.get(f"xf{slot}", 0))]
    if p.xstore[t]:
        w.append(p.xstore[t])
    dxv = p.add("sync", lambda e: e.dma_start(
        out=xsl, in_=x_src[t * 128:(t + 1) * 128, :]), waits=w, inc=f"dx{slot}",
        dma=True)
    gate = p.prep_gate[0]
    gw = [gate] if gate else []
    p.add("vector", lambda e: e.reduce_sum(col(M, 0), xsl, axis=AX.X),
          waits=[(f"dx{slot}", dxv)], inc="ve")
    av = p.add("scalar", lambda e: e.activation(M.sq1024[:], xsl, AF.Square,
                                                accum_out=col(M, 1)),
               waits=[(f"dx{slot}", dxv)] + gw, inc="ae")
    cf = M.coeffs
    p.add("vector", lambda e: e.tensor_scalar(cf[:, t, 0:1], col(M, 0), -1.0 / D,
                                              None, ALU.mult), inc="ve")
    p.add("vector", lambda e: e.tensor_mul(col(M, 2), cf[:, t, 0:1], cf[:, t, 0:1]),
          inc="ve")
    vv = p.add("vector", lambda e: e.tensor_scalar(col(M, 3), col(M, 1), 1.0 / D,
                                                   col(M, 2), ALU.mult, ALU.subtract),
               waits=[("ae", av)], inc="ve")
    if mode == "attn":
        p.add("vector", lambda e: e.tensor_scalar(col(M, 4), col(M, 3), 1e-5, None,
                                                  ALU.add), inc="ve")
        a2 = p.add("scalar", lambda e: e.activation(M.cols2[:, 3:5], M.cols[:, 3:5],
                                                    AF.Sqrt),
                   waits=[("ve", p.c("ve"))], inc="ae")
        p.add("vector", lambda e: e.reciprocal(col(M, 6), col2(M, 4)),
              waits=[("ae", a2)], inc="ve")
        p.add("vector", lambda e: e.tensor_mul(col(M, 7), col(M, 6), col2(M, 3)),
              inc="ve")
        p.add("vector", lambda e: e.tensor_scalar(col(M, 8), col(M, 7), 32.0, 1e-12,
                                                  ALU.mult, ALU.max), inc="ve")
        p.add("vector", lambda e: e.reciprocal(col(M, 9), col(M, 8)), inc="ve")
        p.add("vector", lambda e: e.tensor_mul(col(M, 10), col(M, 6), col(M, 9)),
              inc="ve")
        p.add("vector", lambda e: e.tensor_scalar(cf[:, t, 1:2], col(M, 10), 32.0,
                                                  None, ALU.mult), inc="ve")
        p.add("vector", lambda e: e.tensor_scalar(cf[:, t, 2:3], cf[:, t, 1:2],
                                                  SCALE_ATTN, None, ALU.mult),
              inc="ve")
        vapply = p.add("vector", lambda e: e.tensor_scalar(
            M.prep_bf[:, 0, :], xsl, cf[:, t, 0:1], cf[:, t, 1:2], ALU.add,
            ALU.mult), waits=gw, inc=["ve", f"xf{slot}"])
        p.xq_free[f"xf{slot}"] = p.c(f"xf{slot}")
        return ("ve", vapply)
    if mode == "ln":
        a1 = p.add("scalar", lambda e: e.activation(col2(M, 4), col(M, 3), AF.Sqrt,
                                                    bias=M.epscol[:], scale=1.0),
                   waits=[("ve", vv), ("vc", p.vc)], inc="ae")
        p.add("vector", lambda e: e.reciprocal(cf[:, t, 1:2], col2(M, 4)),
              waits=[("ae", a1)], inc="ve")
        vapply = p.add("vector", lambda e: e.tensor_scalar(
            M.prep_bf[:, 0, :], xsl, cf[:, t, 0:1], cf[:, t, 1:2], ALU.add,
            ALU.mult), waits=gw, inc=["ve", f"xf{slot}"])
        p.xq_free[f"xf{slot}"] = p.c(f"xf{slot}")
        return ("ve", vapply)
    a1 = p.add("scalar", lambda e: e.activation(col2(M, 4), col(M, 3), AF.Sqrt,
                                                bias=M.epscol[:], scale=1.0),
               waits=[("ve", vv)], inc="ae")
    p.add("vector", lambda e: e.reciprocal(cf[:, t, 1:2], col2(M, 4)),
          waits=[("ae", a1)], inc="ve")
    p.add("vector", lambda e: e.tensor_scalar(
        M.scrF[:], xsl, cf[:, t, 0:1], cf[:, t, 1:2], ALU.add, ALU.mult),
        waits=gw, inc=["ve", f"xf{slot}"])
    p.xq_free[f"xf{slot}"] = p.c(f"xf{slot}")
    p.add("vector", lambda e: e.tensor_reduce(col(M, 5), M.scrF[:], AX.X, ALU.max,
                                              apply_absolute_value=True), inc="ve")
    p.add("vector", lambda e: e.tensor_scalar(col(M, 6), col(M, 5), 1e-12,
                                              1.0 / 127.0, ALU.max, ALU.mult),
          inc="ve")
    p.add("vector", lambda e: e.reciprocal(cf[:, t, 2:3], col(M, 6)), inc="ve")
    p.add("vector", lambda e: e.tensor_mul(col(M, 7), cf[:, t, 2:3], cf[:, t, 2:3]),
          inc="ve")
    p.add("vector", lambda e: e.tensor_scalar(cf[:, t, 3:4], col(M, 7), 1e-5,
                                              None, ALU.mult), inc="ve")
    p.add("vector", lambda e: e.tensor_scalar(M.scrF[:], M.scrF[:], cf[:, t, 2:3],
                                              None, ALU.mult), inc="ve")
    p.add("vector", lambda e: e.tensor_scalar(
        M.scrF[:], M.scrF[:], MAGIC, -MAGIC, ALU.add, ALU.add), inc="ve")
    p.add("vector", lambda e: e.tensor_copy(M.scrI[:], M.scrF[:]), inc="ve")
    vapply = p.add("vector", lambda e: e.tensor_copy(M.prep_bf[:, 0, :], M.scrI[:]),
                   inc="ve")
    return ("ve", vapply)


def emit_transpose_prep(p, M, pslot, ready, prep_slot=0):
    w = [ready, ("dc", p.dc)]
    if p.pb_gate[pslot]:
        w.append(p.pb_gate[pslot])
    v = 0
    for j in range(NF):
        v = p.add("tensor", lambda e, j=j: e.transpose(
            M.pb[pslot][:, j, :],
            M.prep_bf[:, prep_slot, j * 128:(j + 1) * 128], M.ident_bf[:]),
            waits=(w if j == 0 else ()), inc=("tp" if j == NF - 1 else None))
    p.prep_gate[prep_slot] = ("tp", v)
    return ("tp", v)


def emit_prep_to_xnT(p, M, t_in_q, xslot, ready_tp):
    pslot = t_in_q % 2
    w = [ready_tp]
    if p.xnT_free[xslot]:
        w.append((f"nf{xslot}", p.xnT_free[xslot]))
    ev = p.add("vector", lambda e: e.tensor_copy(
        M.xnT[:, xslot, :, t_in_q * 128:(t_in_q + 1) * 128], M.pb[pslot][:]),
        waits=w, inc="ve")
    p.pb_gate[pslot] = ("ve", ev)
    return ev


def emit_epilogue_qtile(p, M, io, qt, srcT, src_ready, eps_mode, last_block):
    x_src = io["x_src"]
    x_dst = io["x_dst"]
    for tt in range(4):
        t = qt * 4 + tt
        pslot = tt % 2
        w = [src_ready, ("dc", p.dc)]
        if p.pb_gate[pslot]:
            w.append(p.pb_gate[pslot])
        for oc in range(NF):
            tv = p.add("tensor", lambda e, oc=oc, tt=tt, pslot=pslot: e.transpose(
                M.pb[pslot][:, oc, :], srcT[:, oc, tt * 128:(tt + 1) * 128],
                M.ident_bf[:]),
                waits=(w if oc == 0 else ()), inc=("tp" if oc == NF - 1 else None))
        xslot = t % 2
        xsl = M.xq[:, xslot, :]
        wx = [(f"xf{xslot}", p.xq_free.get(f"xf{xslot}", 0))]
        if p.xstore[t]:
            wx.append(p.xstore[t])
        dxv = p.add("sync", lambda e, t=t, xsl=xsl: e.dma_start(
            out=xsl, in_=x_src[t * 128:(t + 1) * 128, :]), waits=wx,
            inc=f"dx{xslot}", dma=True)
        pbv = M.pb[pslot][:].rearrange("p a b -> p (a b)")
        if EPI_PART < 2:
            addv = p.add("vector", lambda e: _nop(e), waits=[("tp", tv)], inc="ve")
            dov = p.add("sync", lambda e, t=t, xsl=xsl: e.dma_start(
                out=x_dst[t * 128:(t + 1) * 128, :], in_=xsl),
                waits=[("ve", addv), (f"dx{xslot}", dxv)], inc=f"do{xslot}", dma=True)
            p.xstore[t] = (f"do{xslot}", dov)
            p.xq_free[xslot] = p.add("sync", _nop, waits=[(f"do{xslot}", dov)],
                                     inc=f"xf{xslot}")
            p.pb_gate[pslot] = ("ve", addv)
            continue
        rdv = p.add("vector", lambda e, pbv=pbv: e.reduce_sum(col(M, 0), pbv,
                                                              axis=AX.X),
                    waits=[("tp", tv)], inc="ve")
        aw = [("tp", tv), ("ve", rdv)]
        if p.prep_gate[0]:
            aw.append(p.prep_gate[0])
        aq = p.add("scalar", lambda e, pbv=pbv: e.activation(
            M.sq1024[:], pbv, AF.Square, accum_out=col(M, 1)), waits=aw, inc="ae")
        p.prep_gate[0] = ("ae", aq)
        p.add("vector", lambda e: e.tensor_scalar(col(M, 2), col(M, 0), -1.0 / D,
                                                  None, ALU.mult), inc="ve")
        p.add("vector", lambda e: e.tensor_mul(col(M, 3), col(M, 2), col(M, 2)),
              inc="ve")
        vv = p.add("vector", lambda e: e.tensor_scalar(
            col(M, 4), col(M, 1), 1.0 / D, col(M, 3), ALU.mult, ALU.subtract),
            waits=[("ae", aq)], inc="ve")
        if eps_mode == "const":
            p.add("vector", lambda e: e.tensor_scalar(col(M, 4), col(M, 4), 1e-5,
                                                      None, ALU.add), inc="ve")
        else:
            p.add("vector", lambda e, t=t: e.tensor_add(col(M, 4), col(M, 4),
                                                        M.coeffs[:, t, 3:4]),
                  inc="ve")
        a1 = p.add("scalar", lambda e: e.activation(col2(M, 5), col(M, 4), AF.Sqrt),
                   waits=[("ve", p.c("ve"))], inc="ae")
        p.add("vector", lambda e: e.reciprocal(col(M, 6), col2(M, 5)),
              waits=[("ae", a1)], inc="ve")
        aeng = "ve"
        if EPI_PART < 3:
            addv = p.add("vector", lambda e: _nop(e), inc="ve")
            p.pb_gate[pslot] = ("ve", addv)
        elif last_block:
            B = (M.scrF, M.screp)[tt % 2]
            gw = [p.escr_gate[tt % 2]] if p.escr_gate[tt % 2] else []
            tsv = p.add("vector", lambda e, pbv=pbv, B=B: e.tensor_scalar(
                B[:], pbv, col(M, 2), col(M, 6), ALU.add, ALU.mult),
                waits=gw, inc="ve")
            p.pb_gate[pslot] = ("ve", tsv)
            addv = p.add("gpsimd", lambda e, xsl=xsl, B=B: e.tensor_add(
                xsl, xsl, B[:]),
                waits=[(f"dx{xslot}", dxv), ("ve", tsv)], inc="ge")
            p.escr_gate[tt % 2] = ("ge", addv)
            aeng = "ge"
        else:
            p.add("vector", lambda e, pbv=pbv: e.tensor_scalar(
                M.scrF[:], pbv, col(M, 2), col(M, 6), ALU.add, ALU.mult), inc="ve")
            p.pb_gate[pslot] = ("ve", p.c("ve"))
            if EPI_PART < 4:
                addv = p.add("vector", lambda e: _nop(e), inc="ve")
            else:
                addv = p.add("vector", lambda e, xsl=xsl: e.tensor_add(
                    xsl, xsl, M.scrF[:]), waits=[(f"dx{xslot}", dxv)], inc="ve")
        dov = p.add("sync", lambda e, t=t, xsl=xsl: e.dma_start(
            out=x_dst[t * 128:(t + 1) * 128, :], in_=xsl),
            waits=[(aeng, addv), (f"dx{xslot}", dxv)], inc=f"do{xslot}", dma=True)
        p.xstore[t] = (f"do{xslot}", dov)
        p.xq_free[f"xf{xslot}"] = p.add("sync", _nop, waits=[(f"do{xslot}", dov)],
                                        inc=f"xf{xslot}")



def emit_attn(p, M, io, layer):
    x_src = io["x_src"]
    wqkvT, woutT, rowrt = io["wqkvT"], io["woutT"], io["rowrt"]
    gc = M.gcol[:, layer, :]

    def prep_w(wb, src_ap, coloff, apply_gamma, gate):
        done = []
        for fi in range(NF):
            slot = fi % 2
            w = [(f"wf{slot}", p.ws_free[slot])]
            if gate:
                w.append(gate)
            dv = p.add("sync", lambda e, fi=fi, slot=slot: e.dma_start(
                out=M.wstage[:, slot, :],
                in_=src_ap[fi * 128:(fi + 1) * 128, coloff:coloff + D]),
                waits=w, inc=f"dw{slot}", dma=True)
            if apply_gamma:
                vv = p.add("vector", lambda e, fi=fi, slot=slot: e.tensor_scalar(
                    wb[:, fi, :], M.wstage[:, slot, :], gc[:, fi:fi + 1], None,
                    ALU.mult), waits=[(f"dw{slot}", dv), ("dc", p.dc)],
                    inc=["ve", f"wf{slot}"])
            else:
                vv = p.add("vector", lambda e, fi=fi, slot=slot: e.tensor_copy(
                    wb[:, fi, :], M.wstage[:, slot, :]),
                    waits=[(f"dw{slot}", dv)], inc=["ve", f"wf{slot}"])
            p.ws_free[slot] = p.c(f"wf{slot}")
            done.append(vv)
        return ("ve", max(done))

    wk_ready = prep_w(M.wbuf1, wqkvT, D, True, None)
    wv_ready = prep_w(M.wbuf2, wqkvT, 2 * D, True, None)
    p.asq_gate = [None] * NF
    ksq_vs = [None] * NF
    last_sqx_mm = 0

    for C in range(NQ):
        xslot = C % 2
        evs = []

        def consume(t, i, rdy, prep_slot, C=C, xslot=xslot, evs=evs):
            tt = t - C * 4
            tp = emit_transpose_prep(p, M, tt % 2, rdy, prep_slot)
            evs.append(emit_prep_to_xnT(p, M, tt, xslot, tp))

        abufs = [(M.xqa2, 0, "xfa0"), (M.xqa2, 1, "xfa1")]
        emit_prologue_group(p, M, [C * 4, C * 4 + 1], x_src, "attn", abufs, consume)
        emit_prologue_group(p, M, [C * 4 + 2, C * 4 + 3], x_src, "attn", abufs,
                            consume)
        xnT_ready = ("ve", max(evs))
        for fo in range(NF):
            pfk = M.pf[fo % 2]
            w0 = [xnT_ready, wk_ready]
            if p.pf_gate[fo % 2]:
                w0.append(p.pf_gate[fo % 2])
            for fi in range(NF):
                mv = p.add("tensor", lambda e, fo=fo, fi=fi, pfk=pfk, xslot=xslot:
                           e.matmul(pfk[:], M.wbuf1[:, fi, fo * 128:(fo + 1) * 128],
                                    M.xnT[:, xslot, fi, :],
                                    start=(fi == 0), stop=(fi == NF - 1)),
                           waits=(w0 if fi == 0 else ()),
                           inc=("mm" if fi == NF - 1 else None))
            kev = p.add("scalar", lambda e, fo=fo, C=C, pfk=pfk: e.activation(
                M.kT[:, fo, C * 512:(C + 1) * 512], pfk[:], AF.Copy),
                waits=[("mm", mv)], inc="ae")
            p.pf_gate[fo % 2] = ("ae", kev)
            sqb = M.sqA[:, fo, :]
            sw = [("ae", kev)]
            if p.asq_gate[fo]:
                sw.append(p.asq_gate[fo])
            ksq_vs[fo] = p.add("scalar", lambda e, fo=fo, C=C, sqb=sqb:
                               e.activation(sqb, M.kT[:, fo, C * 512:(C + 1) * 512],
                                            AF.Square), waits=sw, inc="ae")
        kT_ready_C = kev
        vevs = []
        for tt in range(4):
            for half in range(2):
                pfv = M.pf[2 + half]
                w0 = [xnT_ready, wv_ready]
                if p.pO_gate[half]:
                    w0.append(p.pO_gate[half])
                for fi in range(NF):
                    mv = p.add("tensor", lambda e, tt=tt, half=half, fi=fi, pfv=pfv,
                               xslot=xslot: e.matmul(
                                   pfv[:],
                                   M.xnT[:, xslot, fi, tt * 128:(tt + 1) * 128],
                                   M.wbuf2[:, fi, half * 512:(half + 1) * 512],
                                   start=(fi == 0), stop=(fi == NF - 1)),
                               waits=(w0 if fi == 0 else ()),
                               inc=("mm" if fi == NF - 1 else None))
                vev = p.add("vector", lambda e, tt=tt, half=half, C=C, pfv=pfv:
                            e.tensor_copy(
                                M.v_sb[:, C * 4 + tt, half * 512:(half + 1) * 512],
                                pfv[:]), waits=[("mm", mv)], inc="ve")
                p.pO_gate[half] = ("ve", vev)
                vevs.append(vev)
        p.xnT_free[xslot] = p.add("vector", _nop, inc=f"nf{xslot}")
        for h in range(H):
            sqb = M.sqA[:, h, :]
            pfs = M.pf[4 + h % 2]
            w = [("ae", ksq_vs[h]), ("vc", p.vc)]
            pg = p.pf4_gate if h % 2 == 0 else p.pf5_gate
            if pg:
                w.append(pg)
            mv = p.add("tensor", lambda e, sqb=sqb, pfs=pfs: e.matmul(
                pfs[:], M.ones128[:], sqb, start=True, stop=True,
                skip_group_check=True), waits=w, inc="mm")
            p.asq_gate[h] = ("mm", mv)
            last_sqx_mm = mv
            rv = p.add("vector", lambda e, h=h, C=C, pfs=pfs: e.reduce_max(
                M.kmax[:, h * 8 + C:h * 8 + C + 1], pfs[:], axis=AX.X),
                waits=[("mm", mv)], inc="ve")
            if h % 2 == 0:
                p.pf4_gate = ("ve", rv)
            else:
                p.pf5_gate = ("ve", rv)
    for xs in (0, 1):
        p.xq_free[f"xf{xs}"] = p.add("sync", _nop,
                                     waits=[("mm", last_sqx_mm)], inc=f"xf{xs}")
    phaseA_done = ("ve", p.c("ve"))

    wq_ready = prep_w(M.wbuf1, wqkvT, 0, True, phaseA_done)
    wo_ready = prep_w(M.wbuf2, woutT, 0, False, phaseA_done)

    nq = min(NQ, LIMIT_BQ)

    def q_prep(qt):
        evs = []
        for tt in range(4):
            t = qt * 4 + tt
            xslot2 = t % 2
            xsl = M.xq[:, xslot2, :]
            wx = [(f"xf{xslot2}", p.xq_free.get(f"xf{xslot2}", 0))]
            if p.xstore[t]:
                wx.append(p.xstore[t])
            dxv = p.add("sync", lambda e, t=t, xsl=xsl: e.dma_start(
                out=xsl, in_=x_src[t * 128:(t + 1) * 128, :]), waits=wx,
                inc=f"dx{xslot2}", dma=True)
            gw = [(f"dx{xslot2}", dxv)]
            if p.prep_gate[0]:
                gw.append(p.prep_gate[0])
            if p.xnT_free[0] and tt == 0:
                gw.append(("nf0", p.xnT_free[0]))
            vapply = p.add("vector", lambda e, t=t, xsl=xsl: e.tensor_scalar(
                M.prep_bf[:, 0, :], xsl, M.coeffs[:, t, 0:1], M.coeffs[:, t, 2:3],
                ALU.add, ALU.mult), waits=gw, inc=["ve", f"xf{xslot2}"])
            p.xq_free[f"xf{xslot2}"] = p.c(f"xf{xslot2}")
            tp = emit_transpose_prep(p, M, tt % 2, ("ve", vapply))
            evs.append(emit_prep_to_xnT(p, M, tt, 0, tp))
        return ("ve", max(evs))

    def q_mm(qt, xnT_ready):
        for fo in range(H):
            pfk = M.pf[fo % 2]
            w0 = [xnT_ready, wq_ready]
            if p.pf_gate[fo % 2]:
                w0.append(p.pf_gate[fo % 2])
            for fi in range(NF):
                mv = p.add("tensor", lambda e, fo=fo, fi=fi, pfk=pfk: e.matmul(
                    pfk[:], M.wbuf1[:, fi, fo * 128:(fo + 1) * 128],
                    M.xnT[:, 0, fi, :], start=(fi == 0), stop=(fi == NF - 1)),
                    waits=(w0 if fi == 0 else ()),
                    inc=("mm" if fi == NF - 1 else None))
            qev = p.add("scalar", lambda e, fo=fo, pfk=pfk: e.activation(
                M.qT[:, fo, :], pfk[:], AF.Copy), waits=[("mm", mv)], inc="ae")
            p.pf_gate[fo % 2] = ("ae", qev)
        p.xnT_free[0] = p.add("scalar", _nop, inc="nf0")
        return ("ae", qev)

    def bound_prep(qt, qT_ready):
        evs = []
        for h in range(H):
            sw = [qT_ready]
            if p.feat8_gate:
                sw.append(p.feat8_gate)
            sq = p.add("scalar", lambda e, h=h: e.activation(
                M.feat8[:, h, :], M.qT[:, h, :], AF.Square), waits=sw, inc="ae")
            bank = M.pf[4 + (h % 2)]
            w = [("ae", sq), ("vc", p.vc)]
            g = p.pf4_gate if h % 2 == 0 else p.pf5_gate
            if g:
                w.append(g)
            mv = p.add("tensor", lambda e, h=h, bank=bank: e.matmul(
                bank[:], M.ones128[:], M.feat8[:, h, :], start=True, stop=True,
                skip_group_check=True), waits=w, inc="mm")
            qc = M.cols2[:, 8 + h:9 + h]
            rv = p.add("vector", lambda e, qc=qc, bank=bank: e.reduce_max(
                qc, bank[:], axis=AX.X), waits=[("mm", mv)], inc="ve")
            if h % 2 == 0:
                p.pf4_gate = ("ve", rv)
            else:
                p.pf5_gate = ("ve", rv)
            p.add("vector", lambda e, h=h, qt=qt: e.reduce_max(
                col(M, 13), M.kmax[:, h * 8:h * 8 + qt + 1], axis=AX.X),
                waits=[phaseA_done], inc="ve")
            p.add("vector", lambda e, qc=qc: e.tensor_scalar(
                col(M, 11), qc, C1 * 0.5, None, ALU.mult), inc="ve")
            p.add("vector", lambda e: e.tensor_scalar(
                col(M, 12), col(M, 13), 0.5 / C1, None, ALU.mult), inc="ve")
            nm = p.add("vector", lambda e, h=h: e.scalar_tensor_tensor(
                M.negM_col[:, h:h + 1], col(M, 11), -1.0, col(M, 12),
                ALU.mult, ALU.subtract), inc="ve")
            evs.append(nm)
        p.feat8_gate = ("mm", p.c("mm"))
        return evs

    xnT_ready = q_prep(0)
    qT_ready = q_mm(0, xnT_ready)
    negM_evs = bound_prep(0, qT_ready)

    for qt in range(nq):
        cur_qT_ready = qT_ready
        cur_negM = negM_evs
        for h in range(min(H, LIMIT_H)):
            hp = h % 2
            negM_ev = cur_negM[h]
            nkc = 4 * qt + 4
            pO = M.pf[2 + hp]

            def emit_S(kc, h=h, cur_qT_ready=cur_qT_ready):
                ps = M.pf[kc % 2]
                w0 = [cur_qT_ready, phaseA_done]
                if p.pf_gate[kc % 2]:
                    w0.append(p.pf_gate[kc % 2])
                p.add("tensor", lambda e, h=h, kc=kc, ps=ps: e.matmul(
                    ps[:], M.kT[:, h, kc * 128:(kc + 1) * 128], M.qT[:, h, :],
                    start=True, stop=True, skip_group_check=True),
                    waits=w0, inc="mm")
                return p.c("mm")

            sv_next = emit_S(0)
            for kc in range(nkc):
                ps = M.pf[kc % 2]
                sv = sv_next
                ev = p.add("scalar", lambda e, kc=kc, h=h, ps=ps: e.activation(
                    M.probs[:, kc % 2, :], ps[:], AF.Exp,
                    bias=M.negM_col[:, h:h + 1], scale=1.0),
                    waits=[("mm", sv), ("ve", negM_ev)], inc="ae")
                p.pf_gate[kc % 2] = ("ae", ev)
                pready = ("ae", ev)
                if kc >= 4 * qt:
                    base = qt * 512 - kc * 128
                    gv = p.add("gpsimd", lambda e, kc=kc, base=base: e.affine_select(
                        M.probs[:, kc % 2, :], M.probs[:, kc % 2, :], [[1, 512]],
                        ALU.is_ge, 0.0, base=base, channel_multiplier=-1),
                        waits=[pready], inc="ge")
                    pready = ("ge", gv)
                if kc + 1 < nkc:
                    sv_next = emit_S(kc + 1)
                wO = [pready]
                if kc == 0:
                    if p.pO_gate[hp]:
                        wO.append(p.pO_gate[hp])
                p.add("tensor", lambda e, h=h, kc=kc, pO=pO: e.matmul(
                    pO[:], M.v_sb[:, kc, h * 128:(h + 1) * 128], M.probs[:, kc % 2, :],
                    start=(kc == 0), stop=(kc == nkc - 1), skip_group_check=True),
                    waits=wO, inc=("mm" if kc == nkc - 1 else None))
                pR = M.pf[4 + hp]
                rw0 = []
                if kc == 0:
                    g = p.pf4_gate if hp == 0 else p.pf5_gate
                    if g:
                        rw0.append(g)
                    rw0.append(("vc", p.vc))
                rsv = p.add("tensor", lambda e, kc=kc, nkc=nkc, pR=pR: e.matmul(
                    pR[0:1, :], M.ones_col[:], M.probs[:, kc % 2, :],
                    start=(kc == 0), stop=(kc == nkc - 1), skip_group_check=True),
                    waits=rw0, inc="mm")
                p.probs_gate_g[kc % 2] = ("mm", rsv)
            av_done = ("mm", p.c("mm") - 1)
            rs_done = ("mm", p.c("mm"))
            rw = [rs_done]
            if p.rcprow_gate:
                rw.append(p.rcprow_gate)
            pR = M.pf[4 + hp]
            rc = p.add("vector", lambda e, pR=pR: e.reciprocal(M.rcp_row[:], pR[0:1, :]),
                       waits=rw, inc="ve")
            if hp == 0:
                p.pf4_gate = ("ve", rc)
            else:
                p.pf5_gate = ("ve", rc)
            rslot = p.rt_i % 8
            p.rt_i += 1
            d1w = [("ve", rc)]
            if p.rowrt_gate[rslot]:
                d1w.append(p.rowrt_gate[rslot])
            d1 = p.add("sync", lambda e, rslot=rslot: e.dma_start(
                out=rowrt[rslot:rslot + 1, :], in_=M.rcp_row[:]), waits=d1w,
                inc=f"dro{rslot}", dma=True)
            p.rcprow_gate = (f"dro{rslot}", d1)
            d2w = [(f"dro{rslot}", d1)]
            if p.rcpbc_gate[hp]:
                d2w.append(p.rcpbc_gate[hp])
            d2 = p.add("sync", lambda e, rslot=rslot, hp=hp: e.dma_start(
                out=M.rcp_bc[:, hp, :],
                in_=rowrt[rslot:rslot + 1, :].broadcast_to([128, 512])),
                waits=d2w, inc="dri", dma=True)
            p.rowrt_gate[rslot] = ("dri", d2)
            nv_w = [("dri", d2), av_done]
            if p.feat8_gate:
                nv_w.append(p.feat8_gate)
            nv = p.add("vector", lambda e, h=h, pO=pO, hp=hp: e.tensor_mul(
                M.feat8[:, h, :], pO[:], M.rcp_bc[:, hp, :]), waits=nv_w, inc="ve")
            p.rcpbc_gate[hp] = ("ve", nv)
            p.pO_gate[hp] = ("ve", nv)
        p.feat8_gate = None
        attnout_ready = ("ve", p.c("ve"))

        if qt + 1 < nq:
            xnT_ready = q_prep(qt + 1)

        for oc in range(NF):
            pfk = M.pf[oc % 2]
            w0 = [attnout_ready, wo_ready]
            if p.pf_gate[oc % 2]:
                w0.append(p.pf_gate[oc % 2])
            for fi in range(NF):
                mv = p.add("tensor", lambda e, oc=oc, fi=fi, pfk=pfk: e.matmul(
                    pfk[:], M.wbuf2[:, fi, oc * 128:(oc + 1) * 128],
                    M.feat8[:, fi, :], start=(fi == 0), stop=(fi == NF - 1)),
                    waits=(w0 if fi == 0 else ()),
                    inc=("mm" if fi == NF - 1 else None))
            fv = p.add("scalar", lambda e, oc=oc, pfk=pfk: e.activation(
                M.finalT[:, oc, :], pfk[:], AF.Copy), waits=[("mm", mv)], inc="ae")
            p.pf_gate[oc % 2] = ("ae", fv)
        p.feat8_gate = ("mm", p.c("mm"))
        final_ready = ("ae", fv)

        if qt + 1 < nq:
            qT_ready = q_mm(qt + 1, xnT_ready)
            negM_evs = bound_prep(qt + 1, qT_ready)

        if not SKIP_EPI:
            emit_epilogue_qtile(p, M, io, qt, M.finalT, final_ready, "const",
                                io.get("last", False))



def emit_f0(p, M, io):
    rhsT = io["rhsT"]
    for i in range(NF):
        slot = i % 2
        dv = p.add("sync", lambda e, i=i, slot=slot: e.dma_start(
            out=M.wstage[:, slot, :], in_=rhsT[i * 128:(i + 1) * 128, :]),
            waits=[(f"wf{slot}", p.ws_free[slot])], inc=f"dw{slot}", dma=True)
        p.add("vector", lambda e, slot=slot: e.tensor_reduce(
            col2(M, 8), M.wstage[:, slot, :], AX.X, ALU.max,
            apply_absolute_value=True), waits=[(f"dw{slot}", dv)], inc="ve")
        p.add("vector", lambda e, i=i: e.tensor_scalar(
            M.scale_r[:, i:i + 1], col2(M, 8), 1e-12, 1.0 / 127.0, ALU.max,
            ALU.mult), inc="ve")
        p.add("vector", lambda e, i=i: e.reciprocal(col2(M, 9), M.scale_r[:, i:i + 1]),
              inc="ve")
        p.add("vector", lambda e, slot=slot: e.tensor_scalar(
            M.scrF2[:], M.wstage[:, slot, :], col2(M, 9), None, ALU.mult),
            inc=["ve", f"wf{slot}"])
        p.ws_free[slot] = p.c(f"wf{slot}")
        p.add("vector", lambda e: e.tensor_scalar(
            M.scrF2[:], M.scrF2[:], MAGIC, -MAGIC, ALU.add, ALU.add), inc="ve")
        p.add("vector", lambda e: e.tensor_copy(M.scrI2[:], M.scrF2[:]), inc="ve")
        qv = p.add("vector", lambda e: e.tensor_copy(M.qbf[:], M.scrI2[:]), inc="ve")
        pslot = i % 2
        wt = [("ve", qv), ("dc", p.dc)]
        if p.pb_gate[pslot]:
            wt.append(p.pb_gate[pslot])
        for j in range(NF):
            tv = p.add("tensor", lambda e, j=j, pslot=pslot: e.transpose(
                M.pb[pslot][:, j, :], M.qbf[:, j * 128:(j + 1) * 128], M.ident_bf[:]),
                waits=(wt if j == 0 else ()), inc=("tp" if j == NF - 1 else None))
        ev = p.add("vector", lambda e, i=i, pslot=pslot: e.tensor_copy(
            M.q_rhs[:, :, i * 128:(i + 1) * 128], M.pb[pslot][:]),
            waits=[("tp", tv)], inc="ve")
        p.pb_gate[pslot] = ("ve", ev)
    qrhs_ready = ("ve", p.c("ve"))

    def f0_prep(qt):
        evs = []
        xslot = qt % 2

        def consume(t, i, rdy, prep_slot, qt=qt, xslot=xslot, evs=evs):
            tt = t - qt * 4
            tp = emit_transpose_prep(p, M, tt % 2, rdy, prep_slot)
            evs.append(emit_prep_to_xnT(p, M, tt, xslot, tp))

        emit_prologue_group(p, M, [qt * 4 + k for k in range(4)], io["x_src"],
                            "quant",
                            [(M.xq4f0, k, f"xf0{k}") for k in range(4)], consume)
        return ("ve", max(evs))

    xnT_ready = f0_prep(0)
    for qt in range(NQ):
        xslot = qt % 2
        for oc in range(NF):
            pfk = M.pf[oc % 2]
            w0 = [xnT_ready, qrhs_ready]
            if p.pf_gate[oc % 2]:
                w0.append(p.pf_gate[oc % 2])
            for j in range(NF):
                mv = p.add("tensor", lambda e, oc=oc, j=j, pfk=pfk, xslot=xslot:
                           e.matmul(pfk[:], M.q_rhs[:, j, oc * 128:(oc + 1) * 128],
                                    M.xnT[:, xslot, j, :],
                                    start=(j == 0), stop=(j == NF - 1)),
                           waits=(w0 if j == 0 else ()),
                           inc=("mm" if j == NF - 1 else None))
            fw = [("mm", mv), ("dc", p.dc)]
            if p.qT_gate:
                fw.append(p.qT_gate)
            fv = p.add("scalar", lambda e, oc=oc, pfk=pfk: e.activation(
                M.finalT[:, oc, :], pfk[:], AF.Copy,
                scale=M.scale_r[:, oc:oc + 1]), waits=fw, inc="ae")
            p.pf_gate[oc % 2] = ("ae", fv)
        p.qT_gate = None
        final_ready = ("ae", fv)
        p.xnT_free[xslot] = p.add("scalar", _nop, inc=f"nf{xslot}")
        if qt + 1 < NQ:
            xnT_ready = f0_prep(qt + 1)
        emit_epilogue_qtile(p, M, io, qt, M.finalT, final_ready, "quant",
                            io.get("last", False))
        p.qT_gate = ("tp", p.c("tp"))



def emit_f1(p, M, io):
    qwT, scT, laT, lbT = io["qwT"], io["scT"], io["laT"], io["lbT"]
    rowrt = io["rowrt"]

    def h0_consume(t, i, rdy, prep_slot):
        pslot = t % 2
        tp = emit_transpose_prep(p, M, pslot, rdy, prep_slot)
        ev = p.add("vector", lambda e, t=t, pslot=pslot: e.tensor_copy(
            M.hT[:, :, t * 128:(t + 1) * 128], M.pb[pslot][:]), waits=[tp], inc="ve")
        p.pb_gate[pslot] = ("ve", ev)

    h_qt_ready = [None] * NQ

    p.screp_gate = None
    p.lora_scr_gate = None
    p.scrF2_gate = None
    W3 = (M.wbuf1, M.wbuf2, M.wbuf3)
    w_big = {}
    w_lora = {}
    lora_lastuse = [None, None]

    def prep_big(i, gate, js):
        wb = W3[i % 3]
        vv = 0
        for j in js:
            slot = j % 2
            w = [(f"wf{slot}", p.ws_free[slot])]
            if gate and j == js[0]:
                w.append(gate)
            dv = p.add("sync", lambda e, i=i, j=j, slot=slot: e.dma_start(
                out=M.wstage_i[:, slot, :], in_=qwT[i, j * 128:(j + 1) * 128, :]),
                waits=w, inc=f"dw{slot}", dma=True)
            sg = [p.screp_gate] if p.screp_gate else []
            dsv = 0
            for g in range(8):
                dsv = p.add("sync", lambda e, i=i, j=j, g=g: e.dma_start(
                    out=M.screp[16 * g:16 * (g + 1), :],
                    in_=scT[i, 8 * j + g:8 * j + g + 1, :].broadcast_to([16, D])),
                    waits=(sg if g == 0 else ()), inc="dsc", dma=True)
            tw = [(f"dw{slot}", dv)]
            if gate and j == js[0]:
                tw.append(gate)
            p.add("gpsimd", lambda e, slot=slot: e.tensor_scalar(
                M.scrF2[:], M.wstage_i[:, slot, :], 1.0 / 7.5, -1.0, ALU.mult,
                ALU.add), waits=tw, inc=["ge", f"wf{slot}"])
            p.ws_free[slot] = p.c(f"wf{slot}")
            vv = p.add("gpsimd", lambda e, j=j, wb=wb: e.tensor_mul(
                wb[:, j, :], M.scrF2[:], M.screp[:]),
                waits=[("dsc", dsv)], inc="ge")
            p.screp_gate = ("ge", vv)
        return ("ge", vv)

    def prep_lora(i, gate):
        li = i % 2
        lw = [gate] if gate else []
        if p.lora_scr_gate:
            lw.append(p.lora_scr_gate)
        dv = p.add("sync", lambda e, i=i: e.dma_start(
            out=M.lora_laF[:], in_=laT[i].rearrange("(a p) r -> p a r", p=128)),
            waits=lw, inc="dla", dma=True)
        lv = p.add("vector", lambda e, li=li: e.tensor_copy(
            M.la_bf[:, li, :, :], M.lora_laF[:]), waits=[("dla", dv)], inc="ve")
        dv2 = p.add("sync", lambda e, i=i: e.dma_start(
            out=M.lora_lbF[:], in_=lbT[i]),
            waits=([gate] if gate else []), inc="dlb", dma=True)
        lv2 = p.add("vector", lambda e, li=li: e.tensor_copy(
            M.lb_bf[:, li, :], M.lora_lbF[:]), waits=[("dlb", dv2)], inc="ve")
        p.lora_scr_gate = ("ve", lv2)
        return ("ve", lv2)

    for g in range(NT // 4):
        if g == 0:
            w_lora[0] = prep_lora(0, None)
            w_big[0] = prep_big(0, None, list(range(NF)))
        elif g == 2:
            w_big[1] = prep_big(1, None, list(range(NF)))
            w_lora[1] = prep_lora(1, None)
        emit_prologue_group(p, M, [g * 4 + k for k in range(4)], io["x_src"],
                            "ln", [(M.xq4f1, k, f"xf1{k}") for k in range(4)],
                            h0_consume)
    h_qt_ready = [("ve", p.c("ve"))] * NQ

    p.ysb_gate = [None, None, None]

    def y_lora(mi, src3, src_ready, qsl, extra=()):
        li = mi % 2
        lw = [src_ready, w_big[mi], w_lora[mi]] + list(extra)
        if p.pO_gate[li]:
            lw.append(p.pO_gate[li])
        if lora_lastuse[li]:
            lw.append(lora_lastuse[li])
        for j in range(NF):
            mv = p.add("tensor", lambda e, j=j, qsl=qsl, li=li: e.matmul(
                M.pf[2 + li][0:32, :], M.la_bf[:, li, j, :], src3[:, j, qsl],
                start=(j == 0), stop=(j == NF - 1), skip_group_check=True),
                waits=(lw if j == 0 else ()),
                inc=("mm" if j == NF - 1 else None))
        l1v = p.add("vector", lambda e, li=li: e.tensor_copy(
            M.l1_bf[:, li, :], M.pf[2 + li][0:32, :]), waits=[("mm", mv)],
            inc="ve")
        p.pO_gate[li] = ("ve", l1v)
        return l1v

    def y_group(mi, src3, src_ready, qsl, slot, l1v=None):
        li = mi % 2
        wb = W3[mi % 3]
        if l1v is None:
            l1v = y_lora(mi, src3, src_ready, qsl)
        evs = []
        for oc in range(NF):
            pfk = M.pf[oc % 2]
            w0 = [src_ready, w_big[mi], w_lora[mi], ("ve", l1v)]
            if p.pf_gate[oc % 2]:
                w0.append(p.pf_gate[oc % 2])
            for j in range(NF):
                p.add("tensor", lambda e, oc=oc, j=j, pfk=pfk, qsl=qsl, wb=wb:
                      e.matmul(pfk[:], wb[:, j, oc * 128:(oc + 1) * 128],
                               src3[:, j, qsl], start=(j == 0), stop=False,
                               skip_group_check=True),
                      waits=(w0 if j == 0 else ()))
            mv = p.add("tensor", lambda e, oc=oc, pfk=pfk, li=li: e.matmul(
                pfk[:], M.lb_bf[:, li, oc * 128:(oc + 1) * 128],
                M.l1_bf[:, li, :],
                start=False, stop=True, skip_group_check=True), inc="mm")
            ew = [("mm", mv), ("dc", p.dc)]
            if oc == 0 and p.ysb_gate[slot]:
                ew.append(p.ysb_gate[slot])
            yev = p.add("scalar", lambda e, oc=oc, pfk=pfk, mi=mi, slot=slot:
                        e.activation(M.ySB[:, slot, oc, :], pfk[:], AF.Identity,
                                     bias=M.bcol[:, mi, oc:oc + 1]),
                        waits=ew, inc="ae")
            p.pf_gate[oc % 2] = ("ae", yev)
            evs.append(yev)
        return evs, p.c("mm")

    sq_gate = [None] * NF

    def _sqb(oc, s):
        if oc < 6:
            return M.sqF1[:, oc, :]
        if s < 2:
            return M.sqX[:, oc - 6, :]
        return M.sqscr512[:]

    def emit_sq(k, s):
        slot = k % 3
        cur_evs = y1q[k][0]
        hi = NF if s < 2 else 7
        vs = {}
        for oc in range(hi):
            qw = [("ae", cur_evs[oc])]
            if oc == 6 and s == 2:
                if p.sqscr_gate:
                    qw.append(p.sqscr_gate)
            elif sq_gate[oc]:
                qw.append(sq_gate[oc])
            vs[oc] = p.add("scalar", lambda e, oc=oc, slot=slot, s=s:
                           e.activation(_sqb(oc, s), M.ySB[:, slot, oc, :],
                                        AF.Square), waits=qw, inc="ae")
        return vs

    def emit_stats_mm(k, sqvs, s):
        slot = k % 3
        cur_evs = y1q[k][0]
        qmv = 0
        for oc in range(NF):
            sw = [("ae", cur_evs[oc]), ("vc", p.vc)]
            if oc == 0 and p.pf4_gate:
                sw.append(p.pf4_gate)
            p.add("tensor", lambda e, oc=oc, slot=slot: e.matmul(
                M.pf[4][:], M.ones128[:], M.ySB[:, slot, oc, :],
                start=(oc == 0), stop=(oc == NF - 1), skip_group_check=True),
                waits=sw, inc="mm")
            if oc in sqvs:
                sqv = sqvs[oc]
            else:
                sqv = p.add("scalar", lambda e, slot=slot: e.activation(
                    M.sqscr512[:], M.ySB[:, slot, 7, :], AF.Square),
                    waits=[("ae", cur_evs[7]), ("mm", qmv)], inc="ae")
            qw2 = [("ae", sqv), ("vc", p.vc)]
            if oc == 0 and p.pf5_gate:
                qw2.append(p.pf5_gate)
            qmv = p.add("tensor", lambda e, oc=oc, slot=slot, s=s: e.matmul(
                M.pf[5][:], M.ones128[:], _sqb(oc, s),
                start=(oc == 0), stop=(oc == NF - 1), skip_group_check=True),
                waits=qw2, inc="mm")
            if oc >= 6 and s == 2:
                p.sqscr_gate = ("mm", qmv)
            else:
                sq_gate[oc] = ("mm", qmv)
        return qmv

    chain_bv = {}

    def _ab(k):
        return (M.a_bc, M.b_bc) if k % 2 == 0 else (M.a_bc2, M.b_bc2)

    def chain_emit(k, qmv):
        ab, bb = _ab(k)
        h0 = M.h0chain[:]
        rw = [("mm", qmv)]
        if p.abc_gate[k % 2]:
            rw.append(p.abc_gate[k % 2])
        p.add("vector", lambda e, bb=bb: e.tensor_scalar(
            bb[:], M.pf[4][:], -1.0 / D, None, ALU.mult), waits=rw, inc="ve")
        p.pf4_gate = ("ve", p.c("ve"))
        p.add("vector", lambda e, ab=ab: e.tensor_scalar(
            ab[:], M.pf[5][:], 1.0 / D, None, ALU.mult), inc="ve")
        p.pf5_gate = ("ve", p.c("ve"))
        p.add("vector", lambda e, h0=h0, bb=bb: e.tensor_mul(h0, bb[:], bb[:]),
              inc="ve")
        p.add("vector", lambda e, h0=h0, ab=ab: e.tensor_sub(ab[:], ab[:], h0),
              inc="ve")
        vv = p.add("vector", lambda e, ab=ab: e.tensor_scalar(
            ab[:], ab[:], 1e-5, None, ALU.add), inc="ve")
        a1 = p.add("scalar", lambda e, h0=h0, ab=ab: e.activation(
            h0, ab[:], AF.Sqrt), waits=[("ve", vv)], inc="ae")
        p.add("vector", lambda e, h0=h0, ab=ab: e.reciprocal(ab[:], h0),
              waits=[("ae", a1)], inc="ve")
        chain_bv[k] = p.add("vector", lambda e, ab=ab, bb=bb: e.tensor_mul(
            bb[:], bb[:], ab[:]), inc="ve")

    NG = 3 * NQ

    def g_args(g):
        s, qt = g // NQ, g % NQ
        return 2 * s, 2 * s + 1, qt, slice(qt * 512, (qt + 1) * 512), g % 3, s

    def y1_emit(g, l1v=None):
        a, _, qt, qsl, slot, s = g_args(g)
        return y_group(a, M.hT, h_qt_ready[qt], qsl, slot, l1v=l1v)

    prev_y2_last = None
    y1q = {}
    y1q[0] = y1_emit(0)
    y1q[1] = y1_emit(1)
    l1v3 = y_lora(0, M.hT, h_qt_ready[2], slice(1024, 1536),
                  extra=[("mm", y1q[1][1])])
    y1q[2] = y1_emit(2, l1v=l1v3)
    qmv = emit_stats_mm(0, emit_sq(0, 0), 0)
    chain_emit(0, qmv)
    sqx_last = None
    for g in range(NG):
        a, b, qt, qsl, slot, s = g_args(g)
        cur_evs = y1q[g][0]
        ab, bb = _ab(g)
        if qt == 0 and s > 0:
            w_lora[b] = prep_lora(b, prev_y2_last)
        if g == 2 * NQ:
            for xs in (0, 1):
                p.xq_free[f"xf{xs}"] = p.add(
                    "sync", _nop, waits=[sqx_last], inc=f"xf{xs}")
            p.escr_gate = [("ge", p.c("ge")), ("ge", p.c("ge"))]
        sqvs_next = emit_sq(g + 1, (g + 1) // NQ) if g + 1 < NG else None
        gelu_vs = []
        for oc in range(NF):
            aslot = oc % 2
            asl = M.apply_scr[:, aslot * 512:(aslot + 1) * 512]
            ub = M.sqjunk[:, aslot * 512:(aslot + 1) * 512]
            pw = [("ve", chain_bv[g]), ("ae", cur_evs[oc])]
            if p.applyscr_gate[aslot]:
                pw.append(p.applyscr_gate[aslot])
            p.add("vector", lambda e, oc=oc, asl=asl, slot=slot, ab=ab:
                  e.tensor_mul(asl, M.ySB[:, slot, oc, :], ab[:]),
                  waits=pw, inc="ve")
            v1 = p.add("vector", lambda e, asl=asl, ub=ub, bb=bb:
                       e.tensor_add(ub, asl, bb[:]), inc="ve")
            gv = p.add("scalar", lambda e, oc=oc, ub=ub, s=s: e.activation(
                M.feat8[:, oc, :], ub, AF.Gelu,
                bias=M.lbcol[:, s, oc:oc + 1], scale=M.lgcol[:, s, oc:oc + 1]),
                waits=[("ve", v1), ("dc", p.dc)], inc="ae")
            p.applyscr_gate[aslot] = ("ae", gv)
            gelu_vs.append(gv)
        p.abc_gate[g % 2] = ("ae", max(gelu_vs))
        p.ysb_gate[slot] = ("ae", max(gelu_vs))
        g_ready = ("ae", max(gelu_vs))
        l1v_next = None
        if g + 3 < NG:
            a3, _, qt3, qsl3, _, _ = g_args(g + 3)
            l1v_next = y_lora(a3, M.hT, h_qt_ready[qt3], qsl3,
                              extra=[("mm", y1q[g + 2][1])])
        y2_evs, y2_last = y_group(b, M.feat8, g_ready, slice(0, 512), slot)
        hv = 0
        for oc in range(NF):
            hv = p.add("vector", lambda e, oc=oc, qsl=qsl, slot=slot:
                       e.tensor_add(M.hT[:, oc, qsl], M.hT[:, oc, qsl],
                                    M.ySB[:, slot, oc, :]),
                       waits=[("ae", y2_evs[oc])], inc="ve")
        h_qt_ready[qt] = ("ve", hv)
        p.ysb_gate[slot] = ("ve", hv)
        if g + 1 < NG:
            qmv = emit_stats_mm(g + 1, sqvs_next, (g + 1) // NQ)
            chain_emit(g + 1, qmv)
            if g + 1 == 2 * NQ:
                sqx_last = ("mm", qmv)
        if s == 2:
            emit_epilogue_qtile(p, M, io, qt,
                                M.hT[:, :, qt * 512:(qt + 1) * 512],
                                h_qt_ready[qt], "const",
                                io.get("last", False))
        if g + 3 < NG:
            y1q[g + 3] = y1_emit(g + 3, l1v=l1v_next)
        if s < 2:
            nx0, nx1 = 2 * s + 2, 2 * s + 3
            g0 = None if s == 0 else prev_y2_last
            if qt == 0:
                prep_big(nx0, g0, [0, 1, 2])
            elif qt == 1:
                prep_big(nx0, None, [3, 4, 5])
            elif qt == 2:
                w_big[nx0] = prep_big(nx0, None, [6, 7])
            elif qt == 4:
                y1_all = max(y1q[gg][1] for gg in range(NQ * s, NQ * s + NQ))
                lora_lastuse[0] = ("mm", y1_all)
                prep_big(nx1, ("mm", y1_all), [0, 1, 2])
                w_lora[nx0] = prep_lora(nx0, ("mm", y1_all))
            elif qt == 5:
                prep_big(nx1, None, [3, 4, 5])
            elif qt == 6:
                w_big[nx1] = prep_big(nx1, None, [6, 7])
        if qt == NQ - 1:
            y1_last = max(y1q[gg][1] for gg in range(NQ * s, NQ * s + NQ))
            prev_y2_last = ("mm", y2_last)
            lora_lastuse = [("mm", y1_all if s < 2 else y1_last),
                            ("mm", y2_last)]



def _new_prog(serial=False):
    nc = bass.Bass("TRN2", target_bir_lowering=False, dynamic_dma_scratch_size=512)
    p = Prog(nc, serial=serial)
    p.probs_gate_g = [None, None]
    M = Mem(nc)
    return nc, p, M


def _finish(p):
    p.add("sync", _nop, waits=[(s, p.c(s)) for s in ("do0", "do1")])
    p.emit()


def build_attn_program(layer, serial=False):
    nc, p, M = _new_prog(serial)
    x = nc.dram_tensor("x", [SEQ, D], F32, kind="ExternalInput").ap()
    out = nc.dram_tensor("out", [SEQ, D], F32, kind="ExternalOutput").ap()
    wqkvT = nc.dram_tensor("wqkvT", [D, 3 * D], F32, kind="ExternalInput").ap()
    woutT = nc.dram_tensor("woutT", [D, D], F32, kind="ExternalInput").ap()
    gcols = nc.dram_tensor("gcols", [128, 2, 8], F32, kind="ExternalInput").ap()
    rowrt = nc.dram_tensor("rowrt", [8, 512], F32, kind="Internal").ap()
    io = {"x_src": x, "x_dst": out, "wqkvT": wqkvT, "woutT": woutT,
          "gcols": gcols, "rowrt": rowrt}
    emit_consts(p, M, io)
    emit_attn(p, M, io, layer)
    _finish(p)
    return nc


def build_f0_program(serial=False):
    nc, p, M = _new_prog(serial)
    x = nc.dram_tensor("x", [SEQ, D], F32, kind="ExternalInput").ap()
    out = nc.dram_tensor("out", [SEQ, D], F32, kind="ExternalOutput").ap()
    rhsT = nc.dram_tensor("rhsT", [D, D], F32, kind="ExternalInput").ap()
    io = {"x_src": x, "x_dst": out, "rhsT": rhsT}
    emit_consts(p, M, io)
    emit_f0(p, M, io)
    _finish(p)
    return nc


def build_f1_program(serial=False):
    nc, p, M = _new_prog(serial)
    x = nc.dram_tensor("x", [SEQ, D], F32, kind="ExternalInput").ap()
    out = nc.dram_tensor("out", [SEQ, D], F32, kind="ExternalOutput").ap()
    qwT = nc.dram_tensor("qwT", [6, D, D], I32, kind="ExternalInput").ap()
    scT = nc.dram_tensor("scT", [6, 64, D], F32, kind="ExternalInput").ap()
    laT = nc.dram_tensor("laT", [6, D, 32], F32, kind="ExternalInput").ap()
    lbT = nc.dram_tensor("lbT", [6, 32, D], F32, kind="ExternalInput").ap()
    lgcols = nc.dram_tensor("lgcols", [128, 3, 8], F32, kind="ExternalInput").ap()
    lbcols = nc.dram_tensor("lbcols", [128, 3, 8], F32, kind="ExternalInput").ap()
    bcols = nc.dram_tensor("bcols", [128, 6, 8], F32, kind="ExternalInput").ap()
    rowrt = nc.dram_tensor("rowrt", [8, 512], F32, kind="Internal").ap()
    io = {"x_src": x, "x_dst": out, "qwT": qwT, "scT": scT, "laT": laT,
          "lbT": lbT, "lgcols": lgcols, "lbcols": lbcols, "bcols": bcols,
          "rowrt": rowrt, "last": True}
    emit_consts(p, M, io)
    emit_f1(p, M, io)
    _finish(p)
    return nc


def build_fused_program(serial=False):
    nc, p, M = _new_prog(serial)
    x = nc.dram_tensor("x", [SEQ, D], F32, kind="ExternalInput").ap()
    out = nc.dram_tensor("out", [SEQ, D], F32, kind="ExternalOutput").ap()
    xbuf = nc.dram_tensor("xbuf", [SEQ, D], F32, kind="Internal").ap()
    wqkvT0 = nc.dram_tensor("wqkvT0", [D, 3 * D], F32, kind="ExternalInput").ap()
    woutT0 = nc.dram_tensor("woutT0", [D, D], F32, kind="ExternalInput").ap()
    wqkvT1 = nc.dram_tensor("wqkvT1", [D, 3 * D], F32, kind="ExternalInput").ap()
    woutT1 = nc.dram_tensor("woutT1", [D, D], F32, kind="ExternalInput").ap()
    gcols = nc.dram_tensor("gcols", [128, 2, 8], F32, kind="ExternalInput").ap()
    rhsT = nc.dram_tensor("rhsT", [D, D], F32, kind="ExternalInput").ap()
    qwT = nc.dram_tensor("qwT", [6, D, D], I32, kind="ExternalInput").ap()
    scT = nc.dram_tensor("scT", [6, 64, D], F32, kind="ExternalInput").ap()
    laT = nc.dram_tensor("laT", [6, D, 32], F32, kind="ExternalInput").ap()
    lbT = nc.dram_tensor("lbT", [6, 32, D], F32, kind="ExternalInput").ap()
    lgcols = nc.dram_tensor("lgcols", [128, 3, 8], F32, kind="ExternalInput").ap()
    lbcols = nc.dram_tensor("lbcols", [128, 3, 8], F32, kind="ExternalInput").ap()
    bcols = nc.dram_tensor("bcols", [128, 6, 8], F32, kind="ExternalInput").ap()
    rowrt = nc.dram_tensor("rowrt", [8, 512], F32, kind="Internal").ap()
    base = {"gcols": gcols, "lgcols": lgcols, "lbcols": lbcols, "bcols": bcols,
            "rowrt": rowrt}
    emit_consts(p, M, base)
    io = dict(base, x_src=x, x_dst=xbuf, wqkvT=wqkvT0, woutT=woutT0)
    emit_attn(p, M, io, 0)
    p.barrier()
    io = dict(base, x_src=xbuf, x_dst=xbuf, rhsT=rhsT)
    emit_f0(p, M, io)
    p.barrier()
    io = dict(base, x_src=xbuf, x_dst=xbuf, wqkvT=wqkvT1, woutT=woutT1)
    emit_attn(p, M, io, 1)
    p.barrier()
    io = dict(base, x_src=xbuf, x_dst=out, qwT=qwT, scT=scT, laT=laT, lbT=lbT,
              last=True)
    emit_f1(p, M, io)
    _finish(p)
    return nc



def _layout_weights(inputs):
    ct = np.ascontiguousarray
    d = {}
    d["wqkvT0"] = ct(inputs["L0_attn_w_qkv"].T)
    d["woutT0"] = ct(inputs["L0_attn_w_out"].T)
    d["wqkvT1"] = ct(inputs["L1_attn_w_qkv"].T)
    d["woutT1"] = ct(inputs["L1_attn_w_out"].T)
    g0 = ct(inputs["L0_attn_gamma"].reshape(8, 128).T)
    g1 = ct(inputs["L1_attn_gamma"].reshape(8, 128).T)
    d["gcols"] = ct(np.stack([g0, g1], axis=1))
    d["rhsT"] = ct(inputs["L0_ffn_rhs"].T)
    d["qwT"] = ct(inputs["L1_ffn_qweights"].transpose(0, 2, 1))
    d["scT"] = ct(inputs["L1_ffn_scales"].transpose(0, 2, 1))
    d["laT"] = ct(inputs["L1_ffn_lora_a"].transpose(0, 2, 1))
    d["lbT"] = ct(inputs["L1_ffn_lora_b"].transpose(0, 2, 1))
    d["lgcols"] = ct(inputs["L1_ffn_ln_gamma"].reshape(3, 8, 128).transpose(2, 0, 1))
    d["lbcols"] = ct(inputs["L1_ffn_ln_beta"].reshape(3, 8, 128).transpose(2, 0, 1))
    d["bcols"] = ct(inputs["L1_ffn_biases"].reshape(6, 8, 128).transpose(2, 0, 1))
    return d


def kernel(**inputs):
    ins = {k: np.asarray(v) for k, v in inputs.items()}
    w = _layout_weights(ins)
    xs = [np.ascontiguousarray(ins["L0_attn_x"][b]) for b in range(8)]
    cores = list(range(8))
    if FUSED:
        nc = build_fused_program(serial=SERIAL)
        in_maps = [dict(w, x=xs[b]) for b in range(8)]
        res = run_bass_kernel_spmd(nc, in_maps, core_ids=cores)
        return np.stack([res.results[b]["out"] for b in range(8)], axis=0)
    cur = xs
    nc = build_attn_program(0, serial=SERIAL)
    res = run_bass_kernel_spmd(nc, [
        {"x": cur[b], "wqkvT": w["wqkvT0"], "woutT": w["woutT0"],
         "gcols": w["gcols"]} for b in range(8)], core_ids=cores)
    cur = [res.results[b]["out"] for b in range(8)]
    nc = build_f0_program(serial=SERIAL)
    res = run_bass_kernel_spmd(nc, [
        {"x": cur[b], "rhsT": w["rhsT"]} for b in range(8)], core_ids=cores)
    cur = [res.results[b]["out"] for b in range(8)]
    nc = build_attn_program(1, serial=SERIAL)
    res = run_bass_kernel_spmd(nc, [
        {"x": cur[b], "wqkvT": w["wqkvT1"], "woutT": w["woutT1"],
         "gcols": w["gcols"]} for b in range(8)], core_ids=cores)
    cur = [res.results[b]["out"] for b in range(8)]
    nc = build_f1_program(serial=SERIAL)
    res = run_bass_kernel_spmd(nc, [
        {"x": cur[b], "qwT": w["qwT"], "scT": w["scT"], "laT": w["laT"],
         "lbT": w["lbT"], "lgcols": w["lgcols"], "lbcols": w["lbcols"],
         "bcols": w["bcols"]} for b in range(8)], core_ids=cores)
    return np.stack([res.results[b]["out"] for b in range(8)], axis=0)
